# version 15
# baseline (speedup 1.0000x reference)
import contextlib
import numpy as np
import ml_dtypes
import concourse.bass as bass
import concourse.mybir as mybir
from concourse.bass_utils import run_bass_kernel_spmd

F32 = mybir.dt.float32
BF16 = mybir.dt.bfloat16
I32 = mybir.dt.int32
AF = mybir.ActivationFunctionType
ALU = mybir.AluOpType
AX = mybir.AxisListType.X

B, T, D = 4, 4096, 1024
H2 = 2048
NHP = 4
TT = 512
NTT = T // TT
NDC = D // 128
EPS_LN = 1e-5
BF = ml_dtypes.bfloat16

BUILD_PHASES = ("p0", "p1", "p2", "p3", "p4")
DEBUG_TAPS = ()


def _cc(a):
    return np.ascontiguousarray(a)


def col_chunks(v):
    v = np.asarray(v, np.float32).reshape(-1)
    n = v.shape[0] // 128
    return _cc(v.reshape(n, 128).T)


def host_prep(inp):
    x = np.asarray(inp["L0_attn_x"], np.float32)
    maps = []
    ramp = np.arange(T, dtype=np.float32).reshape(1, T)
    mask01 = np.zeros((4, 128, TT), BF)
    for v in range(4):
        ki = np.arange(128).reshape(128, 1)
        qi = np.arange(TT).reshape(1, TT)
        mask01[v] = (qi >= (ki + v * 128)).astype(BF)
    mask01 = _cc(mask01.transpose(1, 0, 2).reshape(128, 4 * TT))

    for c in range(8):
        b, j = c // 2, c % 2
        pc = np.arange(1024)
        pe8 = np.r_[4 * j:4 * j + 4, 4 * (1 - j):4 * (1 - j) + 4]
        pslots = np.r_[512 * j:512 * (j + 1), 512 * (1 - j):512 * (2 - j)]
        hs = slice(512 * j, 512 * (j + 1))
        es = slice(4 * j, 4 * (j + 1))
        e1 = slice(8 * j, 8 * (j + 1))

        m = {
            "x_fm": _cc(x[b].T[pc]),
            "ones_col": np.ones((128, 1), np.float32),
            "ones_row": np.ones((1, 128), np.float32),
            "ones_col_bf": np.ones((128, 1), BF),
            "ones_row_bf": np.ones((1, 128), BF),
            "ones64_bf": np.ones((128, 64), BF),
            "ident_bf": np.eye(128, dtype=BF),
            "ramp": ramp, "mask01": mask01,
            "wq_T": _cc(np.asarray(inp["L0_attn_Wq_w"])[hs][:, pc].T),
            "wk_T": _cc(np.asarray(inp["L0_attn_Wk_w"])[hs][:, pc].T),
            "wv_T": _cc(np.asarray(inp["L0_attn_Wv_w"])[hs][:, pc].T),
            "wo_T": _cc(np.asarray(inp["L0_attn_Wo_w"])[pc][:, hs].T),
            "qb_c": col_chunks(np.asarray(inp["L0_attn_Wq_b"])[hs]),
            "kb_c": col_chunks(np.asarray(inp["L0_attn_Wk_b"])[hs]),
            "vb_row": _cc(np.asarray(inp["L0_attn_Wv_b"])[hs].reshape(1, 512)),
            "wob_c": col_chunks(np.asarray(inp["L0_attn_Wo_b"])[pc]),
            "gate_w": _cc(np.asarray(inp["L0_ffn_gate_w"])[pc][:, pe8]),
            "gate_b": _cc(np.asarray(inp["L0_ffn_gate_b"], np.float32)[pe8].reshape(8, 1)),
            "rw1": _cc(np.asarray(inp["L0_ffn_rw1"])[es][:, pc, :]),
            "rw2": _cc(np.asarray(inp["L0_ffn_rw2"])[es][:, pc, :]),
            "rw3": _cc(np.asarray(inp["L0_ffn_rw3"])[es][:, :, pc]),
            "rb1": _cc(np.stack([col_chunks(v) for v in np.asarray(inp["L0_ffn_rb1"])[es]])),
            "rb2": _cc(np.stack([col_chunks(v) for v in np.asarray(inp["L0_ffn_rb2"])[es]])),
            "sw1": _cc(np.asarray(inp["L0_ffn_sw1"])[j][pc]),
            "sw2": _cc(np.asarray(inp["L0_ffn_sw2"])[j][pc]),
            "sw3": _cc(np.asarray(inp["L0_ffn_sw3"])[j][:, pc]),
            "sb1": col_chunks(np.asarray(inp["L0_ffn_sb1"])[j]),
            "sb2": col_chunks(np.asarray(inp["L0_ffn_sb2"])[j]),
            "b3_stack": _cc(np.concatenate([
                np.asarray(inp["L0_ffn_sb3"])[j:j + 1][:, pc],
                np.asarray(inp["L0_ffn_rb3"])[es][:, pc]], axis=0)),
            "ident_f32": np.eye(128, dtype=np.float32),
            "sel8": np.repeat(np.eye(8, dtype=np.float32), 128, axis=1),
            "sel4": _cc(np.stack([np.where(np.arange(4)[:, None] == e4,
                                            np.ones((4, 128), np.float32), 0.0)
                                   for e4 in range(4)], axis=1).reshape(4, 512).astype(BF)),
            "dt_s": _cc(np.asarray(inp["L1_attn_dt"], np.float32).reshape(1, 1)),
            "scale_s": _cc(np.asarray(inp["L1_ffn_scale"], np.float32).reshape(1, 1)),
        }
        wpsi = np.asarray(inp["L1_attn_W_psi"])
        wphi = np.asarray(inp["L1_attn_W_phi"])
        bpsi = np.asarray(inp["L1_attn_b_psi"])
        bphi = np.asarray(inp["L1_attn_b_phi"])
        m["wpsi_re_T"] = _cc(wpsi[hs][:, pc].T)
        m["wpsi_im_T"] = _cc(wpsi[1024 + hs.start:1024 + hs.stop][:, pc].T)
        m["wphi_re_T"] = _cc(wphi[hs][:, pc].T)
        m["wphi_im_T"] = _cc(wphi[1024 + hs.start:1024 + hs.stop][:, pc].T)
        m["wgate_T"] = _cc(np.asarray(inp["L1_attn_W_gate"])[hs][:, pc].T)
        m["bpsi_re_c"] = col_chunks(bpsi[hs])
        m["bpsi_im_c"] = col_chunks(bpsi[1024 + hs.start:1024 + hs.stop])
        m["bphi_re_c"] = col_chunks(bphi[hs])
        m["bphi_im_c"] = col_chunks(bphi[1024 + hs.start:1024 + hs.stop])
        m["bgate_c"] = col_chunks(np.asarray(inp["L1_attn_b_gate"])[hs])
        m["omega_c"] = col_chunks(np.asarray(inp["L1_attn_omega"])[hs])
        m["lgam_c"] = col_chunks(np.asarray(inp["L1_attn_log_gamma"])[hs])
        m["norm_g_c"] = col_chunks(np.asarray(inp["L1_ffn_norm_g"])[pc])
        m["norm_b_c"] = col_chunks(np.asarray(inp["L1_ffn_norm_b"])[pc])
        m["slot_g_row"] = _cc(np.asarray(inp["L1_ffn_slot_g"], np.float32)[pc].reshape(1, 1024))
        m["slot_b_row"] = _cc(np.asarray(inp["L1_ffn_slot_b"], np.float32)[pc].reshape(1, 1024))
        se = np.asarray(inp["L1_ffn_slot_embeds"]).reshape(1024, 1024)[:, pc]
        m["se_perm"] = _cc(se[pslots])
        m["se_glob"] = _cc(se)
        m["wg"] = _cc(np.asarray(inp["L1_ffn_Wg"])[e1][:, pc, :])
        m["wu"] = _cc(np.asarray(inp["L1_ffn_Wu"])[e1][:, pc, :])
        m["wd"] = _cc(np.asarray(inp["L1_ffn_Wd"])[e1][:, :, pc])
        m["bg"] = _cc(np.stack([col_chunks(v) for v in np.asarray(inp["L1_ffn_bg"])[e1]]))
        m["bu"] = _cc(np.stack([col_chunks(v) for v in np.asarray(inp["L1_ffn_bu"])[e1]]))
        m["bd_rows"] = _cc(np.asarray(inp["L1_ffn_bd"])[e1][:, pc])
        maps.append(m)
    return maps


ENG_SECTIONS = ["pe", "dve", "act", "pool", "sync"]
QUEUES = {"qs": "sync", "qa": "sync", "qp": "pool"}
QCHAINS = {"qs": 6, "qa": 3, "qp": 6}
QSEMS = [f"{q}{i}" for q, n in QCHAINS.items() for i in range(n)]


class Op:
    __slots__ = ("eng", "fn", "reads", "writes", "sem", "step", "waits", "cnt", "tag")

    def __init__(self, eng, fn, reads, writes, sem, step, tag=""):
        self.eng, self.fn = eng, fn
        self.reads, self.writes = tuple(reads), tuple(writes)
        self.sem, self.step = sem, step
        self.waits, self.cnt, self.tag = {}, 0, tag


class Sched:
    def __init__(self):
        self.ops = []
        self.cnt = {s: 0 for s in ["pe", "dve", "act", "pool", "cc"] + QSEMS}
        self.rr = {q: 0 for q in QCHAINS}

    def op(self, eng, fn, r=(), w=(), tag=""):
        o = Op(eng, fn, r, w, eng, 1, tag)
        self.ops.append(o)
        return o

    def dma(self, queue, fn, r=(), w=(), tag=""):
        ch = self.rr[queue] % QCHAINS[queue]
        self.rr[queue] += 1
        o = Op(QUEUES[queue], fn, r, w, f"{queue}{ch}", 16, tag)
        self.ops.append(o)
        return o

    def collective(self, fn, r=(), w=(), tag=""):
        o = Op("pool", fn, r, w, "cc", 1, tag)
        self.ops.append(o)
        return o

    def fence(self):
        self.ops.append("FENCE")

    def plan(self):
        last_w = {}
        readers = {}
        cnt = {k: 0 for k in self.cnt}
        fence_need = {}
        real_ops = []
        for o in self.ops:
            if o == "FENCE":
                fence_need = {s: v for s, v in cnt.items() if v > 0 and s != "cc"}
                continue
            real_ops.append(o)
            cnt[o.sem] += o.step
            o.cnt = cnt[o.sem]
            need = dict(fence_need)
            for key in o.reads:
                lw = last_w.get(key)
                if lw:
                    s, v = lw
                    need[s] = max(need.get(s, 0), v)
            for key in o.writes:
                lw = last_w.get(key)
                if lw:
                    s, v = lw
                    need[s] = max(need.get(s, 0), v)
                for s, v in readers.get(key, {}).items():
                    need[s] = max(need.get(s, 0), v)
            if o.step == 16:
                need[o.sem] = max(need.get(o.sem, 0), o.cnt - o.step)
            if o.sem in need and need[o.sem] >= o.cnt:
                need[o.sem] = o.cnt - o.step
            o.waits = need
            for key in o.reads:
                readers.setdefault(key, {})[o.sem] = o.cnt
            for key in o.writes:
                last_w[key] = (o.sem, o.cnt)
                readers[key] = {}
        self.ops = real_ops
        self.cnt = cnt
        marks = {}
        for o in self.ops:
            m = marks.setdefault(o.eng, {})
            o.waits = {s: v for s, v in o.waits.items() if m.get(s, 0) < v}
            for s, v in o.waits.items():
                m[s] = v

    def emit(self, nc, block, stack):
        sems = {}
        for n in ["pe", "dve", "act", "pool", "cc"] + QSEMS:
            sems[n] = stack.enter_context(nc.semaphore(n))
        eng_ops = {e: [o for o in self.ops if o.eng == e] for e in ENG_SECTIONS}

        def mk(name):
            def body(eng):
                for o in eng_ops[name]:
                    for sname, v in sorted(o.waits.items()):
                        eng.wait_ge(sems[sname], v)
                    instr = o.fn(eng)
                    instr.then_inc(sems[o.sem], o.step)
            return body

        for name, meth in [("pe", block.tensor), ("dve", block.vector),
                           ("act", block.scalar), ("pool", block.gpsimd),
                           ("sync", block.sync)]:
            if eng_ops[name]:
                meth(mk(name))


def feat_view(ap):
    return ap.rearrange("(n p) f -> p n f", p=128)


class K:
    pass


EXT_IN = {
    "x_fm": ([D, T], F32), "ones_col": ([128, 1], F32), "ones_row": ([1, 128], F32),
    "ones_col_bf": ([128, 1], BF16), "ones_row_bf": ([1, 128], BF16),
    "ones64_bf": ([128, 64], BF16), "ident_bf": ([128, 128], BF16),
    "ident_f32": ([128, 128], F32), "sel4": ([4, 512], BF16),
    "sel8": ([8, 1024], F32),
    "ramp": ([1, T], F32), "mask01": ([128, 4 * TT], BF16),
    "wq_T": ([D, 512], F32), "wk_T": ([D, 512], F32), "wv_T": ([D, 512], F32),
    "wo_T": ([512, D], F32),
    "qb_c": ([128, 4], F32), "kb_c": ([128, 4], F32), "vb_row": ([1, 512], F32),
    "wob_c": ([128, 8], F32),
    "gate_w": ([D, 8], F32), "gate_b": ([8, 1], F32),
    "rw1": ([4, D, H2], F32), "rw2": ([4, D, H2], F32), "rw3": ([4, H2, D], F32),
    "rb1": ([4, 128, 16], F32), "rb2": ([4, 128, 16], F32),
    "sw1": ([D, H2], F32), "sw2": ([D, H2], F32), "sw3": ([H2, D], F32),
    "sb1": ([128, 16], F32), "sb2": ([128, 16], F32), "b3_stack": ([5, D], F32),
    "wpsi_re_T": ([D, 512], F32), "wpsi_im_T": ([D, 512], F32),
    "wphi_re_T": ([D, 512], F32), "wphi_im_T": ([D, 512], F32),
    "wgate_T": ([D, 512], F32),
    "bpsi_re_c": ([128, 4], F32), "bpsi_im_c": ([128, 4], F32),
    "bphi_re_c": ([128, 4], F32), "bphi_im_c": ([128, 4], F32),
    "bgate_c": ([128, 4], F32),
    "omega_c": ([128, 4], F32), "lgam_c": ([128, 4], F32), "dt_s": ([1, 1], F32),
    "norm_g_c": ([128, 8], F32), "norm_b_c": ([128, 8], F32),
    "slot_g_row": ([1, D], F32), "slot_b_row": ([1, D], F32), "scale_s": ([1, 1], F32),
    "se_perm": ([D, D], F32), "se_glob": ([D, D], F32),
    "wg": ([8, D, H2], F32), "wu": ([8, D, H2], F32), "wd": ([8, H2, D], F32),
    "bg": ([8, 128, 16], F32), "bu": ([8, 128, 16], F32), "bd_rows": ([8, D], F32),
}

DSCR = {
    "xn_dram": ([128, NDC * T], BF16),
    "xn1_dram": ([128, NDC * T], BF16),
    "s_ar_in": ([D, T], F32), "s_ar_out": ([D, T], F32),
    "m8_dram": ([8, T], BF16),
    "g_dram_h0": ([D, T // 2], F32), "g_dram_h1": ([D, T // 2], F32),
    "g_ar_h0": ([D, T // 2], F32), "g_ar_h1": ([D, T // 2], F32),
    "x1_dram": ([D, T], F32),
    "h_re_dram": ([512, T], F32), "h_im_dram": ([512, T], F32),
    "st_sum_in": ([2, T], F32), "st_sum_out": ([2, T], F32),
    "st_sq_in": ([2, T], F32), "st_sq_out": ([2, T], F32),
    "s1_dram": ([512, T], F32), "s1_ag_out": ([1024, T], F32),
    "xln_dram": ([128, NDC * T], BF16),
    "eout_ag_in": ([512, D], F32), "eout_ag_out": ([1024, D], F32),
}

GROUPS = [[0, 1], [2, 3], [4, 5], [6, 7]]

CONSTS = ["ones_col", "ones_row", "ones_col_bf", "ones_row_bf", "ones64_bf",
          "ident_bf", "ident_f32", "qb_c", "kb_c", "wob_c", "gate_b",
          "dt_s", "scale_s"]


def build_program(phases=BUILD_PHASES, taps=DEBUG_TAPS):
    nc = bass.Bass("TRN2", target_bir_lowering=False, debug=False)
    k = K()
    k.nc = nc
    k.s = Sched()
    k.di = {n: nc.dram_tensor(n, shp, dt, kind="ExternalInput").ap()
            for n, (shp, dt) in EXT_IN.items()}
    k.dd = {n: nc.dram_tensor(n, shp, dt,
                              kind=("ExternalOutput" if n in taps else "Internal")).ap()
            for n, (shp, dt) in DSCR.items()}
    k.out = nc.dram_tensor("out_fm", [D, T], F32, kind="ExternalOutput").ap()
    k.sb = {}
    k.taps = taps
    stack = contextlib.ExitStack()
    k.stack = stack

    def alloc(name, shape, dtype=F32, scope=None):
        h = (scope or stack).enter_context(nc.sbuf_tensor(name, shape, dtype))
        k.sb[name] = h
        return h

    k.alloc = alloc
    with stack:
        ps = {}
        for i in range(7):
            ps[f"b{i}"] = stack.enter_context(nc.psum_tensor(f"pb{i}", [128, 512], F32))
        ps["t0"] = stack.enter_context(nc.psum_tensor("pt0", [128, 512], BF16))
        k.ps = ps
        def load_consts(names, scope=None):
            for n in names:
                shp, dt = EXT_IN[n]
                alloc("c_" + n, shp, dt, scope)
                k.s.dma("qs", (lambda nm: (lambda e: e.dma_start(
                    out=k.sb["c_" + nm][:, :], in_=k.di[nm])))(n),
                    w=("c_" + n,), tag="const")
        k.load_consts = load_consts
        load_consts(CONSTS)
        alloc("c_eps1", [1, 1], F32)
        k.s.op("dve", lambda e: e.memset(k.sb["c_eps1"][:, :], EPS_LN), w=("c_eps1",))
        alloc("c_eps128", [128, 1], F32)
        k.s.op("dve", lambda e: e.memset(k.sb["c_eps128"][:, :], EPS_LN), w=("c_eps128",))
        for nm in ["r_mu0", "r_mu1", "r_sq0", "r_sq1", "r_var",
                   "r_lnv", "r_rstd", "r_a"]:
            alloc(nm, [1, TT], F32)
        k.sb["r_musq"] = k.sb["r_lnv"]

        if "p0" in phases:
            emit_p0(k)
        if "p2" in phases:
            emit_p2a(k)
        if "p1" in phases:
            emit_p1(k)
        if "p2" in phases:
            emit_p2b(k)
        if "p3" in phases:
            emit_p3(k)
        if "p4" in phases:
            emit_p4(k)

        for scx in reversed(getattr(k, "scopes", [])):
            scx.close()
        fin = k.s.op("pool", lambda e: e.nop(), tag="final")
        k.s.plan()
        fin.waits = {sname: v for sname, v in k.s.cnt.items()
                     if v > 0 and sname != "pool"}
        with nc.Block() as block:
            k.s.emit(nc, block, stack)
    return nc


def ln_stats(k, tag, src_getter, sq_getter, n_chunks=NDC, par=""):
    s, sb, ps = k.s, k.sb, k.ps
    rmu, rsq = "r_mu" + par, "r_sq" + par

    def mm_sum(e):
        for dc in range(n_chunks):
            i = e.matmul(ps["b5"][0:1, 0:TT], sb["c_ones_col_bf"][:, :], src_getter(dc),
                         start=(dc == 0), stop=(dc == n_chunks - 1))
        return i

    def mm_sq(e):
        for dc in range(n_chunks):
            i = e.matmul(ps["b6"][0:1, 0:TT], sb["c_ones_col_bf"][:, :], sq_getter(dc),
                         start=(dc == 0), stop=(dc == n_chunks - 1))
        return i

    s.op("pe", mm_sum, r=(tag + "_src", "c_ones_col_bf"), w=("pb5",))
    s.op("pe", mm_sq, r=(tag + "_sq", "c_ones_col_bf"), w=("pb6",))
    s.op("dve", (lambda rmu=rmu: (lambda e: e.tensor_scalar_mul(
        sb[rmu][:, :], ps["b5"][0:1, 0:TT], 1.0 / D)))(), r=("pb5",), w=(rmu,))
    s.op("dve", (lambda rsq=rsq: (lambda e: e.tensor_scalar_mul(
        sb[rsq][:, :], ps["b6"][0:1, 0:TT], 1.0 / D)))(), r=("pb6",), w=(rsq,))


def ln_finish(k, eps_key="c_eps1", par=""):
    s, sb, ps = k.s, k.sb, k.ps
    rmu, rsq = "r_mu" + par, "r_sq" + par
    s.op("dve", (lambda rmu=rmu: (lambda e: e.tensor_mul(
        sb["r_musq"][:, :], sb[rmu][:, :], sb[rmu][:, :])))(), r=(rmu,), w=("r_lnv",))
    s.op("dve", (lambda rsq=rsq: (lambda e: e.tensor_tensor(
        sb["r_var"][:, :], sb[rsq][:, :], sb["r_musq"][:, :], ALU.subtract)))(),
        r=(rsq, "r_lnv"), w=("r_var",))
    s.op("act", lambda e: e.activation(sb["r_lnv"][:, :], sb["r_var"][:, :], AF.Ln, bias=sb[eps_key][:, :]),
         r=("r_var", eps_key), w=("r_lnv",))
    s.op("act", lambda e: e.activation(sb["r_rstd"][:, :], sb["r_lnv"][:, :], AF.Exp, scale=-0.5),
         r=("r_lnv",), w=("r_rstd",))
    s.op("dve", (lambda rmu=rmu: (lambda e: e.scalar_tensor_tensor(
        sb["r_a"][:, :], sb[rmu][:, :], -1.0, sb["r_rstd"][:, :], ALU.mult, ALU.mult)))(),
        r=(rmu, "r_rstd"), w=("r_a",))
    s.op("pe", lambda e: e.matmul(ps["b3"][:, 0:TT], sb["c_ones_row"][:, :], sb["r_rstd"][:, :],
                                  start=True, stop=True), r=("r_rstd", "c_ones_row"), w=("pb3",))
    s.op("pe", lambda e: e.matmul(ps["b4"][:, 0:TT], sb["c_ones_row"][:, :], sb["r_a"][:, :],
                                  start=True, stop=True), r=("r_a", "c_ones_row"), w=("pb4",))


def ln_rows(k, tag, src_getter, sq_getter, n_chunks=NDC, eps_key="c_eps1"):
    ln_stats(k, tag, src_getter, sq_getter, n_chunks, par="0")
    ln_finish(k, eps_key, par="0")


def emit_p0(k):
    nc, s, sb, ps, di, dd, alloc = k.nc, k.s, k.sb, k.ps, k.di, k.dd, k.alloc

    k.rows_alloc = lambda scope: None
    k.qkv_scope = qsc = contextlib.ExitStack()
    k.scopes = getattr(k, "scopes", []) + [qsc]
    alloc("tmp_f", [128, TT], F32, qsc)
    alloc("q_sb", [128, NHP * T], BF16, qsc)
    alloc("k_sb", [128, NHP * T], BF16, qsc)
    alloc("v_sb", [128, (T // 128) * 512], BF16, qsc)
    k.p0_scope = sc = contextlib.ExitStack()
    k.scopes.append(sc)
    alloc("wq_sb", [128, NDC * 512], BF16, sc)
    alloc("wk_sb", [128, NDC * 512], BF16, sc)
    alloc("wv_sb", [128, NDC * 512], BF16, sc)
    alloc("vb_row_sb", [1, 512], BF16, sc)
    alloc("xbf0", [128, NDC * TT], BF16, sc)
    for i in range(2):
        alloc(f"xt{i}", [128, NDC * TT], F32, sc)
        alloc(f"xn_t{i}", [128, NDC * TT], BF16, sc)
    alloc("sq_bf", [128, NDC * TT], BF16, sc)

    for wn, sn in [("wq_T", "wq_sb"), ("wk_T", "wk_sb"), ("wv_T", "wv_sb")]:
        s.dma("qp", (lambda wn=wn, sn=sn: (lambda e: e.dma_start(
            out=sb[sn][:, :].rearrange("p (dc h) -> p dc h", h=512),
            in_=feat_view(di[wn]))))(), w=(sn,), tag="w0")
    s.dma("qp", lambda e: e.dma_start(out=sb["vb_row_sb"][:, :], in_=di["vb_row"]),
          w=("vb_row_sb",))

    def p0_stage_a(tt):
        xt, xbf = f"xt{tt % 2}", "xbf0"
        s.dma("qs", (lambda tt=tt, xt=xt: (lambda e: e.dma_start(
            out=sb[xt][:, :].rearrange("p (dc t) -> p dc t", t=TT),
            in_=feat_view(di["x_fm"])[:, :, tt * TT:(tt + 1) * TT])))(),
            w=(xt,), tag=f"x{tt}")
        s.op("dve", (lambda xt=xt, xbf=xbf: (lambda e: e.tensor_copy(sb[xbf][:, :], sb[xt][:, :])))(),
             r=(xt,), w=(xbf, "p0_src"))
        s.op("act", (lambda xbf=xbf: (lambda e: [e.activation(
            sb["sq_bf"][:, dc * TT:(dc + 1) * TT], sb[xbf][:, dc * TT:(dc + 1) * TT], AF.Square)
            for dc in range(NDC)][-1]))(), r=(xbf,), w=("sq_bf", "p0_sq"))
        ln_stats(k, "p0",
                 (lambda xbf=xbf: (lambda dc: sb[xbf][:, dc * TT:(dc + 1) * TT]))(),
                 lambda dc: sb["sq_bf"][:, dc * TT:(dc + 1) * TT], par=str(tt % 2))

    def p0_stage_b(tt):
        xt, xn_t = f"xt{tt % 2}", f"xn_t{tt % 2}"
        ln_finish(k, par=str(tt % 2))
        for dc in range(NDC):
            s.op("dve", (lambda xt=xt, dc=dc: (lambda e: e.tensor_tensor(
                sb["tmp_f"][:, :], sb[xt][:, dc * TT:(dc + 1) * TT],
                ps["b3"][:, 0:TT], ALU.mult)))(), r=(xt, "pb3"), w=("tmp_f",))
            s.op("dve", (lambda xn_t=xn_t, dc=dc: (lambda e: e.tensor_tensor(
                sb[xn_t][:, dc * TT:(dc + 1) * TT], sb["tmp_f"][:, :],
                ps["b4"][:, 0:TT], ALU.add)))(), r=("tmp_f", "pb4"), w=(xn_t,))
        s.dma("qa", (lambda tt=tt, xn_t=xn_t: (lambda e: e.dma_start(
            out=dd["xn_dram"].rearrange("p (dc t) -> p dc t", t=T)[:, :, tt * TT:(tt + 1) * TT],
            in_=sb[xn_t][:, :].rearrange("p (dc t) -> p dc t", t=TT))))(),
            r=(xn_t,), w=("xn_dram",))
        for hp in range(NHP):
            def mm_q(e, xn_t=xn_t, hp=hp, w="wq_sb", bank="b0"):
                for dc in range(NDC):
                    i = e.matmul(ps[bank][:, 0:TT],
                                 sb[w][:, dc * 512 + hp * 128: dc * 512 + (hp + 1) * 128],
                                 sb[xn_t][:, dc * TT:(dc + 1) * TT],
                                 start=(dc == 0), stop=(dc == NDC - 1))
                return i
            s.op("pe", mm_q, r=(xn_t, "wq_sb"), w=("pb0",))
            s.op("act", (lambda tt=tt, hp=hp: (lambda e: e.activation(
                sb["q_sb"][:, hp * T + tt * TT: hp * T + (tt + 1) * TT], ps["b0"][:, 0:TT],
                AF.Identity, bias=sb["c_qb_c"][:, hp:hp + 1])))(),
                r=("pb0", "c_qb_c"), w=(f"q_{hp}_{tt}",))
            s.op("pe", (lambda f=mm_q, xn_t=xn_t, hp=hp: (lambda e: f(e, xn_t, hp, "wk_sb", "b1")))(),
                 r=(xn_t, "wk_sb"), w=("pb1",))
            s.op("act", (lambda tt=tt, hp=hp: (lambda e: e.activation(
                sb["k_sb"][:, hp * T + tt * TT: hp * T + (tt + 1) * TT], ps["b1"][:, 0:TT],
                AF.Identity, bias=sb["c_kb_c"][:, hp:hp + 1])))(),
                r=("pb1", "c_kb_c"), w=(f"k_{hp}_{tt}",))
        for tsub in range(TT // 128):
            g = tt * 4 + tsub

            def mm_v(e, xn_t=xn_t, tsub=tsub):
                for dc in range(NDC):
                    e.matmul(ps["b2"][:, 0:512],
                             sb[xn_t][:, dc * TT + tsub * 128: dc * TT + (tsub + 1) * 128],
                             sb["wv_sb"][:, dc * 512:(dc + 1) * 512],
                             start=(dc == 0), stop=False)
                return e.matmul(ps["b2"][:, 0:512], sb["c_ones_row_bf"][:, :],
                                sb["vb_row_sb"][:, :], start=False, stop=True)
            s.op("pe", mm_v, r=(xn_t, "wv_sb", "vb_row_sb", "c_ones_row_bf"), w=("pb2",))
            s.op("dve", (lambda g=g: (lambda e: e.tensor_copy(
                sb["v_sb"][:, g * 512:(g + 1) * 512], ps["b2"][:, 0:512])))(),
                r=("pb2",), w=(f"v_{g}",))

    for i in range(NTT + 1):
        if i < NTT:
            p0_stage_a(i)
        if i >= 1:
            p0_stage_b(i - 1)
    if "dbg_p0" in k.taps:
        _tap_sbuf(k, "q_sb", [128, NHP * T], BF16, deps=[f"q_{h}_{t}" for h in range(NHP) for t in range(NTT)])
        _tap_sbuf(k, "k_sb", [128, NHP * T], BF16, deps=[f"k_{h}_{t}" for h in range(NHP) for t in range(NTT)])
        _tap_sbuf(k, "v_sb", [128, (T // 128) * 512], BF16, deps=[f"v_{g}" for g in range(T // 128)])
    s.fence()
    k.p0_scope.close()
    k.scopes.remove(k.p0_scope)


def _tap_sbuf(k, name, shape, dtype, deps):
    dram = k.nc.dram_tensor("tap_" + name, shape, dtype, kind="ExternalOutput").ap()
    k.s.dma("qa", (lambda name=name, dram=dram: (lambda e: e.dma_start(
        out=dram, in_=k.sb[name][:, :])))(), r=tuple(deps), w=("tap_" + name,))


def emit_p1(k):
    nc, s, sb, ps, di, dd, alloc = k.nc, k.s, k.sb, k.ps, k.di, k.dd, k.alloc
    sc = contextlib.ExitStack()
    k.scopes.append(sc)
    k.load_consts(["mask01"], sc)
    alloc("wo_sb", [128, 4 * D], BF16, sc)
    s.dma("qp", lambda e: e.dma_start(
        out=sb["wo_sb"][:, :].rearrange("p (ic o) -> p ic o", o=D),
        in_=feat_view(di["wo_T"])), w=("wo_sb",))
    for i in range(6):
        alloc(f"expT{i}", [128, TT], BF16, sc)
    alloc("den_s", [128, TT], F32, sc)
    alloc("den_r", [128, TT], F32, sc)
    alloc("attn_sT", [128, NHP * T], BF16, sc)
    alloc("s_t0", [128, TT], F32, sc)
    alloc("s_t1", [128, TT], F32, sc)

    for hp in range(NHP):
        for qc in range(NTT):
            last_kt = 4 * qc + 3
            units = [(kt, r_h) for kt in range(last_kt + 1) for r_h in range(2)]
            NU = len(units)
            LAG = 4

            def emit_st(i, hp=hp, qc=qc, units=units):
                kt, r_h = units[i]
                sbank = ["b2", "b3", "b4", "b5", "b6"][i % 5]
                et = f"expT{i % 6}"
                s.op("pe", (lambda hp=hp, kt=kt, qc=qc, r_h=r_h, sbank=sbank: (lambda e: e.matmul(
                    ps[sbank][:, 0:TT],
                    sb["k_sb"][64 * r_h:64 * (r_h + 1), hp * T + kt * 128: hp * T + (kt + 1) * 128],
                    sb["q_sb"][64 * r_h:64 * (r_h + 1), hp * T + qc * TT: hp * T + (qc + 1) * TT],
                    start=True, stop=True)))(),
                    r=(f"k_{hp}_{kt // 4}", f"q_{hp}_{qc}"), w=(f"p{sbank}",), tag="st")
                s.op("act", (lambda sbank=sbank, et=et: (lambda e: e.activation(
                    sb[et][:, :], ps[sbank][:, 0:TT], AF.Exp, scale=1.0 / 32.0)))(),
                    r=(f"p{sbank}",), w=(et,), tag="exp")
                if kt >= 4 * qc:
                    v = kt - 4 * qc
                    s.op("dve", (lambda et=et, v=v: (lambda e: e.tensor_tensor(
                        sb[et][:, :], sb[et][:, :],
                        sb["c_mask01"][:, v * TT:(v + 1) * TT], ALU.mult)))(),
                        r=(et, "c_mask01"), w=(et,), tag="mask")

            def emit_denav(i, hp=hp, qc=qc, units=units, last_kt=last_kt):
                kt, r_h = units[i]
                et = f"expT{i % 6}"
                s.op("pe", (lambda et=et, r_h=r_h, kt=kt, last_kt=last_kt: (lambda e: e.matmul(
                    ps["b1"][64 * r_h:64 * (r_h + 1), 0:TT],
                    sb["c_ones64_bf"][:, :], sb[et][:, :],
                    start=(kt == 0), stop=(kt == last_kt),
                    skip_group_check=bool(r_h),
                    tile_position=(0, 64) if r_h else None)))(),
                    r=(et, "c_ones64_bf"), w=("pb1",), tag="den")
                s.op("pe", (lambda et=et, r_h=r_h, kt=kt, last_kt=last_kt, hp=hp: (lambda e: e.matmul(
                    ps["b0"][64 * r_h:64 * (r_h + 1), 0:TT],
                    sb["v_sb"][:, kt * 512 + hp * 128 + r_h * 64: kt * 512 + hp * 128 + (r_h + 1) * 64],
                    sb[et][:, :],
                    start=(kt == 0), stop=(kt == last_kt),
                    skip_group_check=bool(r_h),
                    tile_position=(0, 64) if r_h else None)))(),
                    r=(et, f"v_{kt}"), w=("pb0",), tag="av")

            for i in range(NU):
                emit_st(i)
                if i >= LAG:
                    emit_denav(i - LAG)
            for i in range(NU - LAG, NU):
                emit_denav(i)
            s.op("dve", lambda e: e.tensor_copy(sb["den_s"][:, :], ps["b1"][:, 0:TT]),
                 r=("pb1",), w=("den_s",))
            s.op("act", lambda e: e.activation(sb["den_r"][:, :], sb["den_s"][:, :], AF.Ln),
                 r=("den_s",), w=("den_r",))
            s.op("act", lambda e: e.activation(sb["den_r"][:, :], sb["den_r"][:, :], AF.Exp,
                                               scale=-1.0),
                 r=("den_r",), w=("den_r",))
            s.op("dve", (lambda hp=hp, qc=qc: (lambda e: e.tensor_tensor(
                sb["attn_sT"][:, hp * T + qc * TT: hp * T + (qc + 1) * TT],
                ps["b0"][:, 0:TT], sb["den_r"][:, :], ALU.mult)))(),
                r=("pb0", "den_r"), w=(f"attn_{hp}_{qc}",), tag="norm")

    for tt in range(NTT):
        for oc in range(NDC):
            stile = "s_t0" if (tt * NDC + oc) % 2 == 0 else "s_t1"

            def mm_wo(e, tt=tt, oc=oc):
                for ic in range(4):
                    i = e.matmul(ps["b6"][:, 0:TT],
                                 sb["wo_sb"][:, ic * D + oc * 128: ic * D + (oc + 1) * 128],
                                 sb["attn_sT"][:, ic * T + tt * TT: ic * T + (tt + 1) * TT],
                                 start=(ic == 0), stop=(ic == 3))
                return i
            s.op("pe", mm_wo, r=tuple(f"attn_{ic}_{tt}" for ic in range(4)) + ("wo_sb",),
                 w=("pb6",), tag="wo")
            s.op("dve", (lambda stile=stile: (lambda e: e.tensor_copy(sb[stile][:, :], ps["b6"][:, 0:TT])))(),
                 r=("pb6",), w=(stile,))
            s.dma("qa", (lambda tt=tt, oc=oc, stile=stile: (lambda e: e.dma_start(
                out=dd["s_ar_in"][oc * 128:(oc + 1) * 128, tt * TT:(tt + 1) * TT],
                in_=sb[stile][:, :])))(), r=(stile,), w=("s_ar_in",))
    s.collective(lambda e: e.collective_compute(
        "AllReduce", ALU.add, replica_groups=GROUPS,
        ins=[dd["s_ar_in"][:, :]], outs=[dd["s_ar_out"][:, :]]),
        r=("s_ar_in",), w=("s_ar_out",), tag="cc_s")
    s.fence()
    sc.close()
    k.scopes.remove(sc)


def emit_p2a(k):
    nc, s, sb, ps, di, dd, alloc = k.nc, k.s, k.sb, k.ps, k.di, k.dd, k.alloc

    sca = contextlib.ExitStack()
    k.scopes.append(sca)
    k.p2sca = sca
    alloc("gw_sb", [128, NDC * 8], BF16, sca)
    alloc("xn_a", [128, NDC * TT], BF16, sca)
    alloc("glog", [8, TT], F32, sca)
    alloc("m_tok", [128, 32 * 8], F32, sca)
    alloc("m_tokb", [128, 32 * 8], BF16, sca)
    alloc("m8b_sb", [8, T], BF16, sca)
    alloc("p8f", [128, 32 * 8], F32, sca)
    alloc("mk_buf", [128, 8], F32, sca)
    alloc("ge_buf", [128, 8], F32, sca)
    for i in range(1, 5):
        alloc(f"tcol{i}", [128, 1], F32, sca)
    s.dma("qp", lambda e: e.dma_start(
        out=sb["gw_sb"][:, :].rearrange("p (dc e) -> p dc e", e=8),
        in_=feat_view(di["gate_w"])), w=("gw_sb",))
    for tt in range(NTT):
        s.dma("qs", (lambda tt=tt: (lambda e: e.dma_start(
            out=sb["xn_a"][:, :].rearrange("p (dc t) -> p dc t", t=TT),
            in_=feat_view(dd["xn_dram"].rearrange("p (dc t) -> (dc p) t", t=T))[:, :, tt * TT:(tt + 1) * TT]
            if False else
            dd["xn_dram"].rearrange("p (dc t) -> p dc t", t=T)[:, :, tt * TT:(tt + 1) * TT])))(),
            r=("xn_dram",), w=("xn_a",), tag="xn_a")
        def mm_gl(e, tt=tt):
            for dc in range(NDC):
                i = e.matmul(ps["b0"][0:8, 0:TT], sb["gw_sb"][:, dc * 8:(dc + 1) * 8],
                             sb["xn_a"][:, dc * TT:(dc + 1) * TT],
                             start=(dc == 0), stop=(dc == NDC - 1))
            return i
        s.op("pe", mm_gl, r=("xn_a", "gw_sb"), w=("pb0",))
        s.op("act", lambda e: e.activation(sb["glog"][:, :], ps["b0"][0:8, 0:TT],
                                           AF.Identity, bias=sb["c_gate_b"][:, :]),
             r=("pb0", "c_gate_b"), w=("glog",))
        for cq in range(4):
            c = tt * 4 + cq
            s.op("pe", (lambda cq=cq: (lambda e: e.transpose(
                ps["b1"][:, 0:8], sb["glog"][0:8, cq * 128:(cq + 1) * 128],
                sb["c_ident_f32"][0:8, 0:8])))(),
                r=("glog", "c_ident_f32"), w=("pb1",), tag="tr_gl")
            s.op("act", (lambda c=c: (lambda e: e.activation(
                sb["m_tok"][:, c * 8:(c + 1) * 8], ps["b1"][:, 0:8], AF.Identity)))(),
                r=("pb1",), w=(f"mt_{c}",))
    s.op("act", (lambda e: e.activation(
        sb["p8f"][:, :], sb["m_tok"][:, :], AF.Exp)),
        r=tuple(f"mt_{c}" for c in range(32)), w=("p8f",))
    for c in range(32):
        s.op("dve", (lambda c=c: (lambda e: e.reduce_max(
            sb["tcol1"][:, :], sb["m_tok"][:, c * 8:(c + 1) * 8], axis=AX)))(),
            r=(f"mt_{c}",), w=("tcol1",))
        s.op("dve", (lambda c=c: (lambda e: e.tensor_single_scalar(
            sb["ge_buf"][:, :], sb["m_tok"][:, c * 8:(c + 1) * 8], sb["tcol1"][:, :], ALU.is_ge)))(),
            r=(f"mt_{c}", "tcol1"), w=("ge_buf",))
        s.op("dve", (lambda c=c: (lambda e: e.scalar_tensor_tensor(
            sb["mk_buf"][:, :], sb["ge_buf"][:, :], -1e30,
            sb["m_tok"][:, c * 8:(c + 1) * 8], ALU.mult, ALU.add)))(),
            r=("ge_buf", f"mt_{c}"), w=("mk_buf",))
        s.op("dve", lambda e: e.reduce_max(sb["tcol3"][:, :], sb["mk_buf"][:, :], axis=AX),
             r=("mk_buf",), w=("tcol3",))
        s.op("dve", (lambda c=c: (lambda e: e.tensor_single_scalar(
            sb["ge_buf"][:, :], sb["m_tok"][:, c * 8:(c + 1) * 8], sb["tcol3"][:, :], ALU.is_ge)))(),
            r=(f"mt_{c}", "tcol3"), w=("ge_buf",))
        s.op("dve", (lambda c=c: (lambda e: e.reduce_sum(
            sb["tcol4"][:, :], sb["p8f"][:, c * 8:(c + 1) * 8], axis=AX)))(),
            r=("p8f",), w=("tcol4",))
        s.op("dve", lambda e: e.reciprocal(sb["tcol4"][:, :], sb["tcol4"][:, :]),
             r=("tcol4",), w=("tcol4",))
        s.op("dve", (lambda c=c: (lambda e: e.scalar_tensor_tensor(
            sb["m_tok"][:, c * 8:(c + 1) * 8], sb["p8f"][:, c * 8:(c + 1) * 8], sb["tcol4"][:, :],
            sb["ge_buf"][:, :], ALU.mult, ALU.mult)))(),
            r=("p8f", "tcol4", "ge_buf"), w=(f"mw_{c}",))
    s.op("dve", lambda e: e.tensor_copy(sb["m_tokb"][:, :], sb["m_tok"][:, :]),
         r=tuple(f"mw_{c}" for c in range(32)), w=("m_tokb",))
    for c in range(32):
        s.op("pe", (lambda c=c: (lambda e: e.transpose(
            ps["t0"][0:8, 0:128], sb["m_tokb"][:, c * 8:(c + 1) * 8],
            sb["c_ident_bf"][:, :])))(),
            r=("m_tokb", "c_ident_bf"), w=("pt0",), tag="tr_m")
        s.op("act", (lambda c=c: (lambda e: e.activation(
            sb["m8b_sb"][0:8, c * 128:(c + 1) * 128], ps["t0"][0:8, 0:128], AF.Identity)))(),
            r=("pt0",), w=("m8b",))
    s.dma("qa", lambda e: e.dma_start(out=dd["m8_dram"], in_=sb["m8b_sb"][:, :]),
          r=("m8b",), w=("m8_dram",))


def emit_p2b(k):
    nc, s, sb, ps, di, dd, alloc = k.nc, k.s, k.sb, k.ps, k.di, k.dd, k.alloc
    s.fence()
    k.p2sca.close()
    k.scopes.remove(k.p2sca)
    k.qkv_scope.close()
    k.scopes.remove(k.qkv_scope)
    scm = contextlib.ExitStack()
    k.scopes.append(scm)
    k.p2scm = scm
    alloc("m8_sb", [8, T], BF16, scm)
    alloc("m5_bf", [5, T], BF16, scm)
    k.load_consts(["sel4"], scm)
    s.dma("qs", lambda e: e.dma_start(out=sb["m8_sb"][:, :], in_=dd["m8_dram"]),
          r=("m8_dram",), w=("m8",))
    s.op("dve", lambda e: e.memset(sb["m5_bf"][0:1, :], 0.5), w=("m5_row0",))
    s.dma("qp", lambda e: e.dma_start(out=sb["m5_bf"][1:5, :], in_=sb["m8_sb"][0:4, :]),
          r=("m8",), w=("m5_bf",))
    scb = contextlib.ExitStack()
    k.scopes.append(scb)
    alloc("w1_sb", [128, NDC * H2], BF16, scb)
    alloc("w2_sb", [128, NDC * H2], BF16, scb)
    alloc("w3_sb", [128, 16 * D], BF16, scb)
    alloc("b3s_bf", [5, D], BF16, scb)
    alloc("rb1_sb", [128, 64], F32, scb)
    alloc("rb2_sb", [128, 64], F32, scb)
    alloc("sb1_sb", [128, 16], F32, scb)
    alloc("sb2_sb", [128, 16], F32, scb)
    alloc("xn_b0", [128, NDC * TT], BF16, scb)
    for i in range(2):
        alloc(f"glu{i}", [128, 16 * TT], BF16, scb)
    alloc("g_tile", [128, NDC * TT], F32, scb)
    alloc("silu_b", [128, TT], F32, scb)
    alloc("oew_b", [128, TT], F32, scb)
    alloc("bcast_b", [128, TT], F32, scb)
    s.dma("qp", lambda e: e.dma_start(out=sb["b3s_bf"][:, :], in_=di["b3_stack"]),
          w=("b3s_bf",))
    s.dma("qs", lambda e: e.dma_start(
        out=sb["rb1_sb"][:, :].rearrange("p (e h) -> p e h", h=16),
        in_=di["rb1"].rearrange("e p h -> p e h")), w=("rb1_sb",))
    s.dma("qs", lambda e: e.dma_start(
        out=sb["rb2_sb"][:, :].rearrange("p (e h) -> p e h", h=16),
        in_=di["rb2"].rearrange("e p h -> p e h")), w=("rb2_sb",))
    s.dma("qs", lambda e: e.dma_start(out=sb["sb1_sb"][:, :], in_=di["sb1"]), w=("sb1_sb",))
    s.dma("qs", lambda e: e.dma_start(out=sb["sb2_sb"][:, :], in_=di["sb2"]), w=("sb2_sb",))
    s.op("dve", lambda e: e.tensor_scalar_mul(sb["sb2_sb"][:, :], sb["sb2_sb"][:, :], 0.5),
         r=("sb2_sb",), w=("sb2_sb",))

    for ei in range(5):
        shared = (ei == 4)
        for wn, sn in ([("sw1", "w1_sb"), ("sw2", "w2_sb")] if shared
                       else [("rw1", "w1_sb"), ("rw2", "w2_sb")]):
            src = di[wn] if shared else di[wn][ei]
            for hf in range(2):
                s.dma("qp", (lambda src=src, sn=sn, hf=hf: (lambda e: e.dma_start(
                    out=sb[sn][:, :].rearrange("p (dc h) -> p dc h", h=H2)
                        [:, :, hf * (H2 // 2):(hf + 1) * (H2 // 2)],
                    in_=feat_view(src)[:, :, hf * (H2 // 2):(hf + 1) * (H2 // 2)])))(),
                    w=(f"{sn}h{hf}",), tag="wld")
        w3src = di["sw3"] if shared else di["rw3"][ei]
        for hf in range(2):
            s.dma("qp", (lambda src=w3src, hf=hf: (lambda e: e.dma_start(
                out=sb["w3_sb"][:, :].rearrange("p (hc o) -> p hc o", o=D)
                    [:, hf * 8:(hf + 1) * 8, :],
                in_=feat_view(src)[:, hf * 8:(hf + 1) * 8, :])))(),
                w=(f"w3_sbh{hf}",), tag="wld")
        if shared:
            s.op("dve", lambda e: e.tensor_scalar_mul(sb["w2_sb"][:, :], sb["w2_sb"][:, :], 0.5),
                 r=("w2_sbh0", "w2_sbh1"), w=("w2_sbh0", "w2_sbh1"))
        for tt in range(NTT):
            xnb = "xn_b0"
            glu = f"glu{tt % 2}"
            s.dma("qs", (lambda tt=tt, xnb=xnb: (lambda e: e.dma_start(
                out=sb[xnb][:, :].rearrange("p (dc t) -> p dc t", t=TT),
                in_=dd["xn_dram"].rearrange("p (dc t) -> p dc t", t=T)[:, :, tt * TT:(tt + 1) * TT])))(),
                r=("xn_dram",), w=(xnb,), tag="xn_b")
            for hc in range(16):
                p1b = "b2" if hc % 2 == 0 else "b3"
                p2b = "b4" if hc % 2 == 0 else "b5"
                def mm_p(e, xnb=xnb, hc=hc, w="w1_sb", bank="b2"):
                    for dc in range(NDC):
                        i = e.matmul(ps[bank][:, 0:TT],
                                     sb[w][:, dc * H2 + hc * 128: dc * H2 + (hc + 1) * 128],
                                     sb[xnb][:, dc * TT:(dc + 1) * TT],
                                     start=(dc == 0), stop=(dc == NDC - 1))
                    return i
                whk = f"h{hc // 8}"
                s.op("pe", (lambda f=mm_p, xnb=xnb, hc=hc, p1b=p1b: (lambda e: f(e, xnb, hc, "w1_sb", p1b)))(),
                     r=(xnb, "w1_sb" + whk), w=(f"p{p1b}",))
                s.op("pe", (lambda f=mm_p, xnb=xnb, hc=hc, p2b=p2b: (lambda e: f(e, xnb, hc, "w2_sb", p2b)))(),
                     r=(xnb, "w2_sb" + whk), w=(f"p{p2b}",))
                bcol = (lambda hc=hc, ei=ei: sb["sb1_sb"][:, hc:hc + 1] if ei == 4
                        else sb["rb1_sb"][:, ei * 16 + hc: ei * 16 + hc + 1])
                s.op("act", (lambda hc=hc, ei=ei, p1b=p1b: (lambda e: e.activation(
                    sb["silu_b"][:, :], ps[p1b][:, 0:TT], AF.Silu,
                    bias=(sb["sb1_sb"][:, hc:hc + 1] if ei == 4
                          else sb["rb1_sb"][:, ei * 16 + hc: ei * 16 + hc + 1]))))(),
                    r=(f"p{p1b}", "rb1_sb", "sb1_sb"), w=("silu_b",))
                s.op("dve", (lambda hc=hc, ei=ei, glu=glu, p2b=p2b: (lambda e: e.scalar_tensor_tensor(
                    sb[glu][:, hc * TT:(hc + 1) * TT], ps[p2b][:, 0:TT],
                    (sb["sb2_sb"][:, hc:hc + 1] if ei == 4
                     else sb["rb2_sb"][:, ei * 16 + hc: ei * 16 + hc + 1]),
                    sb["silu_b"][:, :], ALU.add, ALU.mult)))(),
                    r=(f"p{p2b}", "silu_b", "rb2_sb", "sb2_sb"), w=(glu,))
            if not shared:
                s.op("pe", (lambda ei=ei, tt=tt: (lambda e: e.matmul(
                    ps["b6"][:, 0:TT], sb["c_sel4"][:, ei * 128:(ei + 1) * 128],
                    sb["m8_sb"][0:4, tt * TT:(tt + 1) * TT], start=True, stop=True)))(),
                    r=("m8", "c_sel4"), w=("pb6",), tag="bcast")
                s.op("dve", lambda e: e.tensor_copy(sb["bcast_b"][:, :], ps["b6"][:, 0:TT]),
                     r=("pb6",), w=("bcast_b",))
            if ei > 0:
                gd = "g_dram_h0" if tt < 4 else "g_dram_h1"
                toff = (tt % 4) * TT
                s.dma("qa", (lambda gd=gd, toff=toff: (lambda e: e.dma_start(
                    out=sb["g_tile"][:, :].rearrange("p (dc t) -> p dc t", t=TT),
                    in_=feat_view(dd[gd])[:, :, toff:toff + TT])))(),
                    r=(gd,), w=("g_tile",), tag="g_ld")
            for dc in range(NDC):
                ob = "b0" if dc % 2 == 0 else "b1"
                def mm_o(e, glu=glu, dc=dc, ob=ob):
                    for hc in range(16):
                        i = e.matmul(ps[ob][:, 0:TT],
                                     sb["w3_sb"][:, hc * D + dc * 128: hc * D + (dc + 1) * 128],
                                     sb[glu][:, hc * TT:(hc + 1) * TT],
                                     start=(hc == 0), stop=(hc == 15))
                    return i
                s.op("pe", mm_o, r=(glu, "w3_sbh0", "w3_sbh1"), w=(f"p{ob}",))
                if ei == 0:
                    s.op("pe", (lambda dc=dc, tt=tt: (lambda e: e.matmul(
                        ps["b5"][:, 0:TT],
                        sb["b3s_bf"][:, dc * 128:(dc + 1) * 128],
                        sb["m5_bf"][:, tt * TT:(tt + 1) * TT], start=True, stop=True)))(),
                        r=("b3s_bf", "m5_bf"), w=("pb5",), tag="b3mm")
                    s.op("dve", (lambda dc=dc, ob=ob: (lambda e: e.tensor_tensor(
                        sb["oew_b"][:, :], ps[ob][:, 0:TT], sb["bcast_b"][:, :], ALU.mult)))(),
                        r=(f"p{ob}", "bcast_b"), w=("oew_b",))
                    s.op("dve", (lambda dc=dc: (lambda e: e.tensor_tensor(
                        sb["g_tile"][:, dc * TT:(dc + 1) * TT], sb["oew_b"][:, :],
                        ps["b5"][:, 0:TT], ALU.add)))(),
                        r=("oew_b", "pb5"), w=("g_tile",))
                elif not shared:
                    s.op("dve", (lambda dc=dc, ob=ob: (lambda e: e.tensor_tensor(
                        sb["oew_b"][:, :], ps[ob][:, 0:TT], sb["bcast_b"][:, :], ALU.mult)))(),
                        r=(f"p{ob}", "bcast_b"), w=("oew_b",))
                    s.op("pool", (lambda dc=dc: (lambda e: e.tensor_tensor(
                        sb["g_tile"][:, dc * TT:(dc + 1) * TT],
                        sb["g_tile"][:, dc * TT:(dc + 1) * TT], sb["oew_b"][:, :], ALU.add)))(),
                        r=("g_tile", "oew_b"), w=("g_tile",))
                else:
                    s.op("dve", (lambda dc=dc, ob=ob: (lambda e: e.tensor_tensor(
                        sb["g_tile"][:, dc * TT:(dc + 1) * TT],
                        sb["g_tile"][:, dc * TT:(dc + 1) * TT], ps[ob][:, 0:TT], ALU.add)))(),
                        r=("g_tile", f"p{ob}"), w=("g_tile",))
            gd = "g_dram_h0" if tt < 4 else "g_dram_h1"
            toff = (tt % 4) * TT
            s.dma("qa", (lambda gd=gd, toff=toff: (lambda e: e.dma_start(
                out=feat_view(dd[gd])[:, :, toff:toff + TT],
                in_=sb["g_tile"][:, :].rearrange("p (dc t) -> p dc t", t=TT))))(),
                r=("g_tile",), w=(gd,), tag="g_st")
    s.collective(lambda e: e.collective_compute(
        "AllReduce", ALU.add, replica_groups=GROUPS,
        ins=[dd["g_dram_h0"][:, :]], outs=[dd["g_ar_h0"][:, :]]),
        r=("g_dram_h0",), w=("g_ar_h0",), tag="cc_g0")
    s.collective(lambda e: e.collective_compute(
        "AllReduce", ALU.add, replica_groups=GROUPS,
        ins=[dd["g_dram_h1"][:, :]], outs=[dd["g_ar_h1"][:, :]]),
        r=("g_dram_h1",), w=("g_ar_h1",), tag="cc_g1")
    s.fence()
    scb.close()
    k.scopes.remove(scb)
    scm.close()
    k.scopes.remove(scm)

    scc = contextlib.ExitStack()
    k.scopes.append(scc)
    k.rows_alloc(scc)
    alloc("xn1_t", [128, NDC * TT], BF16, scc)
    alloc("gc_t", [128, NDC * TT], F32, scc)
    alloc("sc_t", [128, NDC * TT], F32, scc)
    alloc("xc_t", [128, NDC * TT], F32, scc)
    for i in range(2):
        alloc(f"x1_t{i}", [128, NDC * TT], F32, scc)
    alloc("x1bf", [128, NDC * TT], BF16, scc)
    alloc("sq2_bf", [128, NDC * TT], BF16, scc)
    alloc("g1_b", [128, TT], F32, scc)
    alloc("g1_t4", [128, NDC * TT], F32, scc)

    def p2c_a(tt):
        x1t = f"x1_t{tt % 2}"
        gar = "g_ar_h0" if tt < 4 else "g_ar_h1"
        s.dma("qs", (lambda tt=tt, gar=gar: (lambda e: e.dma_start(
            out=sb["gc_t"][:, :].rearrange("p (dc t) -> p dc t", t=TT),
            in_=feat_view(dd[gar])[:, :, (tt % 4) * TT:(tt % 4 + 1) * TT])))(),
            r=(gar,), w=("gc_t",))
        s.dma("qs", (lambda tt=tt: (lambda e: e.dma_start(
            out=sb["sc_t"][:, :].rearrange("p (dc t) -> p dc t", t=TT),
            in_=feat_view(dd["s_ar_out"])[:, :, tt * TT:(tt + 1) * TT])))(),
            r=("s_ar_out",), w=("sc_t",))
        s.dma("qs", (lambda tt=tt: (lambda e: e.dma_start(
            out=sb["xc_t"][:, :].rearrange("p (dc t) -> p dc t", t=TT),
            in_=feat_view(di["x_fm"])[:, :, tt * TT:(tt + 1) * TT])))(),
            w=("xc_t",))
        s.op("act", lambda e: e.activation(sb["g1_t4"][:, :], sb["gc_t"][:, :], AF.Silu),
             r=("gc_t",), w=("g1_t4",))
        for dc in range(NDC):
            s.op("dve", (lambda dc=dc: (lambda e: e.scalar_tensor_tensor(
                sb["g1_t4"][:, dc * TT:(dc + 1) * TT], sb["sc_t"][:, dc * TT:(dc + 1) * TT],
                sb["c_wob_c"][:, dc:dc + 1], sb["g1_t4"][:, dc * TT:(dc + 1) * TT],
                ALU.add, ALU.mult)))(),
                r=("sc_t", "c_wob_c", "g1_t4"), w=("g1_t4",))
        s.op("pool", (lambda x1t=x1t: (lambda e: e.tensor_tensor(
            sb[x1t][:, :], sb["g1_t4"][:, :], sb["xc_t"][:, :], ALU.add)))(),
            r=("g1_t4", "xc_t"), w=(x1t,))
        s.dma("qa", (lambda tt=tt, x1t=x1t: (lambda e: e.dma_start(
            out=feat_view(dd["x1_dram"])[:, :, tt * TT:(tt + 1) * TT],
            in_=sb[x1t][:, :].rearrange("p (dc t) -> p dc t", t=TT))))(),
            r=(x1t,), w=("x1_dram",))
        s.op("dve", (lambda x1t=x1t: (lambda e: e.tensor_copy(sb["x1bf"][:, :], sb[x1t][:, :])))(),
             r=(x1t,), w=("x1bf", "p2c_src"))
        s.op("act", lambda e: e.activation(sb["sq2_bf"][:, :], sb["x1bf"][:, :], AF.Square),
             r=("x1bf",), w=("sq2_bf", "p2c_sq"))
        ln_stats(k, "p2c",
                 lambda dc: sb["x1bf"][:, dc * TT:(dc + 1) * TT],
                 lambda dc: sb["sq2_bf"][:, dc * TT:(dc + 1) * TT], par=str(tt % 2))

    def p2c_b(tt):
        x1t = f"x1_t{tt % 2}"
        ln_finish(k, par=str(tt % 2))
        for dc in range(NDC):
            s.op("dve", (lambda dc=dc, x1t=x1t: (lambda e: e.tensor_tensor(
                sb["g1_b"][:, :], sb[x1t][:, dc * TT:(dc + 1) * TT],
                ps["b3"][:, 0:TT], ALU.mult)))(), r=(x1t, "pb3"), w=("g1_b",))
            s.op("dve", (lambda dc=dc: (lambda e: e.tensor_tensor(
                sb["xn1_t"][:, dc * TT:(dc + 1) * TT],
                sb["g1_b"][:, :], ps["b4"][:, 0:TT], ALU.add)))(),
                r=("g1_b", "pb4"), w=("xn1_t",))
        s.dma("qa", (lambda tt=tt: (lambda e: e.dma_start(
            out=dd["xn1_dram"].rearrange("p (dc t) -> p dc t", t=T)[:, :, tt * TT:(tt + 1) * TT],
            in_=sb["xn1_t"][:, :].rearrange("p (dc t) -> p dc t", t=TT))))(),
            r=("xn1_t",), w=("xn1_dram",))

    for i in range(NTT + 1):
        if i < NTT:
            p2c_a(i)
        if i >= 1:
            p2c_b(i - 1)
    s.fence()
    scc.close()
    k.scopes.remove(scc)


def emit_p3(k):
    nc, s, sb, ps, di, dd, alloc = k.nc, k.s, k.sb, k.ps, k.di, k.dd, k.alloc
    sc3 = contextlib.ExitStack()
    k.scopes.append(sc3)
    k.load_consts(["ramp", "omega_c", "lgam_c", "bpsi_re_c", "bpsi_im_c",
                   "bgate_c"], sc3)
    for nm in ["A3", "B3", "F3", "D3", "COS3", "SIN3", "U3", "C3"]:
        alloc(nm, [128, T], F32, sc3)
    alloc("I3", [128, T // 2], I32, sc3)
    alloc("strows", [128, 128], F32, sc3)
    for i in range(2):
        alloc(f"st32_{i}", [32, 128], F32, sc3)
    alloc("xn1_a", [128, NDC * TT], BF16, sc3)
    alloc("xn1_b", [128, NDC * TT], BF16, sc3)
    alloc("g3_t", [128, TT], F32, sc3)
    alloc("w_psi_re", [128, NDC * 128], BF16, sc3)
    alloc("w_psi_im", [128, NDC * 128], BF16, sc3)
    alloc("w_gate3", [128, NDC * 128], BF16, sc3)
    for nm in ["rho_c", "thp_c", "t3a", "t3b", "dtabs", "negdt", "thpr_s"]:
        alloc(nm, [128, 1] if nm != "thpr_s" else [1, 128], F32, sc3)
    alloc("c_q25", [128, 1], F32, sc3)
    s.op("dve", lambda e: e.memset(sb["c_q25"][:, :], 0.25), w=("c_q25",))

    s.op("act", lambda e: e.activation(sb["t3a"][0:1, 0:1], sb["c_dt_s"][:, :], AF.Abs),
         r=("c_dt_s",), w=("t3a",))
    s.op("pe", lambda e: e.matmul(ps["b0"][:, 0:1], sb["c_ones_row"][:, :],
                                  sb["t3a"][0:1, 0:1], start=True, stop=True),
         r=("t3a", "c_ones_row"), w=("pb0",))
    s.op("dve", lambda e: e.tensor_copy(sb["dtabs"][:, :], ps["b0"][:, 0:1]),
         r=("pb0",), w=("dtabs",))
    s.op("dve", lambda e: e.tensor_scalar_mul(sb["negdt"][:, :], sb["dtabs"][:, :], -1.0),
         r=("dtabs",), w=("negdt",))

    for cc in range(4):
        for wn, sn in [("wpsi_re_T", "w_psi_re"), ("wpsi_im_T", "w_psi_im"),
                       ("wgate_T", "w_gate3")]:
            s.dma("qp", (lambda wn=wn, sn=sn, cc=cc: (lambda e: e.dma_start(
                out=sb[sn][:, :].rearrange("p (dc m) -> p dc m", m=128),
                in_=feat_view(di[wn])[:, :, cc * 128:(cc + 1) * 128])))(),
                w=(sn,), tag="w3")
        s.op("act", (lambda cc=cc: (lambda e: e.activation(
            sb["t3a"][:, :], sb["c_lgam_c"][:, cc:cc + 1], AF.Exp)))(),
            r=("c_lgam_c",), w=("t3a",))
        s.op("dve", lambda e: e.tensor_mul(sb["t3b"][:, :], sb["t3a"][:, :], sb["negdt"][:, :]),
             r=("t3a", "negdt"), w=("t3b",))
        s.op("act", lambda e: e.activation(sb["rho_c"][:, :], sb["t3b"][:, :], AF.Exp),
             r=("t3b",), w=("rho_c",))
        s.op("dve", (lambda cc=cc: (lambda e: e.tensor_mul(
            sb["thp_c"][:, :], sb["c_omega_c"][:, cc:cc + 1], sb["dtabs"][:, :])))(),
            r=("c_omega_c", "dtabs"), w=("thp_c",))
        s.op("dve", lambda e: e.tensor_scalar_mul(sb["thp_c"][:, :], sb["thp_c"][:, :],
                                                  float(1.0 / (2 * np.pi))),
             r=("thp_c",), w=("thp_c",))
        s.op("pe", lambda e: e.transpose(ps["b0"][0:1, 0:128], sb["thp_c"][:, :],
                                         sb["c_ident_f32"][:, :]),
             r=("thp_c", "c_ident_f32"), w=("pb0",), tag="tr_thp")
        s.op("dve", lambda e: e.tensor_copy(sb["thpr_s"][:, :], ps["b0"][0:1, 0:128]),
             r=("pb0",), w=("thpr_s",))
        def planes(ub, cb, out, bias_key=None):
            for ut in range(NTT):
                s.op("pe", (lambda ut=ut: (lambda e: e.matmul(
                    ps["b1"][:, 0:TT], sb["thpr_s"][:, :],
                    sb["c_ramp"][0:1, ut * TT:(ut + 1) * TT], start=True, stop=True)))(),
                    r=("thpr_s", "c_ramp"), w=("pb1",), tag="u_mm")
                if bias_key is None:
                    s.op("dve", (lambda ut=ut, ub=ub: (lambda e: e.tensor_copy(
                        sb[ub][:, ut * TT:(ut + 1) * TT], ps["b1"][:, 0:TT])))(),
                        r=("pb1",), w=(ub,))
                else:
                    s.op("act", (lambda ut=ut, ub=ub: (lambda e: e.activation(
                        sb[ub][:, ut * TT:(ut + 1) * TT], ps["b1"][:, 0:TT],
                        AF.Identity, bias=sb[bias_key][:, :])))(),
                        r=("pb1", bias_key), w=(ub,))
            for half in range(2):
                sl = slice(half * (T // 2), (half + 1) * (T // 2))
                s.op("dve", (lambda sl=sl, ub=ub: (lambda e: e.tensor_copy(
                    sb["I3"][:, :], sb[ub][:, sl])))(), r=(ub,), w=("I3",))
                s.op("dve", (lambda sl=sl, cb=cb: (lambda e: e.tensor_copy(
                    sb[cb][:, sl], sb["I3"][:, :])))(), r=("I3",), w=(cb,))
                s.op("dve", (lambda sl=sl, ub=ub, cb=cb: (lambda e: e.tensor_tensor(
                    sb[ub][:, sl], sb[ub][:, sl], sb[cb][:, sl], ALU.subtract)))(),
                    r=(ub, cb), w=(ub,))
                s.op("dve", (lambda sl=sl, ub=ub, cb=cb: (lambda e: e.tensor_single_scalar(
                    sb[cb][:, sl], sb[ub][:, sl], 0.5, ALU.is_ge)))(),
                    r=(ub,), w=(cb,))
                s.op("dve", (lambda sl=sl, ub=ub, cb=cb: (lambda e: e.tensor_tensor(
                    sb[ub][:, sl], sb[ub][:, sl], sb[cb][:, sl], ALU.subtract)))(),
                    r=(ub, cb), w=(ub,))
            s.op("act", (lambda ub=ub, out=out: (lambda e: e.activation(
                sb[out][:, :], sb[ub][:, :], AF.Sin, scale=float(2 * np.pi))))(),
                r=(ub,), w=(out,))
        planes("U3", "C3", "SIN3")
        planes("U3", "C3", "COS3", bias_key="c_q25")
        for tt in range(NTT):
            xnb = "xn1_a" if tt % 2 == 0 else "xn1_b"
            s.dma("qs", (lambda tt=tt, xnb=xnb: (lambda e: e.dma_start(
                out=sb[xnb][:, :].rearrange("p (dc t) -> p dc t", t=TT),
                in_=dd["xn1_dram"].rearrange("p (dc t) -> p dc t", t=T)[:, :, tt * TT:(tt + 1) * TT])))(),
                r=("xn1_dram",), w=(xnb,), tag="xn1_ld")
            for wname, target, bias, func in [
                    ("w_psi_re", "A3", "c_bpsi_re_c", AF.Identity),
                    ("w_psi_im", "B3", "c_bpsi_im_c", AF.Identity),
                    ("w_gate3", "g3_t", "c_bgate_c", AF.Sigmoid)]:
                bank = "b2" if wname != "w_gate3" else "b4"
                def mm_pr(e, xnb=xnb, wname=wname, bank=bank):
                    for dc in range(NDC):
                        i = e.matmul(ps[bank][:, 0:TT],
                                     sb[wname][:, dc * 128:(dc + 1) * 128],
                                     sb[xnb][:, dc * TT:(dc + 1) * TT],
                                     start=(dc == 0), stop=(dc == NDC - 1))
                    return i
                s.op("pe", mm_pr, r=(xnb, wname), w=(f"p{bank}",))
                if target == "g3_t":
                    s.op("act", (lambda cc=cc, bias=bias, bank=bank: (lambda e: e.activation(
                        sb["g3_t"][:, :], ps[bank][:, 0:TT], AF.Sigmoid,
                        bias=sb[bias][:, cc:cc + 1])))(),
                        r=(f"p{bank}", bias), w=("g3_t",))
                else:
                    s.op("act", (lambda tt=tt, cc=cc, target=target, bias=bias, bank=bank:
                                 (lambda e: e.activation(
                        sb[target][:, tt * TT:(tt + 1) * TT], ps[bank][:, 0:TT],
                        AF.Identity, bias=sb[bias][:, cc:cc + 1])))(),
                        r=(f"p{bank}", bias), w=(target,))
            s.op("dve", (lambda tt=tt: (lambda e: e.tensor_tensor(
                sb["A3"][:, tt * TT:(tt + 1) * TT], sb["A3"][:, tt * TT:(tt + 1) * TT],
                sb["g3_t"][:, :], ALU.mult)))(), r=("A3", "g3_t"), w=("A3",))
            s.op("dve", (lambda tt=tt: (lambda e: e.tensor_tensor(
                sb["B3"][:, tt * TT:(tt + 1) * TT], sb["B3"][:, tt * TT:(tt + 1) * TT],
                sb["g3_t"][:, :], ALU.mult)))(), r=("B3", "g3_t"), w=("B3",))
        s.op("pool", lambda e: e.tensor_mul(sb["F3"][:, :], sb["SIN3"][:, :], sb["B3"][:, :]),
             r=("SIN3", "B3"), w=("F3",))
        s.op("dve", lambda e: e.tensor_mul(sb["D3"][:, :], sb["SIN3"][:, :], sb["A3"][:, :]),
             r=("SIN3", "A3"), w=("D3",))
        s.op("pool", lambda e: e.tensor_mul(sb["B3"][:, :], sb["COS3"][:, :], sb["B3"][:, :]),
             r=("COS3", "B3"), w=("B3",))
        s.op("dve", lambda e: e.tensor_mul(sb["A3"][:, :], sb["COS3"][:, :], sb["A3"][:, :]),
             r=("COS3", "A3"), w=("A3",))
        s.op("pool", lambda e: e.tensor_tensor(sb["A3"][:, :], sb["A3"][:, :], sb["F3"][:, :], ALU.add),
             r=("A3", "F3"), w=("A3",))
        s.op("dve", lambda e: e.tensor_tensor(sb["B3"][:, :], sb["B3"][:, :], sb["D3"][:, :],
                                              ALU.subtract),
             r=("B3", "D3"), w=("B3",))
        s.op("dve", lambda e: e.memset(sb["F3"][:, :], 1.0), w=("F3",))
        s.op("dve", lambda e: e.tensor_scalar_mul(sb["F3"][:, :], sb["F3"][:, :], sb["rho_c"][:, :]),
             r=("F3", "rho_c"), w=("F3",))
        s.op("dve", lambda e: e.tensor_tensor_scan(sb["D3"][:, :], sb["F3"][:, :],
                                                   sb["A3"][:, :], 0.0, ALU.mult, ALU.add),
             r=("F3", "A3"), w=("D3",))
        s.op("dve", lambda e: e.tensor_tensor_scan(sb["A3"][:, :], sb["F3"][:, :],
                                                   sb["B3"][:, :], 0.0, ALU.mult, ALU.add),
             r=("F3", "B3"), w=("A3",))
        s.op("pool", lambda e: e.tensor_mul(sb["B3"][:, :], sb["COS3"][:, :], sb["D3"][:, :]),
             r=("COS3", "D3"), w=("B3",))
        s.op("dve", lambda e: e.tensor_mul(sb["F3"][:, :], sb["SIN3"][:, :], sb["D3"][:, :]),
             r=("SIN3", "D3"), w=("F3",))
        s.op("pool", lambda e: e.tensor_mul(sb["D3"][:, :], sb["SIN3"][:, :], sb["A3"][:, :]),
             r=("SIN3", "A3"), w=("D3",))
        s.op("dve", lambda e: e.tensor_mul(sb["A3"][:, :], sb["COS3"][:, :], sb["A3"][:, :]),
             r=("COS3", "A3"), w=("A3",))
        s.op("pool", lambda e: e.tensor_tensor(sb["B3"][:, :], sb["B3"][:, :], sb["D3"][:, :],
                                              ALU.subtract), r=("B3", "D3"), w=("B3",))
        s.op("dve", lambda e: e.tensor_tensor(sb["A3"][:, :], sb["A3"][:, :], sb["F3"][:, :],
                                              ALU.add), r=("A3", "F3"), w=("A3",))
        s.dma("qa", (lambda cc=cc: (lambda e: e.dma_start(
            out=dd["h_re_dram"][cc * 128:(cc + 1) * 128, :], in_=sb["B3"][:, :])))(),
            r=("B3",), w=("h_re_dram",))
        s.dma("qa", (lambda cc=cc: (lambda e: e.dma_start(
            out=dd["h_im_dram"][cc * 128:(cc + 1) * 128, :], in_=sb["A3"][:, :])))(),
            r=("A3",), w=("h_im_dram",))
        def mm_st(e, cc=cc, src="B3", qb=0, sqb="U3"):
            for u in range(32):
                for q, buf in [(qb, src), (qb + 2, sqb)]:
                    c = cc * 128 + q * 32 + u
                    i = e.matmul(ps["b3"][:, c:c + 1],
                                 sb[buf][:, u * 128:(u + 1) * 128],
                                 sb["c_ones_col"][:, :],
                                 start=True, stop=True)
            return i
        s.op("pool", lambda e: e.tensor_mul(sb["U3"][:, :], sb["B3"][:, :], sb["B3"][:, :]),
             r=("B3",), w=("U3",))
        s.op("pe", (lambda f=mm_st: (lambda e: f(e)))(),
             r=("B3", "U3", "c_ones_col"), w=("pb3",), tag="stf")
        s.op("dve", lambda e: e.tensor_mul(sb["C3"][:, :], sb["A3"][:, :], sb["A3"][:, :]),
             r=("A3",), w=("C3",))
        s.op("pe", (lambda f=mm_st: (lambda e: f(e, src="A3", qb=1, sqb="C3")))(),
             r=("A3", "C3", "c_ones_col"), w=("pb3",), tag="stf")
    s.op("dve", lambda e: e.tensor_copy(sb["strows"][:, :], ps["b3"][:, 0:128]),
         r=("pb3",), w=("strows",))
    for cc in range(1, 4):
        s.op("dve", (lambda cc=cc: (lambda e: e.tensor_tensor(
            sb["strows"][:, :], sb["strows"][:, :],
            ps["b3"][:, cc * 128:(cc + 1) * 128], ALU.add)))(),
            r=("pb3", "strows"), w=("strows",))
    for q, (dn, row) in enumerate([("st_sum_in", 0), ("st_sum_in", 1),
                                   ("st_sq_in", 0), ("st_sq_in", 1)]):
        stb = f"st32_{q % 2}"
        tb = "b5" if q % 2 == 0 else "b6"
        s.op("pe", (lambda q=q, tb=tb: (lambda e: e.transpose(
            ps[tb][0:32, 0:128], sb["strows"][:, q * 32:(q + 1) * 32],
            sb["c_ident_f32"][:, :])))(),
            r=("strows", "c_ident_f32"), w=(f"p{tb}",), tag="st_tr")
        s.op("dve", (lambda stb=stb, tb=tb: (lambda e: e.tensor_copy(
            sb[stb][:, :], ps[tb][0:32, 0:128])))(), r=(f"p{tb}",), w=(stb,))
        s.dma("qa", (lambda dn=dn, row=row, stb=stb: (lambda e: e.dma_start(
            out=dd[dn][row:row + 1, :].rearrange("r (u p) -> (r u) p", p=128),
            in_=sb[stb][:, :])))(), r=(stb,), w=(dn,))
    s.fence()
    sc3.close()
    k.scopes.remove(sc3)
    emit_p3c2(k)


def emit_p3c2(k):
    nc, s, sb, ps, di, dd, alloc = k.nc, k.s, k.sb, k.ps, k.di, k.dd, k.alloc
    sc = contextlib.ExitStack()
    k.scopes.append(sc)
    k.load_consts(["bphi_re_c", "bphi_im_c", "sel8"], sc)
    for i in range(4):
        alloc(f"h_re_t{i}", [128, TT], F32, sc)
        alloc(f"h_im_t{i}", [128, TT], F32, sc)
    for i in range(2):
        alloc(f"hp_t{i}", [128, TT], F32, sc)
        alloc(f"s1_t{i}", [128, TT], F32, sc)
    alloc("c_tiny8", [8, 1], F32, sc)
    s.op("dve", lambda e: e.memset(sb["c_tiny8"][:, :], 1e-30), w=("c_tiny8",))
    alloc("w_phi_re", [128, NDC * 512], BF16, sc)
    alloc("w_phi_im", [128, NDC * 512], BF16, sc)
    for i in range(2):
        alloc(f"xn1_c{i}", [128, NDC * TT], BF16, sc)
    alloc("phi_re_t", [128, 4 * TT], F32, sc)
    alloc("phi_im_t", [128, 4 * TT], F32, sc)
    for nm in ["rb_mu", "rb_var", "rb_lnv", "rb_rstd",
               "rpr_all", "apr_all", "rpi_all", "api_all"]:
        alloc(nm, [8, TT], F32, sc)
    s.collective(lambda e: e.collective_compute(
        "AllReduce", ALU.add, replica_groups=GROUPS,
        ins=[dd["st_sum_in"][:, :]], outs=[dd["st_sum_out"][:, :]]),
        r=("st_sum_in",), w=("st_sum_out",), tag="cc_st1")
    s.collective(lambda e: e.collective_compute(
        "AllReduce", ALU.add, replica_groups=GROUPS,
        ins=[dd["st_sq_in"][:, :]], outs=[dd["st_sq_out"][:, :]]),
        r=("st_sq_in",), w=("st_sq_out",), tag="cc_st2")
    for wn, sn in [("wphi_re_T", "w_phi_re"), ("wphi_im_T", "w_phi_im")]:
        s.dma("qp", (lambda wn=wn, sn=sn: (lambda e: e.dma_start(
            out=sb[sn][:, :].rearrange("p (dc m) -> p dc m", m=512),
            in_=feat_view(di[wn]))))(), w=(sn,), tag="wphi")
    for pl, (srow, rpa, apa) in enumerate([(0, "rpr_all", "apr_all"),
                                           (1, "rpi_all", "api_all")]):
        s.dma("qs", (lambda srow=srow: (lambda e: e.dma_start(
            out=sb["rb_mu"][:, :],
            in_=dd["st_sum_out"][srow:srow + 1, :].rearrange("r (c t) -> (r c) t", t=TT))))(),
            r=("st_sum_out",), w=("rb_mu",))
        s.dma("qs", (lambda srow=srow: (lambda e: e.dma_start(
            out=sb["rb_var"][:, :],
            in_=dd["st_sq_out"][srow:srow + 1, :].rearrange("r (c t) -> (r c) t", t=TT))))(),
            r=("st_sq_out",), w=("rb_var",))
        s.op("dve", lambda e: e.tensor_scalar_mul(sb["rb_mu"][:, :], sb["rb_mu"][:, :], 1.0 / D),
             r=("rb_mu",), w=("rb_mu",))
        s.op("dve", lambda e: e.tensor_mul(sb["rb_lnv"][:, :], sb["rb_mu"][:, :], sb["rb_mu"][:, :]),
             r=("rb_mu",), w=("rb_lnv",))
        s.op("dve", lambda e: e.scalar_tensor_tensor(sb["rb_var"][:, :], sb["rb_var"][:, :],
                                                     1.0 / D, sb["rb_lnv"][:, :],
                                                     ALU.mult, ALU.subtract),
             r=("rb_var", "rb_lnv"), w=("rb_var",))
        s.op("act", lambda e: e.activation(sb["rb_lnv"][:, :], sb["rb_var"][:, :], AF.Ln,
                                           bias=sb["c_tiny8"][:, :]),
             r=("rb_var", "c_tiny8"), w=("rb_lnv",))
        s.op("act", lambda e: e.activation(sb["rb_rstd"][:, :], sb["rb_lnv"][:, :], AF.Exp,
                                           scale=0.5), r=("rb_lnv",), w=("rb_rstd",))
        s.op("dve", lambda e: e.tensor_scalar_add(sb["rb_rstd"][:, :], sb["rb_rstd"][:, :],
                                                  1e-6), r=("rb_rstd",), w=("rb_rstd",))
        s.op("dve", (lambda rpa=rpa: (lambda e: e.reciprocal(sb[rpa][:, :], sb["rb_rstd"][:, :])))(),
             r=("rb_rstd",), w=(rpa,))
        s.op("dve", (lambda rpa=rpa, apa=apa: (lambda e: e.scalar_tensor_tensor(
            sb[apa][:, :], sb["rb_mu"][:, :], -1.0, sb[rpa][:, :], ALU.mult, ALU.mult)))(),
            r=("rb_mu", rpa), w=(apa,))
    for tt in range(NTT):
        xn1c = f"xn1_c{tt % 2}"
        s.dma("qs", (lambda tt=tt, xn1c=xn1c: (lambda e: e.dma_start(
            out=sb[xn1c][:, :].rearrange("p (dc t) -> p dc t", t=TT),
            in_=dd["xn1_dram"].rearrange("p (dc t) -> p dc t", t=T)[:, :, tt * TT:(tt + 1) * TT])))(),
            r=("xn1_dram",), w=(xn1c,), tag="xn1_c")
        for cc in range(4):
            for wn, tb, bias, bank in [("w_phi_re", "phi_re_t", "c_bphi_re_c", "b0"),
                                       ("w_phi_im", "phi_im_t", "c_bphi_im_c", "b1")]:
                def mm_phi(e, wn=wn, cc=cc, bank=bank, xn1c=xn1c):
                    for dc in range(NDC):
                        i = e.matmul(ps[bank][:, 0:TT],
                                     sb[wn][:, dc * 512 + cc * 128: dc * 512 + (cc + 1) * 128],
                                     sb[xn1c][:, dc * TT:(dc + 1) * TT],
                                     start=(dc == 0), stop=(dc == NDC - 1))
                    return i
                s.op("pe", mm_phi, r=(xn1c, wn), w=(f"p{bank}",))
                s.op("act", (lambda tb=tb, bias=bias, cc=cc, bank=bank:
                             (lambda e: e.activation(
                    sb[tb][:, cc * TT:(cc + 1) * TT], ps[bank][:, 0:TT], AF.Identity,
                    bias=sb[bias][:, cc:cc + 1])))(), r=(f"p{bank}", bias), w=(tb + f"_{cc}",))
        for bank, row in [("b3", "rpr_all"), ("b4", "apr_all"),
                          ("b5", "rpi_all"), ("b6", "api_all")]:
            s.op("pe", (lambda bank=bank, row=row, tt=tt: (lambda e: e.matmul(
                ps[bank][:, 0:TT], sb["c_sel8"][:, tt * 128:(tt + 1) * 128],
                sb[row][:, :], start=True, stop=True)))(),
                r=(row, "c_sel8"), w=(f"p{bank}",))
        for cc in range(4):
            for hd, ht in [("h_re_dram", f"h_re_t{cc}"), ("h_im_dram", f"h_im_t{cc}")]:
                s.dma("qs", (lambda hd=hd, cc=cc, tt=tt, ht=ht: (lambda e: e.dma_start(
                    out=sb[ht][:, :],
                    in_=dd[hd][cc * 128:(cc + 1) * 128, tt * TT:(tt + 1) * TT])))(),
                    r=(hd,), w=(ht,), tag="h_ld2")
        for cc in range(4):
            hre, him = f"h_re_t{cc}", f"h_im_t{cc}"
            hp = "hp_t" + str(cc % 2)
            s1t = "s1_t" + str(cc % 2)
            s.op("dve", (lambda him=him, hp=hp: (lambda e: e.tensor_tensor(
                sb[hp][:, :], sb[him][:, :], ps["b5"][:, 0:TT], ALU.mult)))(),
                r=(him, "pb5"), w=(hp,))
            s.op("dve", (lambda hp=hp: (lambda e: e.tensor_tensor(
                sb[hp][:, :], sb[hp][:, :], ps["b6"][:, 0:TT], ALU.add)))(),
                r=(hp, "pb6"), w=(hp,))
            s.op("pool", (lambda cc=cc, hp=hp: (lambda e: e.tensor_tensor(
                sb[hp][:, :], sb[hp][:, :],
                sb["phi_im_t"][:, cc * TT:(cc + 1) * TT], ALU.mult)))(),
                r=(hp, f"phi_im_t_{cc}"), w=(hp,))
            s.op("dve", (lambda hre=hre, s1t=s1t: (lambda e: e.tensor_tensor(
                sb[s1t][:, :], sb[hre][:, :], ps["b3"][:, 0:TT], ALU.mult)))(),
                r=(hre, "pb3"), w=(s1t,))
            s.op("dve", (lambda s1t=s1t: (lambda e: e.tensor_tensor(
                sb[s1t][:, :], sb[s1t][:, :], ps["b4"][:, 0:TT], ALU.add)))(),
                r=(s1t, "pb4"), w=(s1t,))
            s.op("dve", (lambda cc=cc, s1t=s1t: (lambda e: e.tensor_tensor(
                sb[s1t][:, :], sb[s1t][:, :],
                sb["phi_re_t"][:, cc * TT:(cc + 1) * TT], ALU.mult)))(),
                r=(s1t, f"phi_re_t_{cc}"), w=(s1t,))
            s.op("pool", (lambda hp=hp, s1t=s1t: (lambda e: e.tensor_tensor(
                sb[s1t][:, :], sb[s1t][:, :], sb[hp][:, :], ALU.add)))(),
                r=(s1t, hp), w=(s1t,))
            s.dma("qa", (lambda cc=cc, tt=tt, s1t=s1t: (lambda e: e.dma_start(
                out=dd["s1_dram"][cc * 128:(cc + 1) * 128, tt * TT:(tt + 1) * TT],
                in_=sb[s1t][:, :])))(), r=(s1t,), w=("s1_dram",))
    s.collective(lambda e: e.collective_compute(
        "AllGather", ALU.bypass, replica_groups=GROUPS,
        ins=[dd["s1_dram"][:, :]], outs=[dd["s1_ag_out"][:, :]]),
        r=("s1_dram",), w=("s1_ag_out",), tag="cc_s1")
    s.fence()
    sc.close()
    k.scopes.remove(sc)


def emit_p4(k):
    nc, s, sb, ps, di, dd, alloc = k.nc, k.s, k.sb, k.ps, k.di, k.dd, k.alloc
    sc = contextlib.ExitStack()
    k.scopes.append(sc)
    k.load_consts(["norm_g_c", "norm_b_c", "slot_g_row", "slot_b_row"], sc)

    scln = contextlib.ExitStack()
    k.scopes.append(scln)
    k.rows_alloc(scln)
    for i in range(2):
        alloc(f"x4a{i}", [128, NDC * TT], BF16, scln)
    alloc("x4sq", [128, NDC * TT], BF16, scln)
    alloc("x4o", [128, NDC * TT], BF16, scln)
    alloc("t4f", [128, TT], F32, scln)

    def p4ln_a(tt):
        x4a = f"x4a{tt % 2}"
        s.dma("qs", (lambda tt=tt, x4a=x4a: (lambda e: e.dma_start(
            out=sb[x4a][:, :].rearrange("p (dc t) -> p dc t", t=TT),
            in_=dd["xn1_dram"].rearrange("p (dc t) -> p dc t", t=T)[:, :, tt * TT:(tt + 1) * TT])))(),
            r=("xn1_dram",), w=(x4a, "p4_src"), tag="xln_ld")
        s.op("act", (lambda x4a=x4a: (lambda e: e.activation(
            sb["x4sq"][:, :], sb[x4a][:, :], AF.Square)))(),
            r=(x4a,), w=("x4sq", "p4_sq"))
        ln_stats(k, "p4",
                 (lambda x4a=x4a: (lambda dc: sb[x4a][:, dc * TT:(dc + 1) * TT]))(),
                 lambda dc: sb["x4sq"][:, dc * TT:(dc + 1) * TT], par=str(tt % 2))

    def p4ln_b(tt):
        x4a = f"x4a{tt % 2}"
        ln_finish(k, par=str(tt % 2))
        for dc in range(NDC):
            s.op("dve", (lambda dc=dc, x4a=x4a: (lambda e: e.tensor_tensor(
                sb["t4f"][:, :], sb[x4a][:, dc * TT:(dc + 1) * TT],
                ps["b3"][:, 0:TT], ALU.mult)))(), r=(x4a, "pb3"), w=("t4f",))
            s.op("dve", (lambda dc=dc: (lambda e: e.tensor_tensor(
                sb["t4f"][:, :], sb["t4f"][:, :], ps["b4"][:, 0:TT], ALU.add)))(),
                r=("t4f", "pb4"), w=("t4f",))
            s.op("dve", (lambda dc=dc: (lambda e: e.tensor_scalar(
                sb["x4o"][:, dc * TT:(dc + 1) * TT], sb["t4f"][:, :],
                sb["c_norm_g_c"][:, dc:dc + 1], sb["c_norm_b_c"][:, dc:dc + 1],
                ALU.mult, ALU.add)))(),
                r=("t4f", "c_norm_g_c", "c_norm_b_c"), w=("x4o",))
        s.dma("qa", (lambda tt=tt: (lambda e: e.dma_start(
            out=dd["xln_dram"].rearrange("p (dc t) -> p dc t", t=T)[:, :, tt * TT:(tt + 1) * TT],
            in_=sb["x4o"][:, :].rearrange("p (dc t) -> p dc t", t=TT))))(),
            r=("x4o",), w=("xln_dram",))

    for i in range(NTT + 1):
        if i < NTT:
            p4ln_a(i)
        if i >= 1:
            p4ln_b(i - 1)

    s.fence()
    scln.close()
    k.scopes.remove(scln)
    alloc("se_T_own", [128, NDC * 512], BF16, sc)
    alloc("se_T_glob", [128, NDC * D], BF16, sc)
    alloc("slots_bf", [128, NDC * 512], BF16, sc)
    scse = contextlib.ExitStack()
    k.scopes.append(scse)
    alloc("gb_b", [128, D], F32, scse)
    alloc("bb_b", [128, D], F32, scse)
    alloc("sc_col", [128, 1], F32, scse)
    alloc("se_in", [128, D], F32, scse)
    alloc("se_sq", [128, D], F32, scse)
    alloc("se_ln", [128, D], BF16, scse)
    for nm in ["se_mu", "se_var", "se_rs"]:
        alloc(nm, [128, 1], F32, scse)
    s.op("pe", lambda e: e.matmul(ps["b0"][:, 0:1], sb["c_ones_row"][:, :],
                                  sb["c_scale_s"][:, :], start=True, stop=True),
         r=("c_scale_s", "c_ones_row"), w=("pb0",))
    s.op("dve", lambda e: e.tensor_copy(sb["sc_col"][:, :], ps["b0"][:, 0:1]),
         r=("pb0",), w=("sc_col",))
    for row, plane in [("c_slot_g_row", "gb_b"), ("c_slot_b_row", "bb_b")]:
        for h in range(2):
            s.op("pe", (lambda row=row, h=h: (lambda e: e.matmul(
                ps["b0"][:, 0:TT], sb["c_ones_row"][:, :],
                sb[row][0:1, h * TT:(h + 1) * TT], start=True, stop=True)))(),
                r=(row, "c_ones_row"), w=("pb0",))
            s.op("dve", (lambda plane=plane, h=h: (lambda e: e.tensor_copy(
                sb[plane][:, h * TT:(h + 1) * TT], ps["b0"][:, 0:TT])))(),
                r=("pb0",), w=(plane,))
    for which, src_name, width, nchunks in [("own", "se_perm", 512, 4),
                                            ("glob", "se_glob", D, 8)]:
        for ch in range(nchunks):
            s.dma("qs", (lambda src_name=src_name, ch=ch: (lambda e: e.dma_start(
                out=sb["se_in"][:, :], in_=di[src_name][ch * 128:(ch + 1) * 128, :])))(),
                w=("se_in",), tag="se_ld")
            s.op("dve", lambda e: e.reduce_sum(sb["se_mu"][:, :], sb["se_in"][:, :], axis=AX),
                 r=("se_in",), w=("se_mu",))
            s.op("dve", lambda e: e.tensor_scalar_mul(sb["se_mu"][:, :], sb["se_mu"][:, :], -1.0 / D),
                 r=("se_mu",), w=("se_mu",))
            s.op("dve", lambda e: e.tensor_scalar_add(sb["se_in"][:, :], sb["se_in"][:, :],
                                                      sb["se_mu"][:, :]),
                 r=("se_in", "se_mu"), w=("se_in",))
            s.op("act", lambda e: e.activation(sb["se_sq"][:, :], sb["se_in"][:, :], AF.Square),
                 r=("se_in",), w=("se_sq",))
            s.op("dve", lambda e: e.reduce_sum(sb["se_var"][:, :], sb["se_sq"][:, :], axis=AX),
                 r=("se_sq",), w=("se_var",))
            s.op("act", lambda e: e.activation(sb["se_var"][:, :], sb["se_var"][:, :], AF.Ln,
                                               scale=1.0 / D, bias=sb["c_eps128"][:, :]),
                 r=("se_var", "c_eps128"), w=("se_var",))
            s.op("act", lambda e: e.activation(sb["se_rs"][:, :], sb["se_var"][:, :], AF.Exp,
                                               scale=-0.5), r=("se_var",), w=("se_rs",))
            s.op("dve", lambda e: e.tensor_mul(sb["se_rs"][:, :], sb["se_rs"][:, :], sb["sc_col"][:, :]),
                 r=("se_rs", "sc_col"), w=("se_rs",))
            s.op("dve", lambda e: e.tensor_scalar_mul(sb["se_in"][:, :], sb["se_in"][:, :],
                                                      sb["se_rs"][:, :]),
                 r=("se_in", "se_rs"), w=("se_in",))
            s.op("dve", lambda e: e.tensor_mul(sb["se_in"][:, :], sb["se_in"][:, :], sb["gb_b"][:, :]),
                 r=("se_in", "gb_b"), w=("se_in",))
            s.op("dve", lambda e: e.tensor_tensor(sb["se_ln"][:, :], sb["se_in"][:, :],
                                                  sb["bb_b"][:, :], ALU.add),
                 r=("se_in", "bb_b"), w=("se_ln",))
            tgt = "se_T_own" if which == "own" else "se_T_glob"
            for dcd in range(NDC):
                s.op("pe", (lambda dcd=dcd: (lambda e: e.transpose(
                    ps["t0"][:, 0:128], sb["se_ln"][:, dcd * 128:(dcd + 1) * 128],
                    sb["c_ident_bf"][:, :])))(),
                    r=("se_ln", "c_ident_bf"), w=("pt0",), tag="tr_se")
                s.op("dve", (lambda tgt=tgt, dcd=dcd, ch=ch, width=width: (lambda e: e.tensor_copy(
                    sb[tgt][:, dcd * width + ch * 128: dcd * width + (ch + 1) * 128],
                    ps["t0"][:, 0:128])))(), r=("pt0",), w=(tgt,))

    s.fence()
    scse.close()
    k.scopes.remove(scse)
    scdi = contextlib.ExitStack()
    k.scopes.append(scdi)
    alloc("dispT", [128, 32 * 512], BF16, scdi)
    alloc("xln_a", [128, NDC * TT], BF16, scdi)
    alloc("xln_b", [128, NDC * TT], BF16, scdi)
    alloc("runm", [128, 4], F32, scdi)
    alloc("negm4", [128, 4], F32, scdi)
    alloc("zs", [128, 4], F32, scdi)
    alloc("tc4", [128, 1], F32, scdi)
    alloc("tc5", [128, 1], F32, scdi)
    alloc("tnb", [128, 128], BF16, scdi)
    scd = contextlib.ExitStack()
    k.scopes.append(scd)
    alloc("dispE", [128, 4 * T], BF16, scd)
    def mm_logit_own(e, xnb, esc, bank="b0"):
        for dcd in range(NDC):
            i = e.matmul(ps[bank][:, 0:TT],
                         sb["se_T_own"][:, dcd * 512 + esc * 128: dcd * 512 + (esc + 1) * 128],
                         sb[xnb][:, dcd * TT:(dcd + 1) * TT],
                         start=(dcd == 0), stop=(dcd == NDC - 1))
        return i
    for pas in range(2):
        for tt in range(NTT):
            xnb = "xln_a" if tt % 2 == 0 else "xln_b"
            s.dma("qs", (lambda tt=tt, xnb=xnb: (lambda e: e.dma_start(
                out=sb[xnb][:, :].rearrange("p (dc t) -> p dc t", t=TT),
                in_=dd["xln_dram"].rearrange("p (dc t) -> p dc t", t=T)[:, :, tt * TT:(tt + 1) * TT])))(),
                r=("xln_dram",), w=(xnb,), tag="xln_p")
            for esc in range(4):
                bank = "b0" if esc % 2 == 0 else "b1"
                tc = "tc4" if esc % 2 == 0 else "tc5"
                s.op("pe", (lambda xnb=xnb, esc=esc, bank=bank: (lambda e: mm_logit_own(e, xnb, esc, bank)))(),
                     r=(xnb, "se_T_own"), w=(f"p{bank}",))
                if pas == 0:
                    if tt == 0:
                        s.op("dve", (lambda esc=esc, bank=bank: (lambda e: e.reduce_max(
                            sb["runm"][:, esc:esc + 1], ps[bank][:, 0:TT], axis=AX)))(),
                            r=(f"p{bank}",), w=("runm",))
                    else:
                        s.op("dve", (lambda bank=bank, tc=tc: (lambda e: e.reduce_max(
                            sb[tc][:, :], ps[bank][:, 0:TT], axis=AX)))(),
                            r=(f"p{bank}",), w=(tc,))
                        s.op("dve", (lambda esc=esc, tc=tc: (lambda e: e.tensor_max(
                            sb["runm"][:, esc:esc + 1], sb["runm"][:, esc:esc + 1],
                            sb[tc][:, :])))(), r=("runm", tc), w=("runm",))
                else:
                    s.op("act", (lambda esc=esc, tt=tt, bank=bank, tc=tc: (lambda e: e.activation(
                        sb["dispE"][:, esc * T + tt * TT: esc * T + (tt + 1) * TT],
                        ps[bank][:, 0:TT], AF.Exp, bias=sb["negm4"][:, esc:esc + 1],
                        accum_out=sb[tc][:, :])))(),
                        r=(f"p{bank}", "negm4"), w=(f"dispE_{esc}_{tt}", tc))
                    if tt == 0:
                        s.op("dve", (lambda esc=esc, tc=tc: (lambda e: e.tensor_copy(
                            sb["zs"][:, esc:esc + 1], sb[tc][:, :])))(),
                            r=(tc,), w=("zs",))
                    else:
                        s.op("dve", (lambda esc=esc, tc=tc: (lambda e: e.tensor_tensor(
                            sb["zs"][:, esc:esc + 1], sb["zs"][:, esc:esc + 1],
                            sb[tc][:, :], ALU.add)))(), r=("zs", tc), w=("zs",))
        if pas == 0:
            s.op("dve", lambda e: e.tensor_scalar_mul(sb["negm4"][:, :], sb["runm"][:, :], -1.0),
                 r=("runm",), w=("negm4",))
    s.op("dve", lambda e: e.reciprocal(sb["zs"][:, :], sb["zs"][:, :]), r=("zs",), w=("zs",))
    for esc in range(4):
        for nt in range(32):
            s.op("dve", (lambda esc=esc, nt=nt: (lambda e: e.tensor_scalar_mul(
                sb["tnb"][:, :], sb["dispE"][:, esc * T + nt * 128: esc * T + (nt + 1) * 128],
                sb["zs"][:, esc:esc + 1])))(),
                r=(f"dispE_{esc}_{nt // 4}", "zs"), w=("tnb",))
            s.op("pe", lambda e: e.transpose(ps["t0"][:, 0:128], sb["tnb"][:, :],
                                             sb["c_ident_bf"][:, :]),
                 r=("tnb", "c_ident_bf"), w=("pt0",), tag="tr_d")
            s.op("dve", (lambda esc=esc, nt=nt: (lambda e: e.tensor_copy(
                sb["dispT"][:, nt * 512 + esc * 128: nt * 512 + (esc + 1) * 128],
                ps["t0"][:, 0:128])))(), r=("pt0",), w=("dispT",))
    s.fence()
    scd.close()
    k.scopes.remove(scd)

    alloc("xln_nt", [128, 4 * NDC * 128], BF16, scdi)
    alloc("xlnT", [128, NDC * 128], BF16, scdi)
    alloc("slots_f", [128, NDC * 128 * 4], F32, scdi)
    s.op("dve", lambda e: e.memset(sb["slots_f"][:, :], 0.0), w=("slots_f",))
    for g in range(8):
        for dcd in range(NDC):
            bank = "b0" if dcd % 2 == 0 else "b1"
            for ntl in range(4):
                nt = g * 4 + ntl
                if dcd == 0:
                    s.dma("qs", (lambda nt=nt, ntl=ntl: (lambda e: e.dma_start(
                        out=sb["xln_nt"][:, ntl * 1024:(ntl + 1) * 1024].rearrange(
                            "p (dc t) -> p dc t", t=128),
                        in_=dd["xln_dram"].rearrange("p (dc t) -> p dc t", t=T)[:, :, nt * 128:(nt + 1) * 128])))(),
                        r=("xln_dram",), w=(f"xlnnt_{ntl}",), tag="xln_s")
            for ntl in range(4):
                nt = g * 4 + ntl
                s.op("pe", (lambda dcd=dcd, ntl=ntl: (lambda e: e.transpose(
                    ps["t0"][:, 0:128],
                    sb["xln_nt"][:, ntl * 1024 + dcd * 128: ntl * 1024 + (dcd + 1) * 128],
                    sb["c_ident_bf"][:, :])))(),
                    r=(f"xlnnt_{ntl}", "c_ident_bf"), w=("pt0",), tag="tr_x")
                s.op("dve", (lambda ntl=ntl: (lambda e: e.tensor_copy(
                    sb["xlnT"][:, ntl * 128:(ntl + 1) * 128], ps["t0"][:, 0:128])))(),
                    r=("pt0",), w=("xlnT",))
                s.op("pe", (lambda ntl=ntl, nt=nt, bank=bank: (lambda e: e.matmul(
                    ps[bank][:, 0:TT], sb["xlnT"][:, ntl * 128:(ntl + 1) * 128],
                    sb["dispT"][:, nt * 512:(nt + 1) * 512],
                    start=(ntl == 0), stop=(ntl == 3))))(),
                    r=("xlnT", "dispT"), w=(f"p{bank}",), tag="slots_mm")
            s.op("dve", (lambda dcd=dcd, bank=bank: (lambda e: e.tensor_tensor(
                sb["slots_f"][:, dcd * 512:(dcd + 1) * 512],
                sb["slots_f"][:, dcd * 512:(dcd + 1) * 512], ps[bank][:, 0:TT], ALU.add)))(),
                r=("slots_f", f"p{bank}"), w=("slots_f",))
    s.op("dve", lambda e: e.tensor_copy(sb["slots_bf"][:, :], sb["slots_f"][:, :]),
         r=("slots_f",), w=("slots_bf",))
    s.fence()
    scdi.close()
    k.scopes.remove(scdi)

    sce = contextlib.ExitStack()
    k.scopes.append(sce)
    alloc("eo_t", [64, D], F32, sce)
    for i in range(4):
        alloc(f"wR{i}", [128, 16 * D], BF16, sce)
    alloc("bg_sb", [128, 8 * 16], F32, sce)
    alloc("bu_sb", [128, 8 * 16], F32, sce)
    alloc("bd_bf0", [1, D], BF16, sce)
    alloc("bd_bf1", [1, D], BF16, sce)
    alloc("gelu_t", [128, 16 * 64], F32, sce)
    alloc("h2_bf", [128, 16 * 64], BF16, sce)
    s.dma("qs", lambda e: e.dma_start(
        out=sb["bg_sb"][:, :].rearrange("p (e h) -> p e h", h=16),
        in_=di["bg"].rearrange("e p h -> p e h")), w=("bg_sb",))
    s.dma("qs", lambda e: e.dma_start(
        out=sb["bu_sb"][:, :].rearrange("p (e h) -> p e h", h=16),
        in_=di["bu"].rearrange("e p h -> p e h")), w=("bu_sb",))
    for el in range(8):
        wg_b = f"wR{(3 * el) % 4}"
        wu_b = f"wR{(3 * el + 1) % 4}"
        wd_b = f"wR{(3 * el + 2) % 4}"
        bd_b = f"bd_bf{el % 2}"
        for wn, bname in [("wg", wg_b), ("wu", wu_b)]:
            for hf in range(2):
                s.dma("qp", (lambda el=el, wn=wn, bname=bname, hf=hf: (lambda e: e.dma_start(
                    out=sb[bname][:, :].rearrange("p (dc h) -> p dc h", h=H2)
                        [:, hf * 4:(hf + 1) * 4, :],
                    in_=feat_view(di[wn][el])[:, hf * 4:(hf + 1) * 4, :])))(),
                    w=(f"{bname}h{hf}",), tag=f"{wn}_ld")
        for hf in range(2):
            s.dma("qp", (lambda el=el, wd_b=wd_b, hf=hf: (lambda e: e.dma_start(
                out=sb[wd_b][:, :].rearrange("p (hc o) -> p hc o", o=D)
                    [:, hf * 8:(hf + 1) * 8, :],
                in_=feat_view(di["wd"][el])[:, hf * 8:(hf + 1) * 8, :])))(),
                w=(f"{wd_b}h{hf}",), tag="wd_ld")
        s.dma("qp", (lambda el=el, bd_b=bd_b: (lambda e: e.dma_start(
            out=sb[bd_b][:, :], in_=di["bd_rows"][el:el + 1, :])))(),
            w=(bd_b,), tag="bd_ld")
        for fc in range(16):
            def mm_g(e, el=el, fc=fc, wg_b=wg_b):
                for dcd in range(NDC):
                    i = e.matmul(ps["b2"][0:128, 0:64],
                                 sb[wg_b][:, dcd * H2 + fc * 128: dcd * H2 + (fc + 1) * 128],
                                 sb["slots_bf"][:, dcd * 512 + el * 64: dcd * 512 + (el + 1) * 64],
                                 start=(dcd == 0), stop=(dcd == NDC - 1))
                return i
            s.op("pe", mm_g, r=("slots_bf", f"{wg_b}h0", f"{wg_b}h1"), w=("pb2",))
            s.op("act", (lambda el=el, fc=fc: (lambda e: e.activation(
                sb["gelu_t"][:, fc * 64:(fc + 1) * 64], ps["b2"][0:128, 0:64], AF.Gelu,
                bias=sb["bg_sb"][:, el * 16 + fc: el * 16 + fc + 1])))(),
                r=("pb2", "bg_sb"), w=("gelu_t",))
        for fc in range(16):
            def mm_u(e, el=el, fc=fc, wu_b=wu_b):
                for dcd in range(NDC):
                    i = e.matmul(ps["b3"][0:128, 0:64],
                                 sb[wu_b][:, dcd * H2 + fc * 128: dcd * H2 + (fc + 1) * 128],
                                 sb["slots_bf"][:, dcd * 512 + el * 64: dcd * 512 + (el + 1) * 64],
                                 start=(dcd == 0), stop=(dcd == NDC - 1))
                return i
            s.op("pe", mm_u, r=("slots_bf", f"{wu_b}h0", f"{wu_b}h1"), w=("pb3",))
            s.op("dve", (lambda el=el, fc=fc: (lambda e: e.scalar_tensor_tensor(
                sb["h2_bf"][:, fc * 64:(fc + 1) * 64], ps["b3"][0:128, 0:64],
                sb["bu_sb"][:, el * 16 + fc: el * 16 + fc + 1],
                sb["gelu_t"][:, fc * 64:(fc + 1) * 64], ALU.add, ALU.mult)))(),
                r=("pb3", "bu_sb", "gelu_t"), w=("h2_bf",))
        for oc2 in range(2):
            def mm_d(e, oc2=oc2, wd_b=wd_b, bd_b=bd_b):
                for fc in range(16):
                    e.matmul(ps["b4"][0:64, 0:TT],
                             sb["h2_bf"][:, fc * 64:(fc + 1) * 64],
                             sb[wd_b][:, fc * D + oc2 * TT: fc * D + (oc2 + 1) * TT],
                             start=(fc == 0), stop=False)
                return e.matmul(ps["b4"][0:64, 0:TT], sb["c_ones_row_bf"][0:1, 0:64],
                                sb[bd_b][0:1, oc2 * TT:(oc2 + 1) * TT],
                                start=False, stop=True)
            s.op("pe", (lambda f=mm_d: (lambda e: f(e)))(),
                 r=("h2_bf", f"{wd_b}h0", f"{wd_b}h1", bd_b, "c_ones_row_bf"), w=("pb4",))
            s.op("dve", (lambda oc2=oc2: (lambda e: e.tensor_copy(
                sb["eo_t"][:, oc2 * TT:(oc2 + 1) * TT], ps["b4"][0:64, 0:TT])))(),
                r=("pb4",), w=("eo_t",))
        s.dma("qa", (lambda el=el: (lambda e: e.dma_start(
            out=dd["eout_ag_in"][el * 64:(el + 1) * 64, :], in_=sb["eo_t"][:, :])))(),
            r=("eo_t",), w=("eout_ag_in",))
    s.collective(lambda e: e.collective_compute(
        "AllGather", ALU.bypass, replica_groups=GROUPS,
        ins=[dd["eout_ag_in"][:, :]], outs=[dd["eout_ag_out"][:, :]]),
        r=("eout_ag_in",), w=("eout_ag_out",), tag="cc_eo")
    s.fence()
    sce.close()
    k.scopes.remove(sce)

    scf = contextlib.ExitStack()
    k.scopes.append(scf)
    alloc("eout_bf", [128, NDC * D], BF16, scf)
    for i in range(2):
        alloc(f"xln_n4{i}", [128, NDC * 128], BF16, scf)
        alloc(f"e0b{i}", [128, TT], BF16, scf)
        alloc(f"e1b{i}", [128, TT], BF16, scf)
        alloc(f"combT{i}", [128, NDC * 128], BF16, scf)
        alloc(f"tnb2{i}", [128, 128], BF16, scf)
        alloc(f"g1f{i}", [128, 128], F32, scf)
        alloc(f"x1_n{i}", [128, NDC * 128], F32, scf)
        alloc(f"s1_n{i}", [128, NDC * 128], F32, scf)
        alloc(f"out_n{i}", [128, NDC * 128], F32, scf)
        for nm in ["cm1", "cm2", "cz"]:
            alloc(f"{nm}{i}", [128, 1], F32, scf)
    s.dma("qp", lambda e: e.dma_start(
        out=sb["eout_bf"][:, :].rearrange("p (ec o) -> p ec o", o=D),
        in_=feat_view(dd["eout_ag_out"])), r=("eout_ag_out",), w=("eout_bf",))
    for nt in range(32):
        p = str(nt % 2)
        xlnn, e0b, e1b, combT, tnb2, g1f = (f"xln_n4{p}", f"e0b{p}", f"e1b{p}",
                                            f"combT{p}", f"tnb2{p}", f"g1f{p}")
        x1n, s1n, outn = f"x1_n{p}", f"s1_n{p}", f"out_n{p}"
        cm1, cm2, cz = f"cm1{p}", f"cm2{p}", f"cz{p}"
        bA, bB = ("b0", "b1") if nt % 2 == 0 else ("b4", "b5")
        bF = "b2" if nt % 2 == 0 else "b3"
        s.dma("qs", (lambda nt=nt, xlnn=xlnn: (lambda e: e.dma_start(
            out=sb[xlnn][:, :].rearrange("p (dc t) -> p dc t", t=128),
            in_=dd["xln_dram"].rearrange("p (dc t) -> p dc t", t=T)[:, :, nt * 128:(nt + 1) * 128])))(),
            r=("xln_dram",), w=(xlnn,), tag="xln_f")
        s.dma("qs", (lambda nt=nt, x1n=x1n: (lambda e: e.dma_start(
            out=sb[x1n][:, :].rearrange("p (dc t) -> p dc t", t=128),
            in_=feat_view(dd["x1_dram"])[:, :, nt * 128:(nt + 1) * 128])))(),
            r=("x1_dram",), w=(x1n,), tag="x1_f")
        s.dma("qs", (lambda nt=nt, s1n=s1n: (lambda e: e.dma_start(
            out=sb[s1n][:, :].rearrange("p (dc t) -> p dc t", t=128),
            in_=feat_view(dd["s1_ag_out"])[:, :, nt * 128:(nt + 1) * 128])))(),
            r=("s1_ag_out",), w=(s1n,), tag="s1_f")
        for h in range(2):
            bank = bA if h == 0 else bB
            def mm_cl(e, h=h, bank=bank, xlnn=xlnn):
                for dcd in range(NDC):
                    i = e.matmul(ps[bank][:, 0:TT],
                                 sb[xlnn][:, dcd * 128:(dcd + 1) * 128],
                                 sb["se_T_glob"][:, dcd * D + h * TT: dcd * D + (h + 1) * TT],
                                 start=(dcd == 0), stop=(dcd == NDC - 1))
                return i
            s.op("pe", (lambda f=mm_cl: (lambda e: f(e)))(),
                 r=(xlnn, "se_T_glob"), w=(f"p{bank}",))
        s.op("dve", (lambda cm1=cm1, bA=bA: (lambda e: e.reduce_max(
            sb[cm1][:, :], ps[bA][:, 0:TT], axis=AX)))(), r=(f"p{bA}",), w=(cm1,))
        s.op("dve", (lambda cm2=cm2, bB=bB: (lambda e: e.reduce_max(
            sb[cm2][:, :], ps[bB][:, 0:TT], axis=AX)))(), r=(f"p{bB}",), w=(cm2,))
        s.op("dve", (lambda cm1=cm1, cm2=cm2: (lambda e: e.tensor_max(
            sb[cm1][:, :], sb[cm1][:, :], sb[cm2][:, :])))(), r=(cm1, cm2), w=(cm1,))
        s.op("dve", (lambda cm1=cm1: (lambda e: e.tensor_scalar_mul(
            sb[cm1][:, :], sb[cm1][:, :], -1.0)))(), r=(cm1,), w=(cm1,))
        s.op("act", (lambda e0b=e0b, cm1=cm1, cm2=cm2, bA=bA: (lambda e: e.activation(
            sb[e0b][:, :], ps[bA][:, 0:TT], AF.Exp,
            bias=sb[cm1][:, :], accum_out=sb[cm2][:, :])))(),
            r=(f"p{bA}", cm1), w=(e0b, cm2))
        s.op("act", (lambda e1b=e1b, cm1=cm1, cz=cz, bB=bB: (lambda e: e.activation(
            sb[e1b][:, :], ps[bB][:, 0:TT], AF.Exp,
            bias=sb[cm1][:, :], accum_out=sb[cz][:, :])))(),
            r=(f"p{bB}", cm1), w=(e1b, cz))
        s.op("dve", (lambda cz=cz, cm2=cm2: (lambda e: e.tensor_tensor(
            sb[cz][:, :], sb[cz][:, :], sb[cm2][:, :], ALU.add)))(), r=(cz, cm2), w=(cz,))
        s.op("act", (lambda cz=cz: (lambda e: e.activation(
            sb[cz][:, :], sb[cz][:, :], AF.Ln)))(), r=(cz,), w=(cz,))
        s.op("act", (lambda cz=cz: (lambda e: e.activation(
            sb[cz][:, :], sb[cz][:, :], AF.Exp, scale=-1.0)))(), r=(cz,), w=(cz,))
        for esc in range(NDC):
            ebuf = e0b if esc < 4 else e1b
            eoff = (esc % 4) * 128
            reg = esc % 4
            s.op("dve", (lambda ebuf=ebuf, eoff=eoff, tnb2=tnb2, cz=cz: (lambda e: e.tensor_scalar_mul(
                sb[tnb2][:, :], sb[ebuf][:, eoff:eoff + 128], sb[cz][:, :])))(),
                r=(ebuf, cz), w=(tnb2,))
            s.op("pe", (lambda tnb2=tnb2, reg=reg: (lambda e: e.transpose(
                ps["t0"][:, reg * 128:(reg + 1) * 128], sb[tnb2][:, :],
                sb["c_ident_bf"][:, :])))(),
                r=(tnb2, "c_ident_bf"), w=(f"pt0_{reg}",), tag="tr_c")
            s.op("dve", (lambda esc=esc, combT=combT, reg=reg: (lambda e: e.tensor_copy(
                sb[combT][:, esc * 128:(esc + 1) * 128], ps["t0"][:, reg * 128:(reg + 1) * 128])))(),
                r=(f"pt0_{reg}",), w=(combT,))
        for oc in range(NDC):
            def mm_f(e, oc=oc, combT=combT, bF=bF):
                for esc in range(NDC):
                    i = e.matmul(ps[bF][:, 0:128],
                                 sb["eout_bf"][:, esc * D + oc * 128: esc * D + (oc + 1) * 128],
                                 sb[combT][:, esc * 128:(esc + 1) * 128],
                                 start=(esc == 0), stop=(esc == NDC - 1))
                return i
            s.op("pe", (lambda f=mm_f: (lambda e: f(e)))(),
                 r=(combT, "eout_bf"), w=(f"p{bF}",))
            s.op("act", (lambda oc=oc, g1f=g1f, bF=bF: (lambda e: e.activation(
                sb[g1f][:, :], ps[bF][:, 0:128], AF.Silu)))(),
                r=(f"p{bF}",), w=(g1f,))
            s.op("dve", (lambda oc=oc, g1f=g1f, s1n=s1n: (lambda e: e.tensor_tensor(
                sb[g1f][:, :], sb[g1f][:, :],
                sb[s1n][:, oc * 128:(oc + 1) * 128], ALU.mult)))(),
                r=(g1f, s1n), w=(g1f,))
            s.op("dve", (lambda oc=oc, g1f=g1f, x1n=x1n, outn=outn: (lambda e: e.tensor_tensor(
                sb[outn][:, oc * 128:(oc + 1) * 128], sb[g1f][:, :],
                sb[x1n][:, oc * 128:(oc + 1) * 128], ALU.add)))(),
                r=(g1f, x1n), w=(outn,))
        s.dma("qa", (lambda nt=nt, outn=outn: (lambda e: e.dma_start(
            out=feat_view(k.out)[:, :, nt * 128:(nt + 1) * 128],
            in_=sb[outn][:, :].rearrange("p (dc t) -> p dc t", t=128))))(),
            r=(outn,), w=("out_fm",))
    s.fence()
    scf.close()
    k.scopes.remove(scf)


_PROG_CACHE = {}


def kernel(**inputs):
    import os
    global BUILD_PHASES, DEBUG_TAPS
    if os.environ.get("MK_PHASES"):
        BUILD_PHASES = tuple(os.environ["MK_PHASES"].split(","))
    if os.environ.get("MK_TAPS"):
        DEBUG_TAPS = tuple(os.environ["MK_TAPS"].split(","))
    maps = host_prep(inputs)
    key = (tuple(BUILD_PHASES), tuple(DEBUG_TAPS))
    if key not in _PROG_CACHE:
        _PROG_CACHE[key] = build_program(BUILD_PHASES, DEBUG_TAPS)
    nc = _PROG_CACHE[key]
    res = run_bass_kernel_spmd(nc, maps, core_ids=list(range(8)))
    outs = [res.results[2 * b]["out_fm"].T for b in range(B)]
    return np.stack(outs, axis=0).astype(np.float32)


# version 16
# speedup vs baseline: 1.0396x; 1.0396x over previous
import contextlib
import numpy as np
import ml_dtypes
import concourse.bass as bass
import concourse.mybir as mybir
from concourse.bass_utils import run_bass_kernel_spmd

F32 = mybir.dt.float32
BF16 = mybir.dt.bfloat16
I32 = mybir.dt.int32
AF = mybir.ActivationFunctionType
ALU = mybir.AluOpType
AX = mybir.AxisListType.X

B, T, D = 4, 4096, 1024
H2 = 2048
NHP = 4
TT = 512
NTT = T // TT
NDC = D // 128
EPS_LN = 1e-5
BF = ml_dtypes.bfloat16

BUILD_PHASES = ("p0", "p1", "p2", "p3", "p4")
DEBUG_TAPS = ()


def _cc(a):
    return np.ascontiguousarray(a)


def col_chunks(v):
    v = np.asarray(v, np.float32).reshape(-1)
    n = v.shape[0] // 128
    return _cc(v.reshape(n, 128).T)


def host_prep(inp):
    x = np.asarray(inp["L0_attn_x"], np.float32)
    maps = []
    ramp = np.arange(T, dtype=np.float32).reshape(1, T)
    mask01 = np.zeros((4, 128, TT), BF)
    for v in range(4):
        ki = np.arange(128).reshape(128, 1)
        qi = np.arange(TT).reshape(1, TT)
        mask01[v] = (qi >= (ki + v * 128)).astype(BF)
    mask01 = _cc(mask01.transpose(1, 0, 2).reshape(128, 4 * TT))

    for c in range(8):
        b, j = c // 2, c % 2
        pc = np.arange(1024)
        pe8 = np.r_[4 * j:4 * j + 4, 4 * (1 - j):4 * (1 - j) + 4]
        pslots = np.r_[512 * j:512 * (j + 1), 512 * (1 - j):512 * (2 - j)]
        hs = slice(512 * j, 512 * (j + 1))
        es = slice(4 * j, 4 * (j + 1))
        e1 = slice(8 * j, 8 * (j + 1))

        m = {
            "x_fm": _cc(x[b].T[pc]),
            "ones_col": np.ones((128, 1), np.float32),
            "ones_row": np.ones((1, 128), np.float32),
            "ones_col_bf": np.ones((128, 1), BF),
            "ones_row_bf": np.ones((1, 128), BF),
            "ones64_bf": np.ones((128, 64), BF),
            "ident_bf": np.eye(128, dtype=BF),
            "ramp": ramp, "mask01": mask01,
            "wq_T": _cc(np.asarray(inp["L0_attn_Wq_w"])[hs][:, pc].T),
            "wk_T": _cc(np.asarray(inp["L0_attn_Wk_w"])[hs][:, pc].T),
            "wv_T": _cc(np.asarray(inp["L0_attn_Wv_w"])[hs][:, pc].T),
            "wo_T": _cc(np.asarray(inp["L0_attn_Wo_w"])[pc][:, hs].T),
            "qb_c": col_chunks(np.asarray(inp["L0_attn_Wq_b"])[hs]),
            "kb_c": col_chunks(np.asarray(inp["L0_attn_Wk_b"])[hs]),
            "vb_row": _cc(np.asarray(inp["L0_attn_Wv_b"])[hs].reshape(1, 512)),
            "wob_c": col_chunks(np.asarray(inp["L0_attn_Wo_b"])[pc]),
            "gate_w": _cc(np.asarray(inp["L0_ffn_gate_w"])[pc][:, pe8]),
            "gate_b": _cc(np.asarray(inp["L0_ffn_gate_b"], np.float32)[pe8].reshape(8, 1)),
            "rw1": _cc(np.asarray(inp["L0_ffn_rw1"])[es][:, pc, :]),
            "rw2": _cc(np.asarray(inp["L0_ffn_rw2"])[es][:, pc, :]),
            "rw3": _cc(np.asarray(inp["L0_ffn_rw3"])[es][:, :, pc]),
            "rb1": _cc(np.stack([col_chunks(v) for v in np.asarray(inp["L0_ffn_rb1"])[es]])),
            "rb2": _cc(np.stack([col_chunks(v) for v in np.asarray(inp["L0_ffn_rb2"])[es]])),
            "sw1": _cc(np.asarray(inp["L0_ffn_sw1"])[j][pc]),
            "sw2": _cc(np.asarray(inp["L0_ffn_sw2"])[j][pc]),
            "sw3": _cc(np.asarray(inp["L0_ffn_sw3"])[j][:, pc]),
            "sb1": col_chunks(np.asarray(inp["L0_ffn_sb1"])[j]),
            "sb2": col_chunks(np.asarray(inp["L0_ffn_sb2"])[j]),
            "b3_stack": _cc(np.concatenate([
                np.asarray(inp["L0_ffn_sb3"])[j:j + 1][:, pc],
                np.asarray(inp["L0_ffn_rb3"])[es][:, pc]], axis=0)),
            "ident_f32": np.eye(128, dtype=np.float32),
            "sel8": np.repeat(np.eye(8, dtype=np.float32), 128, axis=1),
            "sel4": _cc(np.stack([np.where(np.arange(4)[:, None] == e4,
                                            np.ones((4, 128), np.float32), 0.0)
                                   for e4 in range(4)], axis=1).reshape(4, 512).astype(BF)),
            "dt_s": _cc(np.asarray(inp["L1_attn_dt"], np.float32).reshape(1, 1)),
            "scale_s": _cc(np.asarray(inp["L1_ffn_scale"], np.float32).reshape(1, 1)),
        }
        wpsi = np.asarray(inp["L1_attn_W_psi"])
        wphi = np.asarray(inp["L1_attn_W_phi"])
        bpsi = np.asarray(inp["L1_attn_b_psi"])
        bphi = np.asarray(inp["L1_attn_b_phi"])
        m["wpsi_re_T"] = _cc(wpsi[hs][:, pc].T)
        m["wpsi_im_T"] = _cc(wpsi[1024 + hs.start:1024 + hs.stop][:, pc].T)
        m["wphi_re_T"] = _cc(wphi[hs][:, pc].T)
        m["wphi_im_T"] = _cc(wphi[1024 + hs.start:1024 + hs.stop][:, pc].T)
        m["wgate_T"] = _cc(np.asarray(inp["L1_attn_W_gate"])[hs][:, pc].T)
        m["bpsi_re_c"] = col_chunks(bpsi[hs])
        m["bpsi_im_c"] = col_chunks(bpsi[1024 + hs.start:1024 + hs.stop])
        m["bphi_re_c"] = col_chunks(bphi[hs])
        m["bphi_im_c"] = col_chunks(bphi[1024 + hs.start:1024 + hs.stop])
        m["bgate_c"] = col_chunks(np.asarray(inp["L1_attn_b_gate"])[hs])
        m["omega_c"] = col_chunks(np.asarray(inp["L1_attn_omega"])[hs])
        m["lgam_c"] = col_chunks(np.asarray(inp["L1_attn_log_gamma"])[hs])
        m["norm_g_c"] = col_chunks(np.asarray(inp["L1_ffn_norm_g"])[pc])
        m["norm_b_c"] = col_chunks(np.asarray(inp["L1_ffn_norm_b"])[pc])
        m["slot_g_row"] = _cc(np.asarray(inp["L1_ffn_slot_g"], np.float32)[pc].reshape(1, 1024))
        m["slot_b_row"] = _cc(np.asarray(inp["L1_ffn_slot_b"], np.float32)[pc].reshape(1, 1024))
        se = np.asarray(inp["L1_ffn_slot_embeds"]).reshape(1024, 1024)[:, pc]
        m["se_perm"] = _cc(se[pslots])
        m["se_glob"] = _cc(se)
        m["wg"] = _cc(np.asarray(inp["L1_ffn_Wg"])[e1][:, pc, :])
        m["wu"] = _cc(np.asarray(inp["L1_ffn_Wu"])[e1][:, pc, :])
        m["wd"] = _cc(np.asarray(inp["L1_ffn_Wd"])[e1][:, :, pc])
        m["bg"] = _cc(np.stack([col_chunks(v) for v in np.asarray(inp["L1_ffn_bg"])[e1]]))
        m["bu"] = _cc(np.stack([col_chunks(v) for v in np.asarray(inp["L1_ffn_bu"])[e1]]))
        m["bd_rows"] = _cc(np.asarray(inp["L1_ffn_bd"])[e1][:, pc])
        maps.append(m)
    return maps


ENG_SECTIONS = ["pe", "dve", "act", "pool", "sync"]
QUEUES = {"qs": "sync", "qa": "sync", "qp": "pool"}
QCHAINS = {"qs": 6, "qa": 3, "qp": 6}
QSEMS = [f"{q}{i}" for q, n in QCHAINS.items() for i in range(n)]


class Op:
    __slots__ = ("eng", "fn", "reads", "writes", "sem", "step", "waits", "cnt", "tag")

    def __init__(self, eng, fn, reads, writes, sem, step, tag=""):
        self.eng, self.fn = eng, fn
        self.reads, self.writes = tuple(reads), tuple(writes)
        self.sem, self.step = sem, step
        self.waits, self.cnt, self.tag = {}, 0, tag


class Sched:
    def __init__(self):
        self.ops = []
        self.cnt = {s: 0 for s in ["pe", "dve", "act", "pool", "cc"] + QSEMS}
        self.rr = {q: 0 for q in QCHAINS}

    def op(self, eng, fn, r=(), w=(), tag=""):
        o = Op(eng, fn, r, w, eng, 1, tag)
        self.ops.append(o)
        return o

    def dma(self, queue, fn, r=(), w=(), tag=""):
        ch = self.rr[queue] % QCHAINS[queue]
        self.rr[queue] += 1
        o = Op(QUEUES[queue], fn, r, w, f"{queue}{ch}", 16, tag)
        self.ops.append(o)
        return o

    def collective(self, fn, r=(), w=(), tag=""):
        o = Op("pool", fn, r, w, "cc", 1, tag)
        self.ops.append(o)
        return o

    def fence(self):
        self.ops.append("FENCE")

    def plan(self):
        last_w = {}
        readers = {}
        cnt = {k: 0 for k in self.cnt}
        fence_need = {}
        real_ops = []
        for o in self.ops:
            if o == "FENCE":
                fence_need = {s: v for s, v in cnt.items() if v > 0 and s != "cc"}
                continue
            real_ops.append(o)
            cnt[o.sem] += o.step
            o.cnt = cnt[o.sem]
            need = dict(fence_need)
            for key in o.reads:
                lw = last_w.get(key)
                if lw:
                    s, v = lw
                    need[s] = max(need.get(s, 0), v)
            for key in o.writes:
                lw = last_w.get(key)
                if lw:
                    s, v = lw
                    need[s] = max(need.get(s, 0), v)
                for s, v in readers.get(key, {}).items():
                    need[s] = max(need.get(s, 0), v)
            if o.step == 16:
                need[o.sem] = max(need.get(o.sem, 0), o.cnt - o.step)
            if o.sem in need and need[o.sem] >= o.cnt:
                need[o.sem] = o.cnt - o.step
            o.waits = need
            for key in o.reads:
                readers.setdefault(key, {})[o.sem] = o.cnt
            for key in o.writes:
                last_w[key] = (o.sem, o.cnt)
                readers[key] = {}
        self.ops = real_ops
        self.cnt = cnt
        marks = {}
        for o in self.ops:
            m = marks.setdefault(o.eng, {})
            o.waits = {s: v for s, v in o.waits.items() if m.get(s, 0) < v}
            for s, v in o.waits.items():
                m[s] = v

    def emit(self, nc, block, stack):
        sems = {}
        for n in ["pe", "dve", "act", "pool", "cc"] + QSEMS:
            sems[n] = stack.enter_context(nc.semaphore(n))
        eng_ops = {e: [o for o in self.ops if o.eng == e] for e in ENG_SECTIONS}

        def mk(name):
            def body(eng):
                for o in eng_ops[name]:
                    for sname, v in sorted(o.waits.items()):
                        eng.wait_ge(sems[sname], v)
                    instr = o.fn(eng)
                    instr.then_inc(sems[o.sem], o.step)
            return body

        for name, meth in [("pe", block.tensor), ("dve", block.vector),
                           ("act", block.scalar), ("pool", block.gpsimd),
                           ("sync", block.sync)]:
            if eng_ops[name]:
                meth(mk(name))


def feat_view(ap):
    return ap.rearrange("(n p) f -> p n f", p=128)


class K:
    pass


EXT_IN = {
    "x_fm": ([D, T], F32), "ones_col": ([128, 1], F32), "ones_row": ([1, 128], F32),
    "ones_col_bf": ([128, 1], BF16), "ones_row_bf": ([1, 128], BF16),
    "ones64_bf": ([128, 64], BF16), "ident_bf": ([128, 128], BF16),
    "ident_f32": ([128, 128], F32), "sel4": ([4, 512], BF16),
    "sel8": ([8, 1024], F32),
    "ramp": ([1, T], F32), "mask01": ([128, 4 * TT], BF16),
    "wq_T": ([D, 512], F32), "wk_T": ([D, 512], F32), "wv_T": ([D, 512], F32),
    "wo_T": ([512, D], F32),
    "qb_c": ([128, 4], F32), "kb_c": ([128, 4], F32), "vb_row": ([1, 512], F32),
    "wob_c": ([128, 8], F32),
    "gate_w": ([D, 8], F32), "gate_b": ([8, 1], F32),
    "rw1": ([4, D, H2], F32), "rw2": ([4, D, H2], F32), "rw3": ([4, H2, D], F32),
    "rb1": ([4, 128, 16], F32), "rb2": ([4, 128, 16], F32),
    "sw1": ([D, H2], F32), "sw2": ([D, H2], F32), "sw3": ([H2, D], F32),
    "sb1": ([128, 16], F32), "sb2": ([128, 16], F32), "b3_stack": ([5, D], F32),
    "wpsi_re_T": ([D, 512], F32), "wpsi_im_T": ([D, 512], F32),
    "wphi_re_T": ([D, 512], F32), "wphi_im_T": ([D, 512], F32),
    "wgate_T": ([D, 512], F32),
    "bpsi_re_c": ([128, 4], F32), "bpsi_im_c": ([128, 4], F32),
    "bphi_re_c": ([128, 4], F32), "bphi_im_c": ([128, 4], F32),
    "bgate_c": ([128, 4], F32),
    "omega_c": ([128, 4], F32), "lgam_c": ([128, 4], F32), "dt_s": ([1, 1], F32),
    "norm_g_c": ([128, 8], F32), "norm_b_c": ([128, 8], F32),
    "slot_g_row": ([1, D], F32), "slot_b_row": ([1, D], F32), "scale_s": ([1, 1], F32),
    "se_perm": ([D, D], F32), "se_glob": ([D, D], F32),
    "wg": ([8, D, H2], F32), "wu": ([8, D, H2], F32), "wd": ([8, H2, D], F32),
    "bg": ([8, 128, 16], F32), "bu": ([8, 128, 16], F32), "bd_rows": ([8, D], F32),
}

DSCR = {
    "xn_dram": ([128, NDC * T], BF16),
    "xn1_dram": ([128, NDC * T], BF16),
    "s_ar_in": ([D, T], F32), "s_ar_out": ([D, T], F32),
    "m8_dram": ([8, T], BF16),
    "g_dram_h0": ([D, T // 2], F32), "g_dram_h1": ([D, T // 2], F32),
    "g_ar_h0": ([D, T // 2], F32), "g_ar_h1": ([D, T // 2], F32),
    "x1_dram": ([D, T], F32),
    "h_re_dram": ([512, T], F32), "h_im_dram": ([512, T], F32),
    "st_sum_in": ([2, T], F32), "st_sum_out": ([2, T], F32),
    "st_sq_in": ([2, T], F32), "st_sq_out": ([2, T], F32),
    "s1_dram": ([512, T], F32), "s1_ag_out": ([1024, T], F32),
    "xln_dram": ([128, NDC * T], BF16),
    "eout_ag_in": ([512, D], F32), "eout_ag_out": ([1024, D], F32),
}

GROUPS = [[0, 1], [2, 3], [4, 5], [6, 7]]

CONSTS = ["ones_col", "ones_row", "ones_col_bf", "ones_row_bf", "ones64_bf",
          "ident_bf", "ident_f32", "qb_c", "kb_c", "wob_c", "gate_b",
          "dt_s", "scale_s"]


def build_program(phases=BUILD_PHASES, taps=DEBUG_TAPS):
    nc = bass.Bass("TRN2", target_bir_lowering=False, debug=False)
    k = K()
    k.nc = nc
    k.s = Sched()
    k.di = {n: nc.dram_tensor(n, shp, dt, kind="ExternalInput").ap()
            for n, (shp, dt) in EXT_IN.items()}
    k.dd = {n: nc.dram_tensor(n, shp, dt,
                              kind=("ExternalOutput" if n in taps else "Internal")).ap()
            for n, (shp, dt) in DSCR.items()}
    k.out = nc.dram_tensor("out_fm", [D, T], F32, kind="ExternalOutput").ap()
    k.sb = {}
    k.taps = taps
    stack = contextlib.ExitStack()
    k.stack = stack

    def alloc(name, shape, dtype=F32, scope=None):
        h = (scope or stack).enter_context(nc.sbuf_tensor(name, shape, dtype))
        k.sb[name] = h
        return h

    k.alloc = alloc
    with stack:
        ps = {}
        for i in range(7):
            ps[f"b{i}"] = stack.enter_context(nc.psum_tensor(f"pb{i}", [128, 512], F32))
        ps["t0"] = stack.enter_context(nc.psum_tensor("pt0", [128, 512], BF16))
        k.ps = ps
        def load_consts(names, scope=None):
            for n in names:
                shp, dt = EXT_IN[n]
                alloc("c_" + n, shp, dt, scope)
                k.s.dma("qs", (lambda nm: (lambda e: e.dma_start(
                    out=k.sb["c_" + nm][:, :], in_=k.di[nm])))(n),
                    w=("c_" + n,), tag="const")
        k.load_consts = load_consts
        load_consts(CONSTS)
        alloc("c_eps1", [1, 1], F32)
        k.s.op("dve", lambda e: e.memset(k.sb["c_eps1"][:, :], EPS_LN), w=("c_eps1",))
        alloc("c_eps128", [128, 1], F32)
        k.s.op("dve", lambda e: e.memset(k.sb["c_eps128"][:, :], EPS_LN), w=("c_eps128",))
        for nm in ["r_mu0", "r_mu1", "r_sq0", "r_sq1", "r_var",
                   "r_lnv", "r_rstd", "r_a"]:
            alloc(nm, [1, TT], F32)
        k.sb["r_musq"] = k.sb["r_lnv"]

        if "p0" in phases:
            emit_p0(k)
        if "p2" in phases:
            emit_p2a(k)
        if "p1" in phases:
            emit_p1(k)
        if "p2" in phases:
            emit_p2b(k)
        if "p3" in phases:
            emit_p3(k)
        if "p4" in phases:
            emit_p4(k)

        for scx in reversed(getattr(k, "scopes", [])):
            scx.close()
        fin = k.s.op("pool", lambda e: e.nop(), tag="final")
        k.s.plan()
        fin.waits = {sname: v for sname, v in k.s.cnt.items()
                     if v > 0 and sname != "pool"}
        with nc.Block() as block:
            k.s.emit(nc, block, stack)
    return nc


def ln_stats(k, tag, src_getter, sq_getter, n_chunks=NDC, par=""):
    s, sb, ps = k.s, k.sb, k.ps
    rmu, rsq = "r_mu" + par, "r_sq" + par

    def mm_sum(e):
        for dc in range(n_chunks):
            i = e.matmul(ps["b5"][0:1, 0:TT], sb["c_ones_col_bf"][:, :], src_getter(dc),
                         start=(dc == 0), stop=(dc == n_chunks - 1))
        return i

    def mm_sq(e):
        for dc in range(n_chunks):
            i = e.matmul(ps["b6"][0:1, 0:TT], sb["c_ones_col_bf"][:, :], sq_getter(dc),
                         start=(dc == 0), stop=(dc == n_chunks - 1))
        return i

    s.op("pe", mm_sum, r=(tag + "_src", "c_ones_col_bf"), w=("pb5",))
    s.op("pe", mm_sq, r=(tag + "_sq", "c_ones_col_bf"), w=("pb6",))
    s.op("dve", (lambda rmu=rmu: (lambda e: e.tensor_scalar_mul(
        sb[rmu][:, :], ps["b5"][0:1, 0:TT], 1.0 / D)))(), r=("pb5",), w=(rmu,))
    s.op("dve", (lambda rsq=rsq: (lambda e: e.tensor_scalar_mul(
        sb[rsq][:, :], ps["b6"][0:1, 0:TT], 1.0 / D)))(), r=("pb6",), w=(rsq,))


def ln_finish(k, eps_key="c_eps1", par=""):
    s, sb, ps = k.s, k.sb, k.ps
    rmu, rsq = "r_mu" + par, "r_sq" + par
    s.op("dve", (lambda rmu=rmu: (lambda e: e.tensor_mul(
        sb["r_musq"][:, :], sb[rmu][:, :], sb[rmu][:, :])))(), r=(rmu,), w=("r_lnv",))
    s.op("dve", (lambda rsq=rsq: (lambda e: e.tensor_tensor(
        sb["r_var"][:, :], sb[rsq][:, :], sb["r_musq"][:, :], ALU.subtract)))(),
        r=(rsq, "r_lnv"), w=("r_var",))
    s.op("act", lambda e: e.activation(sb["r_lnv"][:, :], sb["r_var"][:, :], AF.Ln, bias=sb[eps_key][:, :]),
         r=("r_var", eps_key), w=("r_lnv",))
    s.op("act", lambda e: e.activation(sb["r_rstd"][:, :], sb["r_lnv"][:, :], AF.Exp, scale=-0.5),
         r=("r_lnv",), w=("r_rstd",))
    s.op("dve", (lambda rmu=rmu: (lambda e: e.scalar_tensor_tensor(
        sb["r_a"][:, :], sb[rmu][:, :], -1.0, sb["r_rstd"][:, :], ALU.mult, ALU.mult)))(),
        r=(rmu, "r_rstd"), w=("r_a",))
    s.op("pe", lambda e: e.matmul(ps["b3"][:, 0:TT], sb["c_ones_row"][:, :], sb["r_rstd"][:, :],
                                  start=True, stop=True), r=("r_rstd", "c_ones_row"), w=("pb3",))
    s.op("pe", lambda e: e.matmul(ps["b4"][:, 0:TT], sb["c_ones_row"][:, :], sb["r_a"][:, :],
                                  start=True, stop=True), r=("r_a", "c_ones_row"), w=("pb4",))


def ln_rows(k, tag, src_getter, sq_getter, n_chunks=NDC, eps_key="c_eps1"):
    ln_stats(k, tag, src_getter, sq_getter, n_chunks, par="0")
    ln_finish(k, eps_key, par="0")


def emit_p0(k):
    nc, s, sb, ps, di, dd, alloc = k.nc, k.s, k.sb, k.ps, k.di, k.dd, k.alloc

    k.rows_alloc = lambda scope: None
    k.qkv_scope = qsc = contextlib.ExitStack()
    k.scopes = getattr(k, "scopes", []) + [qsc]
    alloc("tmp_f", [128, TT], F32, qsc)
    alloc("q_sb", [128, NHP * T], BF16, qsc)
    alloc("k_sb", [128, NHP * T], BF16, qsc)
    alloc("v_sb", [128, (T // 128) * 512], BF16, qsc)
    k.p0_scope = sc = contextlib.ExitStack()
    k.scopes.append(sc)
    alloc("wq_sb", [128, NDC * 512], BF16, sc)
    alloc("wk_sb", [128, NDC * 512], BF16, sc)
    alloc("wv_sb", [128, NDC * 512], BF16, sc)
    alloc("vb_row_sb", [1, 512], BF16, sc)
    alloc("xbf0", [128, NDC * TT], BF16, sc)
    for i in range(2):
        alloc(f"xt{i}", [128, NDC * TT], F32, sc)
        alloc(f"xn_t{i}", [128, NDC * TT], BF16, sc)
    alloc("sq_bf", [128, NDC * TT], BF16, sc)

    for wn, sn in [("wq_T", "wq_sb"), ("wk_T", "wk_sb"), ("wv_T", "wv_sb")]:
        s.dma("qp", (lambda wn=wn, sn=sn: (lambda e: e.dma_start(
            out=sb[sn][:, :].rearrange("p (dc h) -> p dc h", h=512),
            in_=feat_view(di[wn]))))(), w=(sn,), tag="w0")
    s.dma("qp", lambda e: e.dma_start(out=sb["vb_row_sb"][:, :], in_=di["vb_row"]),
          w=("vb_row_sb",))

    def p0_stage_a(tt):
        xt, xbf = f"xt{tt % 2}", "xbf0"
        s.dma("qs", (lambda tt=tt, xt=xt: (lambda e: e.dma_start(
            out=sb[xt][:, :].rearrange("p (dc t) -> p dc t", t=TT),
            in_=feat_view(di["x_fm"])[:, :, tt * TT:(tt + 1) * TT])))(),
            w=(xt,), tag=f"x{tt}")
        s.op("dve", (lambda xt=xt, xbf=xbf: (lambda e: e.tensor_copy(sb[xbf][:, :], sb[xt][:, :])))(),
             r=(xt,), w=(xbf, "p0_src"))
        s.op("act", (lambda xbf=xbf: (lambda e: [e.activation(
            sb["sq_bf"][:, dc * TT:(dc + 1) * TT], sb[xbf][:, dc * TT:(dc + 1) * TT], AF.Square)
            for dc in range(NDC)][-1]))(), r=(xbf,), w=("sq_bf", "p0_sq"))
        ln_stats(k, "p0",
                 (lambda xbf=xbf: (lambda dc: sb[xbf][:, dc * TT:(dc + 1) * TT]))(),
                 lambda dc: sb["sq_bf"][:, dc * TT:(dc + 1) * TT], par=str(tt % 2))

    def p0_stage_b(tt):
        xt, xn_t = f"xt{tt % 2}", f"xn_t{tt % 2}"
        ln_finish(k, par=str(tt % 2))
        for dc in range(NDC):
            s.op("dve", (lambda xt=xt, dc=dc: (lambda e: e.tensor_tensor(
                sb["tmp_f"][:, :], sb[xt][:, dc * TT:(dc + 1) * TT],
                ps["b3"][:, 0:TT], ALU.mult)))(), r=(xt, "pb3"), w=("tmp_f",))
            s.op("dve", (lambda xn_t=xn_t, dc=dc: (lambda e: e.tensor_tensor(
                sb[xn_t][:, dc * TT:(dc + 1) * TT], sb["tmp_f"][:, :],
                ps["b4"][:, 0:TT], ALU.add)))(), r=("tmp_f", "pb4"), w=(xn_t,))
        s.dma("qa", (lambda tt=tt, xn_t=xn_t: (lambda e: e.dma_start(
            out=dd["xn_dram"].rearrange("p (dc t) -> p dc t", t=T)[:, :, tt * TT:(tt + 1) * TT],
            in_=sb[xn_t][:, :].rearrange("p (dc t) -> p dc t", t=TT))))(),
            r=(xn_t,), w=("xn_dram",))
        for hp in range(NHP):
            def mm_q(e, xn_t=xn_t, hp=hp, w="wq_sb", bank="b0"):
                for dc in range(NDC):
                    i = e.matmul(ps[bank][:, 0:TT],
                                 sb[w][:, dc * 512 + hp * 128: dc * 512 + (hp + 1) * 128],
                                 sb[xn_t][:, dc * TT:(dc + 1) * TT],
                                 start=(dc == 0), stop=(dc == NDC - 1))
                return i
            s.op("pe", mm_q, r=(xn_t, "wq_sb"), w=("pb0",))
            s.op("act", (lambda tt=tt, hp=hp: (lambda e: e.activation(
                sb["q_sb"][:, hp * T + tt * TT: hp * T + (tt + 1) * TT], ps["b0"][:, 0:TT],
                AF.Identity, bias=sb["c_qb_c"][:, hp:hp + 1])))(),
                r=("pb0", "c_qb_c"), w=(f"q_{hp}_{tt}",))
            s.op("pe", (lambda f=mm_q, xn_t=xn_t, hp=hp: (lambda e: f(e, xn_t, hp, "wk_sb", "b1")))(),
                 r=(xn_t, "wk_sb"), w=("pb1",))
            s.op("act", (lambda tt=tt, hp=hp: (lambda e: e.activation(
                sb["k_sb"][:, hp * T + tt * TT: hp * T + (tt + 1) * TT], ps["b1"][:, 0:TT],
                AF.Identity, bias=sb["c_kb_c"][:, hp:hp + 1])))(),
                r=("pb1", "c_kb_c"), w=(f"k_{hp}_{tt}",))
        for tsub in range(TT // 128):
            g = tt * 4 + tsub

            def mm_v(e, xn_t=xn_t, tsub=tsub):
                for dc in range(NDC):
                    e.matmul(ps["b2"][:, 0:512],
                             sb[xn_t][:, dc * TT + tsub * 128: dc * TT + (tsub + 1) * 128],
                             sb["wv_sb"][:, dc * 512:(dc + 1) * 512],
                             start=(dc == 0), stop=False)
                return e.matmul(ps["b2"][:, 0:512], sb["c_ones_row_bf"][:, :],
                                sb["vb_row_sb"][:, :], start=False, stop=True)
            s.op("pe", mm_v, r=(xn_t, "wv_sb", "vb_row_sb", "c_ones_row_bf"), w=("pb2",))
            s.op("dve", (lambda g=g: (lambda e: e.tensor_copy(
                sb["v_sb"][:, g * 512:(g + 1) * 512], ps["b2"][:, 0:512])))(),
                r=("pb2",), w=(f"v_{g}",))

    for i in range(NTT + 1):
        if i < NTT:
            p0_stage_a(i)
        if i >= 1:
            p0_stage_b(i - 1)
    if "dbg_p0" in k.taps:
        _tap_sbuf(k, "q_sb", [128, NHP * T], BF16, deps=[f"q_{h}_{t}" for h in range(NHP) for t in range(NTT)])
        _tap_sbuf(k, "k_sb", [128, NHP * T], BF16, deps=[f"k_{h}_{t}" for h in range(NHP) for t in range(NTT)])
        _tap_sbuf(k, "v_sb", [128, (T // 128) * 512], BF16, deps=[f"v_{g}" for g in range(T // 128)])
    s.fence()
    k.p0_scope.close()
    k.scopes.remove(k.p0_scope)


def _tap_sbuf(k, name, shape, dtype, deps):
    dram = k.nc.dram_tensor("tap_" + name, shape, dtype, kind="ExternalOutput").ap()
    k.s.dma("qa", (lambda name=name, dram=dram: (lambda e: e.dma_start(
        out=dram, in_=k.sb[name][:, :])))(), r=tuple(deps), w=("tap_" + name,))


def emit_p1(k):
    nc, s, sb, ps, di, dd, alloc = k.nc, k.s, k.sb, k.ps, k.di, k.dd, k.alloc
    sc = contextlib.ExitStack()
    k.scopes.append(sc)
    k.load_consts(["mask01"], sc)
    alloc("wo_sb", [128, 4 * D], BF16, sc)
    s.dma("qp", lambda e: e.dma_start(
        out=sb["wo_sb"][:, :].rearrange("p (ic o) -> p ic o", o=D),
        in_=feat_view(di["wo_T"])), w=("wo_sb",))
    for i in range(6):
        alloc(f"expT{i}", [128, TT], BF16, sc)
    alloc("den_s", [128, TT], F32, sc)
    alloc("den_r", [128, TT], F32, sc)
    alloc("attn_sT", [128, NHP * T], BF16, sc)
    alloc("s_t0", [128, TT], F32, sc)
    alloc("s_t1", [128, TT], F32, sc)

    for hp in range(NHP):
        for qc in range(NTT):
            last_kt = 4 * qc + 3
            units = [(kt, r_h) for kt in range(last_kt + 1) for r_h in range(2)]
            NU = len(units)
            LAG = 4

            def emit_st(i, hp=hp, qc=qc, units=units):
                kt, r_h = units[i]
                sbank = ["b2", "b3", "b4", "b5", "b6"][i % 5]
                et = f"expT{i % 6}"
                s.op("pe", (lambda hp=hp, kt=kt, qc=qc, r_h=r_h, sbank=sbank: (lambda e: e.matmul(
                    ps[sbank][:, 0:TT],
                    sb["k_sb"][64 * r_h:64 * (r_h + 1), hp * T + kt * 128: hp * T + (kt + 1) * 128],
                    sb["q_sb"][64 * r_h:64 * (r_h + 1), hp * T + qc * TT: hp * T + (qc + 1) * TT],
                    start=True, stop=True)))(),
                    r=(f"k_{hp}_{kt // 4}", f"q_{hp}_{qc}"), w=(f"p{sbank}",), tag="st")
                s.op("act", (lambda sbank=sbank, et=et: (lambda e: e.activation(
                    sb[et][:, :], ps[sbank][:, 0:TT], AF.Exp, scale=1.0 / 32.0)))(),
                    r=(f"p{sbank}",), w=(et,), tag="exp")
                if kt >= 4 * qc:
                    v = kt - 4 * qc
                    s.op("dve", (lambda et=et, v=v: (lambda e: e.tensor_tensor(
                        sb[et][:, :], sb[et][:, :],
                        sb["c_mask01"][:, v * TT:(v + 1) * TT], ALU.mult)))(),
                        r=(et, "c_mask01"), w=(et,), tag="mask")

            def emit_denav(i, hp=hp, qc=qc, units=units, last_kt=last_kt):
                kt, r_h = units[i]
                et = f"expT{i % 6}"
                s.op("pe", (lambda et=et, r_h=r_h, kt=kt, last_kt=last_kt: (lambda e: e.matmul(
                    ps["b1"][64 * r_h:64 * (r_h + 1), 0:TT],
                    sb["c_ones64_bf"][:, :], sb[et][:, :],
                    start=(kt == 0), stop=(kt == last_kt),
                    skip_group_check=bool(r_h),
                    tile_position=(0, 64) if r_h else None)))(),
                    r=(et, "c_ones64_bf"), w=("pb1",), tag="den")
                s.op("pe", (lambda et=et, r_h=r_h, kt=kt, last_kt=last_kt, hp=hp: (lambda e: e.matmul(
                    ps["b0"][64 * r_h:64 * (r_h + 1), 0:TT],
                    sb["v_sb"][:, kt * 512 + hp * 128 + r_h * 64: kt * 512 + hp * 128 + (r_h + 1) * 64],
                    sb[et][:, :],
                    start=(kt == 0), stop=(kt == last_kt),
                    skip_group_check=bool(r_h),
                    tile_position=(0, 64) if r_h else None)))(),
                    r=(et, f"v_{kt}"), w=("pb0",), tag="av")

            for i in range(NU):
                emit_st(i)
                if i >= LAG:
                    emit_denav(i - LAG)
            for i in range(NU - LAG, NU):
                emit_denav(i)
            s.op("dve", lambda e: e.tensor_copy(sb["den_s"][:, :], ps["b1"][:, 0:TT]),
                 r=("pb1",), w=("den_s",))
            s.op("act", lambda e: e.activation(sb["den_r"][:, :], sb["den_s"][:, :], AF.Ln),
                 r=("den_s",), w=("den_r",))
            s.op("act", lambda e: e.activation(sb["den_r"][:, :], sb["den_r"][:, :], AF.Exp,
                                               scale=-1.0),
                 r=("den_r",), w=("den_r",))
            s.op("dve", (lambda hp=hp, qc=qc: (lambda e: e.tensor_tensor(
                sb["attn_sT"][:, hp * T + qc * TT: hp * T + (qc + 1) * TT],
                ps["b0"][:, 0:TT], sb["den_r"][:, :], ALU.mult)))(),
                r=("pb0", "den_r"), w=(f"attn_{hp}_{qc}",), tag="norm")

    for tt in range(NTT):
        for oc in range(NDC):
            stile = "s_t0" if (tt * NDC + oc) % 2 == 0 else "s_t1"

            def mm_wo(e, tt=tt, oc=oc):
                for ic in range(4):
                    i = e.matmul(ps["b6"][:, 0:TT],
                                 sb["wo_sb"][:, ic * D + oc * 128: ic * D + (oc + 1) * 128],
                                 sb["attn_sT"][:, ic * T + tt * TT: ic * T + (tt + 1) * TT],
                                 start=(ic == 0), stop=(ic == 3))
                return i
            s.op("pe", mm_wo, r=tuple(f"attn_{ic}_{tt}" for ic in range(4)) + ("wo_sb",),
                 w=("pb6",), tag="wo")
            s.op("dve", (lambda stile=stile: (lambda e: e.tensor_copy(sb[stile][:, :], ps["b6"][:, 0:TT])))(),
                 r=("pb6",), w=(stile,))
            s.dma("qa", (lambda tt=tt, oc=oc, stile=stile: (lambda e: e.dma_start(
                out=dd["s_ar_in"][oc * 128:(oc + 1) * 128, tt * TT:(tt + 1) * TT],
                in_=sb[stile][:, :])))(), r=(stile,), w=("s_ar_in",))
    s.collective(lambda e: e.collective_compute(
        "AllReduce", ALU.add, replica_groups=GROUPS,
        ins=[dd["s_ar_in"][:, :]], outs=[dd["s_ar_out"][:, :]]),
        r=("s_ar_in",), w=("s_ar_out",), tag="cc_s")
    s.fence()
    sc.close()
    k.scopes.remove(sc)


def emit_p2a(k):
    nc, s, sb, ps, di, dd, alloc = k.nc, k.s, k.sb, k.ps, k.di, k.dd, k.alloc

    sca = contextlib.ExitStack()
    k.scopes.append(sca)
    k.p2sca = sca
    alloc("gw_sb", [128, NDC * 8], BF16, sca)
    alloc("xn_a", [128, NDC * TT], BF16, sca)
    alloc("glog", [8, TT], F32, sca)
    alloc("m_tok", [128, 32 * 8], F32, sca)
    alloc("m_tokb", [128, 32 * 8], BF16, sca)
    alloc("m8b_sb", [8, T], BF16, sca)
    alloc("p8f", [128, 32 * 8], F32, sca)
    alloc("mk_buf", [128, 8], F32, sca)
    alloc("ge_buf", [128, 8], F32, sca)
    for i in range(1, 5):
        alloc(f"tcol{i}", [128, 1], F32, sca)
    s.dma("qp", lambda e: e.dma_start(
        out=sb["gw_sb"][:, :].rearrange("p (dc e) -> p dc e", e=8),
        in_=feat_view(di["gate_w"])), w=("gw_sb",))
    for tt in range(NTT):
        s.dma("qs", (lambda tt=tt: (lambda e: e.dma_start(
            out=sb["xn_a"][:, :].rearrange("p (dc t) -> p dc t", t=TT),
            in_=feat_view(dd["xn_dram"].rearrange("p (dc t) -> (dc p) t", t=T))[:, :, tt * TT:(tt + 1) * TT]
            if False else
            dd["xn_dram"].rearrange("p (dc t) -> p dc t", t=T)[:, :, tt * TT:(tt + 1) * TT])))(),
            r=("xn_dram",), w=("xn_a",), tag="xn_a")
        def mm_gl(e, tt=tt):
            for dc in range(NDC):
                i = e.matmul(ps["b0"][0:8, 0:TT], sb["gw_sb"][:, dc * 8:(dc + 1) * 8],
                             sb["xn_a"][:, dc * TT:(dc + 1) * TT],
                             start=(dc == 0), stop=(dc == NDC - 1))
            return i
        s.op("pe", mm_gl, r=("xn_a", "gw_sb"), w=("pb0",))
        s.op("act", lambda e: e.activation(sb["glog"][:, :], ps["b0"][0:8, 0:TT],
                                           AF.Identity, bias=sb["c_gate_b"][:, :]),
             r=("pb0", "c_gate_b"), w=("glog",))
        for cq in range(4):
            c = tt * 4 + cq
            s.op("pe", (lambda cq=cq: (lambda e: e.transpose(
                ps["b1"][:, 0:8], sb["glog"][0:8, cq * 128:(cq + 1) * 128],
                sb["c_ident_f32"][0:8, 0:8])))(),
                r=("glog", "c_ident_f32"), w=("pb1",), tag="tr_gl")
            s.op("act", (lambda c=c: (lambda e: e.activation(
                sb["m_tok"][:, c * 8:(c + 1) * 8], ps["b1"][:, 0:8], AF.Identity)))(),
                r=("pb1",), w=(f"mt_{c}",))
    s.op("act", (lambda e: e.activation(
        sb["p8f"][:, :], sb["m_tok"][:, :], AF.Exp)),
        r=tuple(f"mt_{c}" for c in range(32)), w=("p8f",))
    for c in range(32):
        s.op("dve", (lambda c=c: (lambda e: e.reduce_max(
            sb["tcol1"][:, :], sb["m_tok"][:, c * 8:(c + 1) * 8], axis=AX)))(),
            r=(f"mt_{c}",), w=("tcol1",))
        s.op("dve", (lambda c=c: (lambda e: e.tensor_single_scalar(
            sb["ge_buf"][:, :], sb["m_tok"][:, c * 8:(c + 1) * 8], sb["tcol1"][:, :], ALU.is_ge)))(),
            r=(f"mt_{c}", "tcol1"), w=("ge_buf",))
        s.op("dve", (lambda c=c: (lambda e: e.scalar_tensor_tensor(
            sb["mk_buf"][:, :], sb["ge_buf"][:, :], -1e30,
            sb["m_tok"][:, c * 8:(c + 1) * 8], ALU.mult, ALU.add)))(),
            r=("ge_buf", f"mt_{c}"), w=("mk_buf",))
        s.op("dve", lambda e: e.reduce_max(sb["tcol3"][:, :], sb["mk_buf"][:, :], axis=AX),
             r=("mk_buf",), w=("tcol3",))
        s.op("dve", (lambda c=c: (lambda e: e.tensor_single_scalar(
            sb["ge_buf"][:, :], sb["m_tok"][:, c * 8:(c + 1) * 8], sb["tcol3"][:, :], ALU.is_ge)))(),
            r=(f"mt_{c}", "tcol3"), w=("ge_buf",))
        s.op("dve", (lambda c=c: (lambda e: e.reduce_sum(
            sb["tcol4"][:, :], sb["p8f"][:, c * 8:(c + 1) * 8], axis=AX)))(),
            r=("p8f",), w=("tcol4",))
        s.op("dve", lambda e: e.reciprocal(sb["tcol4"][:, :], sb["tcol4"][:, :]),
             r=("tcol4",), w=("tcol4",))
        s.op("dve", (lambda c=c: (lambda e: e.scalar_tensor_tensor(
            sb["m_tok"][:, c * 8:(c + 1) * 8], sb["p8f"][:, c * 8:(c + 1) * 8], sb["tcol4"][:, :],
            sb["ge_buf"][:, :], ALU.mult, ALU.mult)))(),
            r=("p8f", "tcol4", "ge_buf"), w=(f"mw_{c}",))
    s.op("dve", lambda e: e.tensor_copy(sb["m_tokb"][:, :], sb["m_tok"][:, :]),
         r=tuple(f"mw_{c}" for c in range(32)), w=("m_tokb",))
    for c in range(32):
        s.op("pe", (lambda c=c: (lambda e: e.transpose(
            ps["t0"][0:8, 0:128], sb["m_tokb"][:, c * 8:(c + 1) * 8],
            sb["c_ident_bf"][:, :])))(),
            r=("m_tokb", "c_ident_bf"), w=("pt0",), tag="tr_m")
        s.op("act", (lambda c=c: (lambda e: e.activation(
            sb["m8b_sb"][0:8, c * 128:(c + 1) * 128], ps["t0"][0:8, 0:128], AF.Identity)))(),
            r=("pt0",), w=("m8b",))
    s.dma("qa", lambda e: e.dma_start(out=dd["m8_dram"], in_=sb["m8b_sb"][:, :]),
          r=("m8b",), w=("m8_dram",))


def emit_p2b(k):
    nc, s, sb, ps, di, dd, alloc = k.nc, k.s, k.sb, k.ps, k.di, k.dd, k.alloc
    s.fence()
    k.p2sca.close()
    k.scopes.remove(k.p2sca)
    k.qkv_scope.close()
    k.scopes.remove(k.qkv_scope)
    scm = contextlib.ExitStack()
    k.scopes.append(scm)
    k.p2scm = scm
    alloc("m8_sb", [8, T], BF16, scm)
    alloc("m5_bf", [5, T], BF16, scm)
    k.load_consts(["sel4"], scm)
    s.dma("qs", lambda e: e.dma_start(out=sb["m8_sb"][:, :], in_=dd["m8_dram"]),
          r=("m8_dram",), w=("m8",))
    s.op("dve", lambda e: e.memset(sb["m5_bf"][0:1, :], 0.5), w=("m5_row0",))
    s.dma("qp", lambda e: e.dma_start(out=sb["m5_bf"][1:5, :], in_=sb["m8_sb"][0:4, :]),
          r=("m8",), w=("m5_bf",))
    scb = contextlib.ExitStack()
    k.scopes.append(scb)
    alloc("w1_sb", [128, NDC * H2], BF16, scb)
    alloc("w2_sb", [128, NDC * H2], BF16, scb)
    alloc("w3_sb", [128, 16 * D], BF16, scb)
    alloc("b3s_bf", [5, D], BF16, scb)
    alloc("rb1_sb", [128, 64], F32, scb)
    alloc("rb2_sb", [128, 64], F32, scb)
    alloc("sb1_sb", [128, 16], F32, scb)
    alloc("sb2_sb", [128, 16], F32, scb)
    alloc("xn_b0", [128, NDC * TT], BF16, scb)
    for i in range(2):
        alloc(f"glu{i}", [128, 16 * TT], BF16, scb)
    alloc("g_tile", [128, NDC * TT], F32, scb)
    alloc("silu_b", [128, TT], F32, scb)
    alloc("oew_b", [128, TT], F32, scb)
    alloc("bcast_b", [128, TT], F32, scb)
    s.dma("qp", lambda e: e.dma_start(out=sb["b3s_bf"][:, :], in_=di["b3_stack"]),
          w=("b3s_bf",))
    s.dma("qs", lambda e: e.dma_start(
        out=sb["rb1_sb"][:, :].rearrange("p (e h) -> p e h", h=16),
        in_=di["rb1"].rearrange("e p h -> p e h")), w=("rb1_sb",))
    s.dma("qs", lambda e: e.dma_start(
        out=sb["rb2_sb"][:, :].rearrange("p (e h) -> p e h", h=16),
        in_=di["rb2"].rearrange("e p h -> p e h")), w=("rb2_sb",))
    s.dma("qs", lambda e: e.dma_start(out=sb["sb1_sb"][:, :], in_=di["sb1"]), w=("sb1_sb",))
    s.dma("qs", lambda e: e.dma_start(out=sb["sb2_sb"][:, :], in_=di["sb2"]), w=("sb2_sb",))
    s.op("dve", lambda e: e.tensor_scalar_mul(sb["sb2_sb"][:, :], sb["sb2_sb"][:, :], 0.5),
         r=("sb2_sb",), w=("sb2_sb",))

    for ei in range(5):
        shared = (ei == 4)
        for qf in range(4):
            for wn, sn in ([("sw1", "w1_sb"), ("sw2", "w2_sb")] if shared
                           else [("rw1", "w1_sb"), ("rw2", "w2_sb")]):
                src = di[wn] if shared else di[wn][ei]
                s.dma("qp", (lambda src=src, sn=sn, qf=qf: (lambda e: e.dma_start(
                    out=sb[sn][:, :].rearrange("p (dc h) -> p dc h", h=H2)
                        [:, :, qf * (H2 // 4):(qf + 1) * (H2 // 4)],
                    in_=feat_view(src)[:, :, qf * (H2 // 4):(qf + 1) * (H2 // 4)])))(),
                    w=(f"{sn}q{qf}",), tag="wld")
        w3src = di["sw3"] if shared else di["rw3"][ei]
        for hf in range(2):
            s.dma("qp", (lambda src=w3src, hf=hf: (lambda e: e.dma_start(
                out=sb["w3_sb"][:, :].rearrange("p (hc o) -> p hc o", o=D)
                    [:, hf * 8:(hf + 1) * 8, :],
                in_=feat_view(src)[:, hf * 8:(hf + 1) * 8, :])))(),
                w=(f"w3_sbh{hf}",), tag="wld")
        if shared:
            s.op("dve", lambda e: e.tensor_scalar_mul(sb["w2_sb"][:, :], sb["w2_sb"][:, :], 0.5),
                 r=tuple(f"w2_sbq{q}" for q in range(4)),
                 w=tuple(f"w2_sbq{q}" for q in range(4)))
        for tt in range(NTT):
            xnb = "xn_b0"
            glu = f"glu{tt % 2}"
            s.dma("qs", (lambda tt=tt, xnb=xnb: (lambda e: e.dma_start(
                out=sb[xnb][:, :].rearrange("p (dc t) -> p dc t", t=TT),
                in_=dd["xn_dram"].rearrange("p (dc t) -> p dc t", t=T)[:, :, tt * TT:(tt + 1) * TT])))(),
                r=("xn_dram",), w=(xnb,), tag="xn_b")
            for hc in range(16):
                p1b = "b2" if hc % 2 == 0 else "b3"
                p2b = "b4" if hc % 2 == 0 else "b5"
                def mm_p(e, xnb=xnb, hc=hc, w="w1_sb", bank="b2"):
                    for dc in range(NDC):
                        i = e.matmul(ps[bank][:, 0:TT],
                                     sb[w][:, dc * H2 + hc * 128: dc * H2 + (hc + 1) * 128],
                                     sb[xnb][:, dc * TT:(dc + 1) * TT],
                                     start=(dc == 0), stop=(dc == NDC - 1))
                    return i
                whk = f"q{hc // 4}"
                s.op("pe", (lambda f=mm_p, xnb=xnb, hc=hc, p1b=p1b: (lambda e: f(e, xnb, hc, "w1_sb", p1b)))(),
                     r=(xnb, "w1_sb" + whk), w=(f"p{p1b}",))
                s.op("pe", (lambda f=mm_p, xnb=xnb, hc=hc, p2b=p2b: (lambda e: f(e, xnb, hc, "w2_sb", p2b)))(),
                     r=(xnb, "w2_sb" + whk), w=(f"p{p2b}",))
                bcol = (lambda hc=hc, ei=ei: sb["sb1_sb"][:, hc:hc + 1] if ei == 4
                        else sb["rb1_sb"][:, ei * 16 + hc: ei * 16 + hc + 1])
                s.op("act", (lambda hc=hc, ei=ei, p1b=p1b: (lambda e: e.activation(
                    sb["silu_b"][:, :], ps[p1b][:, 0:TT], AF.Silu,
                    bias=(sb["sb1_sb"][:, hc:hc + 1] if ei == 4
                          else sb["rb1_sb"][:, ei * 16 + hc: ei * 16 + hc + 1]))))(),
                    r=(f"p{p1b}", "rb1_sb", "sb1_sb"), w=("silu_b",))
                s.op("dve", (lambda hc=hc, ei=ei, glu=glu, p2b=p2b: (lambda e: e.scalar_tensor_tensor(
                    sb[glu][:, hc * TT:(hc + 1) * TT], ps[p2b][:, 0:TT],
                    (sb["sb2_sb"][:, hc:hc + 1] if ei == 4
                     else sb["rb2_sb"][:, ei * 16 + hc: ei * 16 + hc + 1]),
                    sb["silu_b"][:, :], ALU.add, ALU.mult)))(),
                    r=(f"p{p2b}", "silu_b", "rb2_sb", "sb2_sb"), w=(glu,))
            if not shared:
                s.op("pe", (lambda ei=ei, tt=tt: (lambda e: e.matmul(
                    ps["b6"][:, 0:TT], sb["c_sel4"][:, ei * 128:(ei + 1) * 128],
                    sb["m8_sb"][0:4, tt * TT:(tt + 1) * TT], start=True, stop=True)))(),
                    r=("m8", "c_sel4"), w=("pb6",), tag="bcast")
                s.op("dve", lambda e: e.tensor_copy(sb["bcast_b"][:, :], ps["b6"][:, 0:TT]),
                     r=("pb6",), w=("bcast_b",))
            if ei > 0:
                gd = "g_dram_h0" if tt < 4 else "g_dram_h1"
                toff = (tt % 4) * TT
                s.dma("qa", (lambda gd=gd, toff=toff: (lambda e: e.dma_start(
                    out=sb["g_tile"][:, :].rearrange("p (dc t) -> p dc t", t=TT),
                    in_=feat_view(dd[gd])[:, :, toff:toff + TT])))(),
                    r=(gd,), w=("g_tile",), tag="g_ld")
            for dc in range(NDC):
                ob = "b0" if dc % 2 == 0 else "b1"
                def mm_o(e, glu=glu, dc=dc, ob=ob):
                    for hc in range(16):
                        i = e.matmul(ps[ob][:, 0:TT],
                                     sb["w3_sb"][:, hc * D + dc * 128: hc * D + (dc + 1) * 128],
                                     sb[glu][:, hc * TT:(hc + 1) * TT],
                                     start=(hc == 0), stop=(hc == 15))
                    return i
                s.op("pe", mm_o, r=(glu, "w3_sbh0", "w3_sbh1"), w=(f"p{ob}",))
                if ei == 0:
                    s.op("pe", (lambda dc=dc, tt=tt: (lambda e: e.matmul(
                        ps["b5"][:, 0:TT],
                        sb["b3s_bf"][:, dc * 128:(dc + 1) * 128],
                        sb["m5_bf"][:, tt * TT:(tt + 1) * TT], start=True, stop=True)))(),
                        r=("b3s_bf", "m5_bf"), w=("pb5",), tag="b3mm")
                    s.op("dve", (lambda dc=dc, ob=ob: (lambda e: e.tensor_tensor(
                        sb["oew_b"][:, :], ps[ob][:, 0:TT], sb["bcast_b"][:, :], ALU.mult)))(),
                        r=(f"p{ob}", "bcast_b"), w=("oew_b",))
                    s.op("dve", (lambda dc=dc: (lambda e: e.tensor_tensor(
                        sb["g_tile"][:, dc * TT:(dc + 1) * TT], sb["oew_b"][:, :],
                        ps["b5"][:, 0:TT], ALU.add)))(),
                        r=("oew_b", "pb5"), w=("g_tile",))
                elif not shared:
                    s.op("dve", (lambda dc=dc, ob=ob: (lambda e: e.tensor_tensor(
                        sb["oew_b"][:, :], ps[ob][:, 0:TT], sb["bcast_b"][:, :], ALU.mult)))(),
                        r=(f"p{ob}", "bcast_b"), w=("oew_b",))
                    s.op("pool", (lambda dc=dc: (lambda e: e.tensor_tensor(
                        sb["g_tile"][:, dc * TT:(dc + 1) * TT],
                        sb["g_tile"][:, dc * TT:(dc + 1) * TT], sb["oew_b"][:, :], ALU.add)))(),
                        r=("g_tile", "oew_b"), w=("g_tile",))
                else:
                    s.op("dve", (lambda dc=dc, ob=ob: (lambda e: e.tensor_tensor(
                        sb["g_tile"][:, dc * TT:(dc + 1) * TT],
                        sb["g_tile"][:, dc * TT:(dc + 1) * TT], ps[ob][:, 0:TT], ALU.add)))(),
                        r=("g_tile", f"p{ob}"), w=("g_tile",))
            gd = "g_dram_h0" if tt < 4 else "g_dram_h1"
            toff = (tt % 4) * TT
            s.dma("qa", (lambda gd=gd, toff=toff: (lambda e: e.dma_start(
                out=feat_view(dd[gd])[:, :, toff:toff + TT],
                in_=sb["g_tile"][:, :].rearrange("p (dc t) -> p dc t", t=TT))))(),
                r=("g_tile",), w=(gd,), tag="g_st")
    s.collective(lambda e: e.collective_compute(
        "AllReduce", ALU.add, replica_groups=GROUPS,
        ins=[dd["g_dram_h0"][:, :]], outs=[dd["g_ar_h0"][:, :]]),
        r=("g_dram_h0",), w=("g_ar_h0",), tag="cc_g0")
    s.collective(lambda e: e.collective_compute(
        "AllReduce", ALU.add, replica_groups=GROUPS,
        ins=[dd["g_dram_h1"][:, :]], outs=[dd["g_ar_h1"][:, :]]),
        r=("g_dram_h1",), w=("g_ar_h1",), tag="cc_g1")
    s.fence()
    scb.close()
    k.scopes.remove(scb)
    scm.close()
    k.scopes.remove(scm)

    scc = contextlib.ExitStack()
    k.scopes.append(scc)
    k.rows_alloc(scc)
    alloc("xn1_t", [128, NDC * TT], BF16, scc)
    alloc("gc_t", [128, NDC * TT], F32, scc)
    alloc("sc_t", [128, NDC * TT], F32, scc)
    alloc("xc_t", [128, NDC * TT], F32, scc)
    for i in range(2):
        alloc(f"x1_t{i}", [128, NDC * TT], F32, scc)
    alloc("x1bf", [128, NDC * TT], BF16, scc)
    alloc("sq2_bf", [128, NDC * TT], BF16, scc)
    alloc("g1_b", [128, TT], F32, scc)
    alloc("g1_t4", [128, NDC * TT], F32, scc)

    def p2c_a(tt):
        x1t = f"x1_t{tt % 2}"
        gar = "g_ar_h0" if tt < 4 else "g_ar_h1"
        for hf in range(2):
            s.dma("qs", (lambda tt=tt, gar=gar, hf=hf: (lambda e: e.dma_start(
                out=sb["gc_t"][:, :].rearrange("p (dc t) -> p dc t", t=TT)[:, hf * 4:(hf + 1) * 4, :],
                in_=feat_view(dd[gar])[:, hf * 4:(hf + 1) * 4, (tt % 4) * TT:(tt % 4 + 1) * TT])))(),
                r=(gar,), w=(f"gc_th{hf}",))
            s.dma("qs", (lambda tt=tt, hf=hf: (lambda e: e.dma_start(
                out=sb["sc_t"][:, :].rearrange("p (dc t) -> p dc t", t=TT)[:, hf * 4:(hf + 1) * 4, :],
                in_=feat_view(dd["s_ar_out"])[:, hf * 4:(hf + 1) * 4, tt * TT:(tt + 1) * TT])))(),
                r=("s_ar_out",), w=(f"sc_th{hf}",))
            s.dma("qs", (lambda tt=tt, hf=hf: (lambda e: e.dma_start(
                out=sb["xc_t"][:, :].rearrange("p (dc t) -> p dc t", t=TT)[:, hf * 4:(hf + 1) * 4, :],
                in_=feat_view(di["x_fm"])[:, hf * 4:(hf + 1) * 4, tt * TT:(tt + 1) * TT])))(),
                w=(f"xc_th{hf}",))
        s.op("act", lambda e: e.activation(sb["g1_t4"][:, :], sb["gc_t"][:, :], AF.Silu),
             r=("gc_th0", "gc_th1"), w=("g1_t4",))
        for dc in range(NDC):
            s.op("dve", (lambda dc=dc: (lambda e: e.scalar_tensor_tensor(
                sb["g1_t4"][:, dc * TT:(dc + 1) * TT], sb["sc_t"][:, dc * TT:(dc + 1) * TT],
                sb["c_wob_c"][:, dc:dc + 1], sb["g1_t4"][:, dc * TT:(dc + 1) * TT],
                ALU.add, ALU.mult)))(),
                r=(f"sc_th{dc // 4}", "c_wob_c", "g1_t4"), w=("g1_t4",))
        s.op("pool", (lambda x1t=x1t: (lambda e: e.tensor_tensor(
            sb[x1t][:, :], sb["g1_t4"][:, :], sb["xc_t"][:, :], ALU.add)))(),
            r=("g1_t4", "xc_th0", "xc_th1"), w=(x1t,))
        s.dma("qa", (lambda tt=tt, x1t=x1t: (lambda e: e.dma_start(
            out=feat_view(dd["x1_dram"])[:, :, tt * TT:(tt + 1) * TT],
            in_=sb[x1t][:, :].rearrange("p (dc t) -> p dc t", t=TT))))(),
            r=(x1t,), w=("x1_dram",))
        s.op("dve", (lambda x1t=x1t: (lambda e: e.tensor_copy(sb["x1bf"][:, :], sb[x1t][:, :])))(),
             r=(x1t,), w=("x1bf", "p2c_src"))
        s.op("act", lambda e: e.activation(sb["sq2_bf"][:, :], sb["x1bf"][:, :], AF.Square),
             r=("x1bf",), w=("sq2_bf", "p2c_sq"))
        ln_stats(k, "p2c",
                 lambda dc: sb["x1bf"][:, dc * TT:(dc + 1) * TT],
                 lambda dc: sb["sq2_bf"][:, dc * TT:(dc + 1) * TT], par=str(tt % 2))

    def p2c_b(tt):
        x1t = f"x1_t{tt % 2}"
        ln_finish(k, par=str(tt % 2))
        for dc in range(NDC):
            s.op("dve", (lambda dc=dc, x1t=x1t: (lambda e: e.tensor_tensor(
                sb["g1_b"][:, :], sb[x1t][:, dc * TT:(dc + 1) * TT],
                ps["b3"][:, 0:TT], ALU.mult)))(), r=(x1t, "pb3"), w=("g1_b",))
            s.op("dve", (lambda dc=dc: (lambda e: e.tensor_tensor(
                sb["xn1_t"][:, dc * TT:(dc + 1) * TT],
                sb["g1_b"][:, :], ps["b4"][:, 0:TT], ALU.add)))(),
                r=("g1_b", "pb4"), w=("xn1_t",))
        s.dma("qa", (lambda tt=tt: (lambda e: e.dma_start(
            out=dd["xn1_dram"].rearrange("p (dc t) -> p dc t", t=T)[:, :, tt * TT:(tt + 1) * TT],
            in_=sb["xn1_t"][:, :].rearrange("p (dc t) -> p dc t", t=TT))))(),
            r=("xn1_t",), w=("xn1_dram",))

    for i in range(NTT + 1):
        if i < NTT:
            p2c_a(i)
        if i >= 1:
            p2c_b(i - 1)
    s.fence()
    scc.close()
    k.scopes.remove(scc)


def emit_p3(k):
    nc, s, sb, ps, di, dd, alloc = k.nc, k.s, k.sb, k.ps, k.di, k.dd, k.alloc
    sc3 = contextlib.ExitStack()
    k.scopes.append(sc3)
    k.load_consts(["ramp", "omega_c", "lgam_c", "bpsi_re_c", "bpsi_im_c",
                   "bgate_c"], sc3)
    for nm in ["A3", "B3", "F3", "D3", "COS3", "SIN3", "U3", "C3"]:
        alloc(nm, [128, T], F32, sc3)
    alloc("I3", [128, T // 2], I32, sc3)
    alloc("strows", [128, 128], F32, sc3)
    for i in range(2):
        alloc(f"st32_{i}", [32, 128], F32, sc3)
    alloc("xn1_a", [128, NDC * TT], BF16, sc3)
    alloc("xn1_b", [128, NDC * TT], BF16, sc3)
    alloc("g3_t", [128, TT], F32, sc3)
    alloc("w_psi_re", [128, NDC * 128], BF16, sc3)
    alloc("w_psi_im", [128, NDC * 128], BF16, sc3)
    alloc("w_gate3", [128, NDC * 128], BF16, sc3)
    for nm in ["rho_c", "thp_c", "t3a", "t3b", "dtabs", "negdt", "thpr_s"]:
        alloc(nm, [128, 1] if nm != "thpr_s" else [1, 128], F32, sc3)
    alloc("c_q25", [128, 1], F32, sc3)
    s.op("dve", lambda e: e.memset(sb["c_q25"][:, :], 0.25), w=("c_q25",))

    s.op("act", lambda e: e.activation(sb["t3a"][0:1, 0:1], sb["c_dt_s"][:, :], AF.Abs),
         r=("c_dt_s",), w=("t3a",))
    s.op("pe", lambda e: e.matmul(ps["b0"][:, 0:1], sb["c_ones_row"][:, :],
                                  sb["t3a"][0:1, 0:1], start=True, stop=True),
         r=("t3a", "c_ones_row"), w=("pb0",))
    s.op("dve", lambda e: e.tensor_copy(sb["dtabs"][:, :], ps["b0"][:, 0:1]),
         r=("pb0",), w=("dtabs",))
    s.op("dve", lambda e: e.tensor_scalar_mul(sb["negdt"][:, :], sb["dtabs"][:, :], -1.0),
         r=("dtabs",), w=("negdt",))

    for cc in range(4):
        for wn, sn in [("wpsi_re_T", "w_psi_re"), ("wpsi_im_T", "w_psi_im"),
                       ("wgate_T", "w_gate3")]:
            s.dma("qp", (lambda wn=wn, sn=sn, cc=cc: (lambda e: e.dma_start(
                out=sb[sn][:, :].rearrange("p (dc m) -> p dc m", m=128),
                in_=feat_view(di[wn])[:, :, cc * 128:(cc + 1) * 128])))(),
                w=(sn,), tag="w3")
        s.op("act", (lambda cc=cc: (lambda e: e.activation(
            sb["t3a"][:, :], sb["c_lgam_c"][:, cc:cc + 1], AF.Exp)))(),
            r=("c_lgam_c",), w=("t3a",))
        s.op("dve", lambda e: e.tensor_mul(sb["t3b"][:, :], sb["t3a"][:, :], sb["negdt"][:, :]),
             r=("t3a", "negdt"), w=("t3b",))
        s.op("act", lambda e: e.activation(sb["rho_c"][:, :], sb["t3b"][:, :], AF.Exp),
             r=("t3b",), w=("rho_c",))
        s.op("dve", (lambda cc=cc: (lambda e: e.tensor_mul(
            sb["thp_c"][:, :], sb["c_omega_c"][:, cc:cc + 1], sb["dtabs"][:, :])))(),
            r=("c_omega_c", "dtabs"), w=("thp_c",))
        s.op("dve", lambda e: e.tensor_scalar_mul(sb["thp_c"][:, :], sb["thp_c"][:, :],
                                                  float(1.0 / (2 * np.pi))),
             r=("thp_c",), w=("thp_c",))
        s.op("pe", lambda e: e.transpose(ps["b0"][0:1, 0:128], sb["thp_c"][:, :],
                                         sb["c_ident_f32"][:, :]),
             r=("thp_c", "c_ident_f32"), w=("pb0",), tag="tr_thp")
        s.op("dve", lambda e: e.tensor_copy(sb["thpr_s"][:, :], ps["b0"][0:1, 0:128]),
             r=("pb0",), w=("thpr_s",))
        def planes(ub, cb, out, bias_key=None):
            for ut in range(NTT):
                s.op("pe", (lambda ut=ut: (lambda e: e.matmul(
                    ps["b1"][:, 0:TT], sb["thpr_s"][:, :],
                    sb["c_ramp"][0:1, ut * TT:(ut + 1) * TT], start=True, stop=True)))(),
                    r=("thpr_s", "c_ramp"), w=("pb1",), tag="u_mm")
                if bias_key is None:
                    s.op("dve", (lambda ut=ut, ub=ub: (lambda e: e.tensor_copy(
                        sb[ub][:, ut * TT:(ut + 1) * TT], ps["b1"][:, 0:TT])))(),
                        r=("pb1",), w=(ub,))
                else:
                    s.op("act", (lambda ut=ut, ub=ub: (lambda e: e.activation(
                        sb[ub][:, ut * TT:(ut + 1) * TT], ps["b1"][:, 0:TT],
                        AF.Identity, bias=sb[bias_key][:, :])))(),
                        r=("pb1", bias_key), w=(ub,))
            for half in range(2):
                sl = slice(half * (T // 2), (half + 1) * (T // 2))
                s.op("dve", (lambda sl=sl, ub=ub: (lambda e: e.tensor_copy(
                    sb["I3"][:, :], sb[ub][:, sl])))(), r=(ub,), w=("I3",))
                s.op("dve", (lambda sl=sl, cb=cb: (lambda e: e.tensor_copy(
                    sb[cb][:, sl], sb["I3"][:, :])))(), r=("I3",), w=(cb,))
                s.op("dve", (lambda sl=sl, ub=ub, cb=cb: (lambda e: e.tensor_tensor(
                    sb[ub][:, sl], sb[ub][:, sl], sb[cb][:, sl], ALU.subtract)))(),
                    r=(ub, cb), w=(ub,))
                s.op("dve", (lambda sl=sl, ub=ub, cb=cb: (lambda e: e.tensor_single_scalar(
                    sb[cb][:, sl], sb[ub][:, sl], 0.5, ALU.is_ge)))(),
                    r=(ub,), w=(cb,))
                s.op("dve", (lambda sl=sl, ub=ub, cb=cb: (lambda e: e.tensor_tensor(
                    sb[ub][:, sl], sb[ub][:, sl], sb[cb][:, sl], ALU.subtract)))(),
                    r=(ub, cb), w=(ub,))
            s.op("act", (lambda ub=ub, out=out: (lambda e: e.activation(
                sb[out][:, :], sb[ub][:, :], AF.Sin, scale=float(2 * np.pi))))(),
                r=(ub,), w=(out,))
        planes("U3", "C3", "SIN3")
        planes("U3", "C3", "COS3", bias_key="c_q25")
        for tt in range(NTT):
            xnb = "xn1_a" if tt % 2 == 0 else "xn1_b"
            s.dma("qs", (lambda tt=tt, xnb=xnb: (lambda e: e.dma_start(
                out=sb[xnb][:, :].rearrange("p (dc t) -> p dc t", t=TT),
                in_=dd["xn1_dram"].rearrange("p (dc t) -> p dc t", t=T)[:, :, tt * TT:(tt + 1) * TT])))(),
                r=("xn1_dram",), w=(xnb,), tag="xn1_ld")
            for wname, target, bias, func in [
                    ("w_psi_re", "A3", "c_bpsi_re_c", AF.Identity),
                    ("w_psi_im", "B3", "c_bpsi_im_c", AF.Identity),
                    ("w_gate3", "g3_t", "c_bgate_c", AF.Sigmoid)]:
                bank = "b2" if wname != "w_gate3" else "b4"
                def mm_pr(e, xnb=xnb, wname=wname, bank=bank):
                    for dc in range(NDC):
                        i = e.matmul(ps[bank][:, 0:TT],
                                     sb[wname][:, dc * 128:(dc + 1) * 128],
                                     sb[xnb][:, dc * TT:(dc + 1) * TT],
                                     start=(dc == 0), stop=(dc == NDC - 1))
                    return i
                s.op("pe", mm_pr, r=(xnb, wname), w=(f"p{bank}",))
                if target == "g3_t":
                    s.op("act", (lambda cc=cc, bias=bias, bank=bank: (lambda e: e.activation(
                        sb["g3_t"][:, :], ps[bank][:, 0:TT], AF.Sigmoid,
                        bias=sb[bias][:, cc:cc + 1])))(),
                        r=(f"p{bank}", bias), w=("g3_t",))
                else:
                    s.op("act", (lambda tt=tt, cc=cc, target=target, bias=bias, bank=bank:
                                 (lambda e: e.activation(
                        sb[target][:, tt * TT:(tt + 1) * TT], ps[bank][:, 0:TT],
                        AF.Identity, bias=sb[bias][:, cc:cc + 1])))(),
                        r=(f"p{bank}", bias), w=(target,))
            s.op("dve", (lambda tt=tt: (lambda e: e.tensor_tensor(
                sb["A3"][:, tt * TT:(tt + 1) * TT], sb["A3"][:, tt * TT:(tt + 1) * TT],
                sb["g3_t"][:, :], ALU.mult)))(), r=("A3", "g3_t"), w=("A3",))
            s.op("dve", (lambda tt=tt: (lambda e: e.tensor_tensor(
                sb["B3"][:, tt * TT:(tt + 1) * TT], sb["B3"][:, tt * TT:(tt + 1) * TT],
                sb["g3_t"][:, :], ALU.mult)))(), r=("B3", "g3_t"), w=("B3",))
        s.op("pool", lambda e: e.tensor_mul(sb["F3"][:, :], sb["SIN3"][:, :], sb["B3"][:, :]),
             r=("SIN3", "B3"), w=("F3",))
        s.op("dve", lambda e: e.tensor_mul(sb["D3"][:, :], sb["SIN3"][:, :], sb["A3"][:, :]),
             r=("SIN3", "A3"), w=("D3",))
        s.op("pool", lambda e: e.tensor_mul(sb["B3"][:, :], sb["COS3"][:, :], sb["B3"][:, :]),
             r=("COS3", "B3"), w=("B3",))
        s.op("dve", lambda e: e.tensor_mul(sb["A3"][:, :], sb["COS3"][:, :], sb["A3"][:, :]),
             r=("COS3", "A3"), w=("A3",))
        s.op("pool", lambda e: e.tensor_tensor(sb["A3"][:, :], sb["A3"][:, :], sb["F3"][:, :], ALU.add),
             r=("A3", "F3"), w=("A3",))
        s.op("dve", lambda e: e.tensor_tensor(sb["B3"][:, :], sb["B3"][:, :], sb["D3"][:, :],
                                              ALU.subtract),
             r=("B3", "D3"), w=("B3",))
        s.op("dve", lambda e: e.memset(sb["F3"][:, :], 1.0), w=("F3",))
        s.op("dve", lambda e: e.tensor_scalar_mul(sb["F3"][:, :], sb["F3"][:, :], sb["rho_c"][:, :]),
             r=("F3", "rho_c"), w=("F3",))
        s.op("dve", lambda e: e.tensor_tensor_scan(sb["D3"][:, :], sb["F3"][:, :],
                                                   sb["A3"][:, :], 0.0, ALU.mult, ALU.add),
             r=("F3", "A3"), w=("D3",))
        s.op("dve", lambda e: e.tensor_tensor_scan(sb["A3"][:, :], sb["F3"][:, :],
                                                   sb["B3"][:, :], 0.0, ALU.mult, ALU.add),
             r=("F3", "B3"), w=("A3",))
        s.op("pool", lambda e: e.tensor_mul(sb["B3"][:, :], sb["COS3"][:, :], sb["D3"][:, :]),
             r=("COS3", "D3"), w=("B3",))
        s.op("dve", lambda e: e.tensor_mul(sb["F3"][:, :], sb["SIN3"][:, :], sb["D3"][:, :]),
             r=("SIN3", "D3"), w=("F3",))
        s.op("pool", lambda e: e.tensor_mul(sb["D3"][:, :], sb["SIN3"][:, :], sb["A3"][:, :]),
             r=("SIN3", "A3"), w=("D3",))
        s.op("dve", lambda e: e.tensor_mul(sb["A3"][:, :], sb["COS3"][:, :], sb["A3"][:, :]),
             r=("COS3", "A3"), w=("A3",))
        s.op("pool", lambda e: e.tensor_tensor(sb["B3"][:, :], sb["B3"][:, :], sb["D3"][:, :],
                                              ALU.subtract), r=("B3", "D3"), w=("B3",))
        s.op("dve", lambda e: e.tensor_tensor(sb["A3"][:, :], sb["A3"][:, :], sb["F3"][:, :],
                                              ALU.add), r=("A3", "F3"), w=("A3",))
        s.dma("qa", (lambda cc=cc: (lambda e: e.dma_start(
            out=dd["h_re_dram"][cc * 128:(cc + 1) * 128, :], in_=sb["B3"][:, :])))(),
            r=("B3",), w=("h_re_dram",))
        s.dma("qa", (lambda cc=cc: (lambda e: e.dma_start(
            out=dd["h_im_dram"][cc * 128:(cc + 1) * 128, :], in_=sb["A3"][:, :])))(),
            r=("A3",), w=("h_im_dram",))
        def mm_st(e, cc=cc, src="B3", qb=0, sqb="U3"):
            for u in range(32):
                for q, buf in [(qb, src), (qb + 2, sqb)]:
                    c = cc * 128 + q * 32 + u
                    i = e.matmul(ps["b3"][:, c:c + 1],
                                 sb[buf][:, u * 128:(u + 1) * 128],
                                 sb["c_ones_col"][:, :],
                                 start=True, stop=True)
            return i
        s.op("pool", lambda e: e.tensor_mul(sb["U3"][:, :], sb["B3"][:, :], sb["B3"][:, :]),
             r=("B3",), w=("U3",))
        s.op("pe", (lambda f=mm_st: (lambda e: f(e)))(),
             r=("B3", "U3", "c_ones_col"), w=("pb3",), tag="stf")
        s.op("dve", lambda e: e.tensor_mul(sb["C3"][:, :], sb["A3"][:, :], sb["A3"][:, :]),
             r=("A3",), w=("C3",))
        s.op("pe", (lambda f=mm_st: (lambda e: f(e, src="A3", qb=1, sqb="C3")))(),
             r=("A3", "C3", "c_ones_col"), w=("pb3",), tag="stf")
    s.op("dve", lambda e: e.tensor_copy(sb["strows"][:, :], ps["b3"][:, 0:128]),
         r=("pb3",), w=("strows",))
    for cc in range(1, 4):
        s.op("dve", (lambda cc=cc: (lambda e: e.tensor_tensor(
            sb["strows"][:, :], sb["strows"][:, :],
            ps["b3"][:, cc * 128:(cc + 1) * 128], ALU.add)))(),
            r=("pb3", "strows"), w=("strows",))
    for q, (dn, row) in enumerate([("st_sum_in", 0), ("st_sum_in", 1),
                                   ("st_sq_in", 0), ("st_sq_in", 1)]):
        stb = f"st32_{q % 2}"
        tb = "b5" if q % 2 == 0 else "b6"
        s.op("pe", (lambda q=q, tb=tb: (lambda e: e.transpose(
            ps[tb][0:32, 0:128], sb["strows"][:, q * 32:(q + 1) * 32],
            sb["c_ident_f32"][:, :])))(),
            r=("strows", "c_ident_f32"), w=(f"p{tb}",), tag="st_tr")
        s.op("dve", (lambda stb=stb, tb=tb: (lambda e: e.tensor_copy(
            sb[stb][:, :], ps[tb][0:32, 0:128])))(), r=(f"p{tb}",), w=(stb,))
        s.dma("qa", (lambda dn=dn, row=row, stb=stb: (lambda e: e.dma_start(
            out=dd[dn][row:row + 1, :].rearrange("r (u p) -> (r u) p", p=128),
            in_=sb[stb][:, :])))(), r=(stb,), w=(dn,))
    s.fence()
    sc3.close()
    k.scopes.remove(sc3)
    emit_p3c2(k)


def emit_p3c2(k):
    nc, s, sb, ps, di, dd, alloc = k.nc, k.s, k.sb, k.ps, k.di, k.dd, k.alloc
    sc = contextlib.ExitStack()
    k.scopes.append(sc)
    k.load_consts(["bphi_re_c", "bphi_im_c", "sel8"], sc)
    for i in range(4):
        alloc(f"h_re_t{i}", [128, TT], F32, sc)
        alloc(f"h_im_t{i}", [128, TT], F32, sc)
    for i in range(2):
        alloc(f"hp_t{i}", [128, TT], F32, sc)
        alloc(f"s1_t{i}", [128, TT], F32, sc)
    alloc("c_tiny8", [8, 1], F32, sc)
    s.op("dve", lambda e: e.memset(sb["c_tiny8"][:, :], 1e-30), w=("c_tiny8",))
    alloc("w_phi_re", [128, NDC * 512], BF16, sc)
    alloc("w_phi_im", [128, NDC * 512], BF16, sc)
    for i in range(2):
        alloc(f"xn1_c{i}", [128, NDC * TT], BF16, sc)
    alloc("phi_re_t", [128, 4 * TT], F32, sc)
    alloc("phi_im_t", [128, 4 * TT], F32, sc)
    for nm in ["rb_mu", "rb_var", "rb_lnv", "rb_rstd",
               "rpr_all", "apr_all", "rpi_all", "api_all"]:
        alloc(nm, [8, TT], F32, sc)
    s.collective(lambda e: e.collective_compute(
        "AllReduce", ALU.add, replica_groups=GROUPS,
        ins=[dd["st_sum_in"][:, :]], outs=[dd["st_sum_out"][:, :]]),
        r=("st_sum_in",), w=("st_sum_out",), tag="cc_st1")
    s.collective(lambda e: e.collective_compute(
        "AllReduce", ALU.add, replica_groups=GROUPS,
        ins=[dd["st_sq_in"][:, :]], outs=[dd["st_sq_out"][:, :]]),
        r=("st_sq_in",), w=("st_sq_out",), tag="cc_st2")
    for wn, sn in [("wphi_re_T", "w_phi_re"), ("wphi_im_T", "w_phi_im")]:
        s.dma("qp", (lambda wn=wn, sn=sn: (lambda e: e.dma_start(
            out=sb[sn][:, :].rearrange("p (dc m) -> p dc m", m=512),
            in_=feat_view(di[wn]))))(), w=(sn,), tag="wphi")
    for pl, (srow, rpa, apa) in enumerate([(0, "rpr_all", "apr_all"),
                                           (1, "rpi_all", "api_all")]):
        s.dma("qs", (lambda srow=srow: (lambda e: e.dma_start(
            out=sb["rb_mu"][:, :],
            in_=dd["st_sum_out"][srow:srow + 1, :].rearrange("r (c t) -> (r c) t", t=TT))))(),
            r=("st_sum_out",), w=("rb_mu",))
        s.dma("qs", (lambda srow=srow: (lambda e: e.dma_start(
            out=sb["rb_var"][:, :],
            in_=dd["st_sq_out"][srow:srow + 1, :].rearrange("r (c t) -> (r c) t", t=TT))))(),
            r=("st_sq_out",), w=("rb_var",))
        s.op("dve", lambda e: e.tensor_scalar_mul(sb["rb_mu"][:, :], sb["rb_mu"][:, :], 1.0 / D),
             r=("rb_mu",), w=("rb_mu",))
        s.op("dve", lambda e: e.tensor_mul(sb["rb_lnv"][:, :], sb["rb_mu"][:, :], sb["rb_mu"][:, :]),
             r=("rb_mu",), w=("rb_lnv",))
        s.op("dve", lambda e: e.scalar_tensor_tensor(sb["rb_var"][:, :], sb["rb_var"][:, :],
                                                     1.0 / D, sb["rb_lnv"][:, :],
                                                     ALU.mult, ALU.subtract),
             r=("rb_var", "rb_lnv"), w=("rb_var",))
        s.op("act", lambda e: e.activation(sb["rb_lnv"][:, :], sb["rb_var"][:, :], AF.Ln,
                                           bias=sb["c_tiny8"][:, :]),
             r=("rb_var", "c_tiny8"), w=("rb_lnv",))
        s.op("act", lambda e: e.activation(sb["rb_rstd"][:, :], sb["rb_lnv"][:, :], AF.Exp,
                                           scale=0.5), r=("rb_lnv",), w=("rb_rstd",))
        s.op("dve", lambda e: e.tensor_scalar_add(sb["rb_rstd"][:, :], sb["rb_rstd"][:, :],
                                                  1e-6), r=("rb_rstd",), w=("rb_rstd",))
        s.op("dve", (lambda rpa=rpa: (lambda e: e.reciprocal(sb[rpa][:, :], sb["rb_rstd"][:, :])))(),
             r=("rb_rstd",), w=(rpa,))
        s.op("dve", (lambda rpa=rpa, apa=apa: (lambda e: e.scalar_tensor_tensor(
            sb[apa][:, :], sb["rb_mu"][:, :], -1.0, sb[rpa][:, :], ALU.mult, ALU.mult)))(),
            r=("rb_mu", rpa), w=(apa,))
    for tt in range(NTT):
        xn1c = f"xn1_c{tt % 2}"
        s.dma("qs", (lambda tt=tt, xn1c=xn1c: (lambda e: e.dma_start(
            out=sb[xn1c][:, :].rearrange("p (dc t) -> p dc t", t=TT),
            in_=dd["xn1_dram"].rearrange("p (dc t) -> p dc t", t=T)[:, :, tt * TT:(tt + 1) * TT])))(),
            r=("xn1_dram",), w=(xn1c,), tag="xn1_c")
        for cc in range(4):
            for wn, tb, bias, bank in [("w_phi_re", "phi_re_t", "c_bphi_re_c", "b0"),
                                       ("w_phi_im", "phi_im_t", "c_bphi_im_c", "b1")]:
                def mm_phi(e, wn=wn, cc=cc, bank=bank, xn1c=xn1c):
                    for dc in range(NDC):
                        i = e.matmul(ps[bank][:, 0:TT],
                                     sb[wn][:, dc * 512 + cc * 128: dc * 512 + (cc + 1) * 128],
                                     sb[xn1c][:, dc * TT:(dc + 1) * TT],
                                     start=(dc == 0), stop=(dc == NDC - 1))
                    return i
                s.op("pe", mm_phi, r=(xn1c, wn), w=(f"p{bank}",))
                s.op("act", (lambda tb=tb, bias=bias, cc=cc, bank=bank:
                             (lambda e: e.activation(
                    sb[tb][:, cc * TT:(cc + 1) * TT], ps[bank][:, 0:TT], AF.Identity,
                    bias=sb[bias][:, cc:cc + 1])))(), r=(f"p{bank}", bias), w=(tb + f"_{cc}",))
        for bank, row in [("b3", "rpr_all"), ("b4", "apr_all"),
                          ("b5", "rpi_all"), ("b6", "api_all")]:
            s.op("pe", (lambda bank=bank, row=row, tt=tt: (lambda e: e.matmul(
                ps[bank][:, 0:TT], sb["c_sel8"][:, tt * 128:(tt + 1) * 128],
                sb[row][:, :], start=True, stop=True)))(),
                r=(row, "c_sel8"), w=(f"p{bank}",))
        for cc in range(4):
            for hd, ht in [("h_re_dram", f"h_re_t{cc}"), ("h_im_dram", f"h_im_t{cc}")]:
                s.dma("qs", (lambda hd=hd, cc=cc, tt=tt, ht=ht: (lambda e: e.dma_start(
                    out=sb[ht][:, :],
                    in_=dd[hd][cc * 128:(cc + 1) * 128, tt * TT:(tt + 1) * TT])))(),
                    r=(hd,), w=(ht,), tag="h_ld2")
        for cc in range(4):
            hre, him = f"h_re_t{cc}", f"h_im_t{cc}"
            hp = "hp_t" + str(cc % 2)
            s1t = "s1_t" + str(cc % 2)
            s.op("dve", (lambda him=him, hp=hp: (lambda e: e.tensor_tensor(
                sb[hp][:, :], sb[him][:, :], ps["b5"][:, 0:TT], ALU.mult)))(),
                r=(him, "pb5"), w=(hp,))
            s.op("dve", (lambda hp=hp: (lambda e: e.tensor_tensor(
                sb[hp][:, :], sb[hp][:, :], ps["b6"][:, 0:TT], ALU.add)))(),
                r=(hp, "pb6"), w=(hp,))
            s.op("pool", (lambda cc=cc, hp=hp: (lambda e: e.tensor_tensor(
                sb[hp][:, :], sb[hp][:, :],
                sb["phi_im_t"][:, cc * TT:(cc + 1) * TT], ALU.mult)))(),
                r=(hp, f"phi_im_t_{cc}"), w=(hp,))
            s.op("dve", (lambda hre=hre, s1t=s1t: (lambda e: e.tensor_tensor(
                sb[s1t][:, :], sb[hre][:, :], ps["b3"][:, 0:TT], ALU.mult)))(),
                r=(hre, "pb3"), w=(s1t,))
            s.op("dve", (lambda s1t=s1t: (lambda e: e.tensor_tensor(
                sb[s1t][:, :], sb[s1t][:, :], ps["b4"][:, 0:TT], ALU.add)))(),
                r=(s1t, "pb4"), w=(s1t,))
            s.op("dve", (lambda cc=cc, s1t=s1t: (lambda e: e.tensor_tensor(
                sb[s1t][:, :], sb[s1t][:, :],
                sb["phi_re_t"][:, cc * TT:(cc + 1) * TT], ALU.mult)))(),
                r=(s1t, f"phi_re_t_{cc}"), w=(s1t,))
            s.op("pool", (lambda hp=hp, s1t=s1t: (lambda e: e.tensor_tensor(
                sb[s1t][:, :], sb[s1t][:, :], sb[hp][:, :], ALU.add)))(),
                r=(s1t, hp), w=(s1t,))
            s.dma("qa", (lambda cc=cc, tt=tt, s1t=s1t: (lambda e: e.dma_start(
                out=dd["s1_dram"][cc * 128:(cc + 1) * 128, tt * TT:(tt + 1) * TT],
                in_=sb[s1t][:, :])))(), r=(s1t,), w=("s1_dram",))
    s.collective(lambda e: e.collective_compute(
        "AllGather", ALU.bypass, replica_groups=GROUPS,
        ins=[dd["s1_dram"][:, :]], outs=[dd["s1_ag_out"][:, :]]),
        r=("s1_dram",), w=("s1_ag_out",), tag="cc_s1")
    s.fence()
    sc.close()
    k.scopes.remove(sc)


def emit_p4(k):
    nc, s, sb, ps, di, dd, alloc = k.nc, k.s, k.sb, k.ps, k.di, k.dd, k.alloc
    sc = contextlib.ExitStack()
    k.scopes.append(sc)
    k.load_consts(["norm_g_c", "norm_b_c", "slot_g_row", "slot_b_row"], sc)

    scln = contextlib.ExitStack()
    k.scopes.append(scln)
    k.rows_alloc(scln)
    for i in range(2):
        alloc(f"x4a{i}", [128, NDC * TT], BF16, scln)
    alloc("x4sq", [128, NDC * TT], BF16, scln)
    alloc("x4o", [128, NDC * TT], BF16, scln)
    alloc("t4f", [128, TT], F32, scln)

    def p4ln_a(tt):
        x4a = f"x4a{tt % 2}"
        for hf in range(2):
            s.dma("qs", (lambda tt=tt, x4a=x4a, hf=hf: (lambda e: e.dma_start(
                out=sb[x4a][:, :].rearrange("p (dc t) -> p dc t", t=TT)[:, hf * 4:(hf + 1) * 4, :],
                in_=dd["xn1_dram"].rearrange("p (dc t) -> p dc t", t=T)
                    [:, hf * 4:(hf + 1) * 4, tt * TT:(tt + 1) * TT])))(),
                r=("xn1_dram",), w=(f"{x4a}h{hf}", "p4_src"), tag="xln_ld")
        s.op("act", (lambda x4a=x4a: (lambda e: e.activation(
            sb["x4sq"][:, :], sb[x4a][:, :], AF.Square)))(),
            r=(f"{x4a}h0", f"{x4a}h1"), w=("x4sq", "p4_sq"))
        ln_stats(k, "p4",
                 (lambda x4a=x4a: (lambda dc: sb[x4a][:, dc * TT:(dc + 1) * TT]))(),
                 lambda dc: sb["x4sq"][:, dc * TT:(dc + 1) * TT], par=str(tt % 2))

    def p4ln_b(tt):
        x4a = f"x4a{tt % 2}"
        ln_finish(k, par=str(tt % 2))
        for dc in range(NDC):
            s.op("dve", (lambda dc=dc, x4a=x4a: (lambda e: e.tensor_tensor(
                sb["t4f"][:, :], sb[x4a][:, dc * TT:(dc + 1) * TT],
                ps["b3"][:, 0:TT], ALU.mult)))(), r=(f"{x4a}h{dc // 4}", "pb3"), w=("t4f",))
            s.op("dve", (lambda dc=dc: (lambda e: e.tensor_tensor(
                sb["t4f"][:, :], sb["t4f"][:, :], ps["b4"][:, 0:TT], ALU.add)))(),
                r=("t4f", "pb4"), w=("t4f",))
            s.op("dve", (lambda dc=dc: (lambda e: e.tensor_scalar(
                sb["x4o"][:, dc * TT:(dc + 1) * TT], sb["t4f"][:, :],
                sb["c_norm_g_c"][:, dc:dc + 1], sb["c_norm_b_c"][:, dc:dc + 1],
                ALU.mult, ALU.add)))(),
                r=("t4f", "c_norm_g_c", "c_norm_b_c"), w=("x4o",))
        s.dma("qa", (lambda tt=tt: (lambda e: e.dma_start(
            out=dd["xln_dram"].rearrange("p (dc t) -> p dc t", t=T)[:, :, tt * TT:(tt + 1) * TT],
            in_=sb["x4o"][:, :].rearrange("p (dc t) -> p dc t", t=TT))))(),
            r=("x4o",), w=("xln_dram",))

    for i in range(NTT + 1):
        if i < NTT:
            p4ln_a(i)
        if i >= 1:
            p4ln_b(i - 1)

    s.fence()
    scln.close()
    k.scopes.remove(scln)
    alloc("se_T_own", [128, NDC * 512], BF16, sc)
    alloc("se_T_glob", [128, NDC * D], BF16, sc)
    alloc("slots_bf", [128, NDC * 512], BF16, sc)
    scse = contextlib.ExitStack()
    k.scopes.append(scse)
    alloc("gb_b", [128, D], F32, scse)
    alloc("bb_b", [128, D], F32, scse)
    alloc("sc_col", [128, 1], F32, scse)
    alloc("se_in", [128, D], F32, scse)
    alloc("se_sq", [128, D], F32, scse)
    alloc("se_ln", [128, D], BF16, scse)
    for nm in ["se_mu", "se_var", "se_rs"]:
        alloc(nm, [128, 1], F32, scse)
    s.op("pe", lambda e: e.matmul(ps["b0"][:, 0:1], sb["c_ones_row"][:, :],
                                  sb["c_scale_s"][:, :], start=True, stop=True),
         r=("c_scale_s", "c_ones_row"), w=("pb0",))
    s.op("dve", lambda e: e.tensor_copy(sb["sc_col"][:, :], ps["b0"][:, 0:1]),
         r=("pb0",), w=("sc_col",))
    for row, plane in [("c_slot_g_row", "gb_b"), ("c_slot_b_row", "bb_b")]:
        for h in range(2):
            s.op("pe", (lambda row=row, h=h: (lambda e: e.matmul(
                ps["b0"][:, 0:TT], sb["c_ones_row"][:, :],
                sb[row][0:1, h * TT:(h + 1) * TT], start=True, stop=True)))(),
                r=(row, "c_ones_row"), w=("pb0",))
            s.op("dve", (lambda plane=plane, h=h: (lambda e: e.tensor_copy(
                sb[plane][:, h * TT:(h + 1) * TT], ps["b0"][:, 0:TT])))(),
                r=("pb0",), w=(plane,))
    for which, src_name, width, nchunks in [("own", "se_perm", 512, 4),
                                            ("glob", "se_glob", D, 8)]:
        for ch in range(nchunks):
            s.dma("qs", (lambda src_name=src_name, ch=ch: (lambda e: e.dma_start(
                out=sb["se_in"][:, :], in_=di[src_name][ch * 128:(ch + 1) * 128, :])))(),
                w=("se_in",), tag="se_ld")
            s.op("dve", lambda e: e.reduce_sum(sb["se_mu"][:, :], sb["se_in"][:, :], axis=AX),
                 r=("se_in",), w=("se_mu",))
            s.op("dve", lambda e: e.tensor_scalar_mul(sb["se_mu"][:, :], sb["se_mu"][:, :], -1.0 / D),
                 r=("se_mu",), w=("se_mu",))
            s.op("dve", lambda e: e.tensor_scalar_add(sb["se_in"][:, :], sb["se_in"][:, :],
                                                      sb["se_mu"][:, :]),
                 r=("se_in", "se_mu"), w=("se_in",))
            s.op("act", lambda e: e.activation(sb["se_sq"][:, :], sb["se_in"][:, :], AF.Square),
                 r=("se_in",), w=("se_sq",))
            s.op("dve", lambda e: e.reduce_sum(sb["se_var"][:, :], sb["se_sq"][:, :], axis=AX),
                 r=("se_sq",), w=("se_var",))
            s.op("act", lambda e: e.activation(sb["se_var"][:, :], sb["se_var"][:, :], AF.Ln,
                                               scale=1.0 / D, bias=sb["c_eps128"][:, :]),
                 r=("se_var", "c_eps128"), w=("se_var",))
            s.op("act", lambda e: e.activation(sb["se_rs"][:, :], sb["se_var"][:, :], AF.Exp,
                                               scale=-0.5), r=("se_var",), w=("se_rs",))
            s.op("dve", lambda e: e.tensor_mul(sb["se_rs"][:, :], sb["se_rs"][:, :], sb["sc_col"][:, :]),
                 r=("se_rs", "sc_col"), w=("se_rs",))
            s.op("dve", lambda e: e.tensor_scalar_mul(sb["se_in"][:, :], sb["se_in"][:, :],
                                                      sb["se_rs"][:, :]),
                 r=("se_in", "se_rs"), w=("se_in",))
            s.op("dve", lambda e: e.tensor_mul(sb["se_in"][:, :], sb["se_in"][:, :], sb["gb_b"][:, :]),
                 r=("se_in", "gb_b"), w=("se_in",))
            s.op("dve", lambda e: e.tensor_tensor(sb["se_ln"][:, :], sb["se_in"][:, :],
                                                  sb["bb_b"][:, :], ALU.add),
                 r=("se_in", "bb_b"), w=("se_ln",))
            tgt = "se_T_own" if which == "own" else "se_T_glob"
            for dcd in range(NDC):
                s.op("pe", (lambda dcd=dcd: (lambda e: e.transpose(
                    ps["t0"][:, 0:128], sb["se_ln"][:, dcd * 128:(dcd + 1) * 128],
                    sb["c_ident_bf"][:, :])))(),
                    r=("se_ln", "c_ident_bf"), w=("pt0",), tag="tr_se")
                s.op("dve", (lambda tgt=tgt, dcd=dcd, ch=ch, width=width: (lambda e: e.tensor_copy(
                    sb[tgt][:, dcd * width + ch * 128: dcd * width + (ch + 1) * 128],
                    ps["t0"][:, 0:128])))(), r=("pt0",), w=(tgt,))

    s.fence()
    scse.close()
    k.scopes.remove(scse)
    scdi = contextlib.ExitStack()
    k.scopes.append(scdi)
    alloc("dispT", [128, 32 * 512], BF16, scdi)
    alloc("xln_a", [128, NDC * TT], BF16, scdi)
    alloc("xln_b", [128, NDC * TT], BF16, scdi)
    alloc("runm", [128, 4], F32, scdi)
    alloc("negm4", [128, 4], F32, scdi)
    alloc("zs", [128, 4], F32, scdi)
    alloc("tc4", [128, 1], F32, scdi)
    alloc("tc5", [128, 1], F32, scdi)
    alloc("tnb", [128, 128], BF16, scdi)
    scd = contextlib.ExitStack()
    k.scopes.append(scd)
    alloc("dispE", [128, 4 * T], BF16, scd)
    def mm_logit_own(e, xnb, esc, bank="b0"):
        for dcd in range(NDC):
            i = e.matmul(ps[bank][:, 0:TT],
                         sb["se_T_own"][:, dcd * 512 + esc * 128: dcd * 512 + (esc + 1) * 128],
                         sb[xnb][:, dcd * TT:(dcd + 1) * TT],
                         start=(dcd == 0), stop=(dcd == NDC - 1))
        return i
    for pas in range(2):
        for tt in range(NTT):
            xnb = "xln_a" if tt % 2 == 0 else "xln_b"
            s.dma("qs", (lambda tt=tt, xnb=xnb: (lambda e: e.dma_start(
                out=sb[xnb][:, :].rearrange("p (dc t) -> p dc t", t=TT),
                in_=dd["xln_dram"].rearrange("p (dc t) -> p dc t", t=T)[:, :, tt * TT:(tt + 1) * TT])))(),
                r=("xln_dram",), w=(xnb,), tag="xln_p")
            for esc in range(4):
                bank = "b0" if esc % 2 == 0 else "b1"
                tc = "tc4" if esc % 2 == 0 else "tc5"
                s.op("pe", (lambda xnb=xnb, esc=esc, bank=bank: (lambda e: mm_logit_own(e, xnb, esc, bank)))(),
                     r=(xnb, "se_T_own"), w=(f"p{bank}",))
                if pas == 0:
                    if tt == 0:
                        s.op("dve", (lambda esc=esc, bank=bank: (lambda e: e.reduce_max(
                            sb["runm"][:, esc:esc + 1], ps[bank][:, 0:TT], axis=AX)))(),
                            r=(f"p{bank}",), w=("runm",))
                    else:
                        s.op("dve", (lambda bank=bank, tc=tc: (lambda e: e.reduce_max(
                            sb[tc][:, :], ps[bank][:, 0:TT], axis=AX)))(),
                            r=(f"p{bank}",), w=(tc,))
                        s.op("dve", (lambda esc=esc, tc=tc: (lambda e: e.tensor_max(
                            sb["runm"][:, esc:esc + 1], sb["runm"][:, esc:esc + 1],
                            sb[tc][:, :])))(), r=("runm", tc), w=("runm",))
                else:
                    s.op("act", (lambda esc=esc, tt=tt, bank=bank, tc=tc: (lambda e: e.activation(
                        sb["dispE"][:, esc * T + tt * TT: esc * T + (tt + 1) * TT],
                        ps[bank][:, 0:TT], AF.Exp, bias=sb["negm4"][:, esc:esc + 1],
                        accum_out=sb[tc][:, :])))(),
                        r=(f"p{bank}", "negm4"), w=(f"dispE_{esc}_{tt}", tc))
                    if tt == 0:
                        s.op("dve", (lambda esc=esc, tc=tc: (lambda e: e.tensor_copy(
                            sb["zs"][:, esc:esc + 1], sb[tc][:, :])))(),
                            r=(tc,), w=("zs",))
                    else:
                        s.op("dve", (lambda esc=esc, tc=tc: (lambda e: e.tensor_tensor(
                            sb["zs"][:, esc:esc + 1], sb["zs"][:, esc:esc + 1],
                            sb[tc][:, :], ALU.add)))(), r=("zs", tc), w=("zs",))
        if pas == 0:
            s.op("dve", lambda e: e.tensor_scalar_mul(sb["negm4"][:, :], sb["runm"][:, :], -1.0),
                 r=("runm",), w=("negm4",))
    s.op("dve", lambda e: e.reciprocal(sb["zs"][:, :], sb["zs"][:, :]), r=("zs",), w=("zs",))
    for esc in range(4):
        for nt in range(32):
            s.op("dve", (lambda esc=esc, nt=nt: (lambda e: e.tensor_scalar_mul(
                sb["tnb"][:, :], sb["dispE"][:, esc * T + nt * 128: esc * T + (nt + 1) * 128],
                sb["zs"][:, esc:esc + 1])))(),
                r=(f"dispE_{esc}_{nt // 4}", "zs"), w=("tnb",))
            s.op("pe", lambda e: e.transpose(ps["t0"][:, 0:128], sb["tnb"][:, :],
                                             sb["c_ident_bf"][:, :]),
                 r=("tnb", "c_ident_bf"), w=("pt0",), tag="tr_d")
            s.op("dve", (lambda esc=esc, nt=nt: (lambda e: e.tensor_copy(
                sb["dispT"][:, nt * 512 + esc * 128: nt * 512 + (esc + 1) * 128],
                ps["t0"][:, 0:128])))(), r=("pt0",), w=("dispT",))
    s.fence()
    scd.close()
    k.scopes.remove(scd)

    alloc("xln_nt", [128, 4 * NDC * 128], BF16, scdi)
    alloc("xlnT", [128, NDC * 128], BF16, scdi)
    alloc("slots_f", [128, NDC * 128 * 4], F32, scdi)
    s.op("dve", lambda e: e.memset(sb["slots_f"][:, :], 0.0), w=("slots_f",))
    for g in range(8):
        for dcd in range(NDC):
            bank = "b0" if dcd % 2 == 0 else "b1"
            for ntl in range(4):
                nt = g * 4 + ntl
                if dcd == 0:
                    s.dma("qs", (lambda nt=nt, ntl=ntl: (lambda e: e.dma_start(
                        out=sb["xln_nt"][:, ntl * 1024:(ntl + 1) * 1024].rearrange(
                            "p (dc t) -> p dc t", t=128),
                        in_=dd["xln_dram"].rearrange("p (dc t) -> p dc t", t=T)[:, :, nt * 128:(nt + 1) * 128])))(),
                        r=("xln_dram",), w=(f"xlnnt_{ntl}",), tag="xln_s")
            for ntl in range(4):
                nt = g * 4 + ntl
                s.op("pe", (lambda dcd=dcd, ntl=ntl: (lambda e: e.transpose(
                    ps["t0"][:, 0:128],
                    sb["xln_nt"][:, ntl * 1024 + dcd * 128: ntl * 1024 + (dcd + 1) * 128],
                    sb["c_ident_bf"][:, :])))(),
                    r=(f"xlnnt_{ntl}", "c_ident_bf"), w=("pt0",), tag="tr_x")
                s.op("dve", (lambda ntl=ntl: (lambda e: e.tensor_copy(
                    sb["xlnT"][:, ntl * 128:(ntl + 1) * 128], ps["t0"][:, 0:128])))(),
                    r=("pt0",), w=("xlnT",))
                s.op("pe", (lambda ntl=ntl, nt=nt, bank=bank: (lambda e: e.matmul(
                    ps[bank][:, 0:TT], sb["xlnT"][:, ntl * 128:(ntl + 1) * 128],
                    sb["dispT"][:, nt * 512:(nt + 1) * 512],
                    start=(ntl == 0), stop=(ntl == 3))))(),
                    r=("xlnT", "dispT"), w=(f"p{bank}",), tag="slots_mm")
            s.op("dve", (lambda dcd=dcd, bank=bank: (lambda e: e.tensor_tensor(
                sb["slots_f"][:, dcd * 512:(dcd + 1) * 512],
                sb["slots_f"][:, dcd * 512:(dcd + 1) * 512], ps[bank][:, 0:TT], ALU.add)))(),
                r=("slots_f", f"p{bank}"), w=("slots_f",))
    s.op("dve", lambda e: e.tensor_copy(sb["slots_bf"][:, :], sb["slots_f"][:, :]),
         r=("slots_f",), w=("slots_bf",))
    s.fence()
    scdi.close()
    k.scopes.remove(scdi)

    sce = contextlib.ExitStack()
    k.scopes.append(sce)
    alloc("eo_t", [64, D], F32, sce)
    for i in range(4):
        alloc(f"wR{i}", [128, 16 * D], BF16, sce)
    alloc("bg_sb", [128, 8 * 16], F32, sce)
    alloc("bu_sb", [128, 8 * 16], F32, sce)
    alloc("bd_bf0", [1, D], BF16, sce)
    alloc("bd_bf1", [1, D], BF16, sce)
    alloc("gelu_t", [128, 16 * 64], F32, sce)
    alloc("h2_bf", [128, 16 * 64], BF16, sce)
    s.dma("qs", lambda e: e.dma_start(
        out=sb["bg_sb"][:, :].rearrange("p (e h) -> p e h", h=16),
        in_=di["bg"].rearrange("e p h -> p e h")), w=("bg_sb",))
    s.dma("qs", lambda e: e.dma_start(
        out=sb["bu_sb"][:, :].rearrange("p (e h) -> p e h", h=16),
        in_=di["bu"].rearrange("e p h -> p e h")), w=("bu_sb",))
    for el in range(8):
        wg_b = f"wR{(3 * el) % 4}"
        wu_b = f"wR{(3 * el + 1) % 4}"
        wd_b = f"wR{(3 * el + 2) % 4}"
        bd_b = f"bd_bf{el % 2}"
        for wn, bname in [("wg", wg_b), ("wu", wu_b)]:
            for hf in range(2):
                s.dma("qp", (lambda el=el, wn=wn, bname=bname, hf=hf: (lambda e: e.dma_start(
                    out=sb[bname][:, :].rearrange("p (dc h) -> p dc h", h=H2)
                        [:, hf * 4:(hf + 1) * 4, :],
                    in_=feat_view(di[wn][el])[:, hf * 4:(hf + 1) * 4, :])))(),
                    w=(f"{bname}h{hf}",), tag=f"{wn}_ld")
        for hf in range(2):
            s.dma("qp", (lambda el=el, wd_b=wd_b, hf=hf: (lambda e: e.dma_start(
                out=sb[wd_b][:, :].rearrange("p (hc o) -> p hc o", o=D)
                    [:, hf * 8:(hf + 1) * 8, :],
                in_=feat_view(di["wd"][el])[:, hf * 8:(hf + 1) * 8, :])))(),
                w=(f"{wd_b}h{hf}",), tag="wd_ld")
        s.dma("qp", (lambda el=el, bd_b=bd_b: (lambda e: e.dma_start(
            out=sb[bd_b][:, :], in_=di["bd_rows"][el:el + 1, :])))(),
            w=(bd_b,), tag="bd_ld")
        for fc in range(16):
            def mm_g(e, el=el, fc=fc, wg_b=wg_b):
                for dcd in range(NDC):
                    i = e.matmul(ps["b2"][0:128, 0:64],
                                 sb[wg_b][:, dcd * H2 + fc * 128: dcd * H2 + (fc + 1) * 128],
                                 sb["slots_bf"][:, dcd * 512 + el * 64: dcd * 512 + (el + 1) * 64],
                                 start=(dcd == 0), stop=(dcd == NDC - 1))
                return i
            s.op("pe", mm_g, r=("slots_bf", f"{wg_b}h0", f"{wg_b}h1"), w=("pb2",))
            s.op("act", (lambda el=el, fc=fc: (lambda e: e.activation(
                sb["gelu_t"][:, fc * 64:(fc + 1) * 64], ps["b2"][0:128, 0:64], AF.Gelu,
                bias=sb["bg_sb"][:, el * 16 + fc: el * 16 + fc + 1])))(),
                r=("pb2", "bg_sb"), w=("gelu_t",))
        for fc in range(16):
            def mm_u(e, el=el, fc=fc, wu_b=wu_b):
                for dcd in range(NDC):
                    i = e.matmul(ps["b3"][0:128, 0:64],
                                 sb[wu_b][:, dcd * H2 + fc * 128: dcd * H2 + (fc + 1) * 128],
                                 sb["slots_bf"][:, dcd * 512 + el * 64: dcd * 512 + (el + 1) * 64],
                                 start=(dcd == 0), stop=(dcd == NDC - 1))
                return i
            s.op("pe", mm_u, r=("slots_bf", f"{wu_b}h0", f"{wu_b}h1"), w=("pb3",))
            s.op("dve", (lambda el=el, fc=fc: (lambda e: e.scalar_tensor_tensor(
                sb["h2_bf"][:, fc * 64:(fc + 1) * 64], ps["b3"][0:128, 0:64],
                sb["bu_sb"][:, el * 16 + fc: el * 16 + fc + 1],
                sb["gelu_t"][:, fc * 64:(fc + 1) * 64], ALU.add, ALU.mult)))(),
                r=("pb3", "bu_sb", "gelu_t"), w=("h2_bf",))
        for oc2 in range(2):
            def mm_d(e, oc2=oc2, wd_b=wd_b, bd_b=bd_b):
                for fc in range(16):
                    e.matmul(ps["b4"][0:64, 0:TT],
                             sb["h2_bf"][:, fc * 64:(fc + 1) * 64],
                             sb[wd_b][:, fc * D + oc2 * TT: fc * D + (oc2 + 1) * TT],
                             start=(fc == 0), stop=False)
                return e.matmul(ps["b4"][0:64, 0:TT], sb["c_ones_row_bf"][0:1, 0:64],
                                sb[bd_b][0:1, oc2 * TT:(oc2 + 1) * TT],
                                start=False, stop=True)
            s.op("pe", (lambda f=mm_d: (lambda e: f(e)))(),
                 r=("h2_bf", f"{wd_b}h0", f"{wd_b}h1", bd_b, "c_ones_row_bf"), w=("pb4",))
            s.op("dve", (lambda oc2=oc2: (lambda e: e.tensor_copy(
                sb["eo_t"][:, oc2 * TT:(oc2 + 1) * TT], ps["b4"][0:64, 0:TT])))(),
                r=("pb4",), w=("eo_t",))
        s.dma("qa", (lambda el=el: (lambda e: e.dma_start(
            out=dd["eout_ag_in"][el * 64:(el + 1) * 64, :], in_=sb["eo_t"][:, :])))(),
            r=("eo_t",), w=("eout_ag_in",))
    s.collective(lambda e: e.collective_compute(
        "AllGather", ALU.bypass, replica_groups=GROUPS,
        ins=[dd["eout_ag_in"][:, :]], outs=[dd["eout_ag_out"][:, :]]),
        r=("eout_ag_in",), w=("eout_ag_out",), tag="cc_eo")
    s.fence()
    sce.close()
    k.scopes.remove(sce)

    scf = contextlib.ExitStack()
    k.scopes.append(scf)
    alloc("eout_bf", [128, NDC * D], BF16, scf)
    for i in range(2):
        alloc(f"xln_n4{i}", [128, NDC * 128], BF16, scf)
        alloc(f"e0b{i}", [128, TT], BF16, scf)
        alloc(f"e1b{i}", [128, TT], BF16, scf)
        alloc(f"combT{i}", [128, NDC * 128], BF16, scf)
        alloc(f"tnb2{i}", [128, 128], BF16, scf)
        alloc(f"g1f{i}", [128, 128], F32, scf)
        alloc(f"x1_n{i}", [128, NDC * 128], F32, scf)
        alloc(f"s1_n{i}", [128, NDC * 128], F32, scf)
        alloc(f"out_n{i}", [128, NDC * 128], F32, scf)
        for nm in ["cm1", "cm2", "cz"]:
            alloc(f"{nm}{i}", [128, 1], F32, scf)
    s.dma("qp", lambda e: e.dma_start(
        out=sb["eout_bf"][:, :].rearrange("p (ec o) -> p ec o", o=D),
        in_=feat_view(dd["eout_ag_out"])), r=("eout_ag_out",), w=("eout_bf",))
    for nt in range(32):
        p = str(nt % 2)
        xlnn, e0b, e1b, combT, tnb2, g1f = (f"xln_n4{p}", f"e0b{p}", f"e1b{p}",
                                            f"combT{p}", f"tnb2{p}", f"g1f{p}")
        x1n, s1n, outn = f"x1_n{p}", f"s1_n{p}", f"out_n{p}"
        cm1, cm2, cz = f"cm1{p}", f"cm2{p}", f"cz{p}"
        bA, bB = ("b0", "b1") if nt % 2 == 0 else ("b4", "b5")
        bF = "b2" if nt % 2 == 0 else "b3"
        s.dma("qs", (lambda nt=nt, xlnn=xlnn: (lambda e: e.dma_start(
            out=sb[xlnn][:, :].rearrange("p (dc t) -> p dc t", t=128),
            in_=dd["xln_dram"].rearrange("p (dc t) -> p dc t", t=T)[:, :, nt * 128:(nt + 1) * 128])))(),
            r=("xln_dram",), w=(xlnn,), tag="xln_f")
        s.dma("qs", (lambda nt=nt, x1n=x1n: (lambda e: e.dma_start(
            out=sb[x1n][:, :].rearrange("p (dc t) -> p dc t", t=128),
            in_=feat_view(dd["x1_dram"])[:, :, nt * 128:(nt + 1) * 128])))(),
            r=("x1_dram",), w=(x1n,), tag="x1_f")
        s.dma("qs", (lambda nt=nt, s1n=s1n: (lambda e: e.dma_start(
            out=sb[s1n][:, :].rearrange("p (dc t) -> p dc t", t=128),
            in_=feat_view(dd["s1_ag_out"])[:, :, nt * 128:(nt + 1) * 128])))(),
            r=("s1_ag_out",), w=(s1n,), tag="s1_f")
        for h in range(2):
            bank = bA if h == 0 else bB
            def mm_cl(e, h=h, bank=bank, xlnn=xlnn):
                for dcd in range(NDC):
                    i = e.matmul(ps[bank][:, 0:TT],
                                 sb[xlnn][:, dcd * 128:(dcd + 1) * 128],
                                 sb["se_T_glob"][:, dcd * D + h * TT: dcd * D + (h + 1) * TT],
                                 start=(dcd == 0), stop=(dcd == NDC - 1))
                return i
            s.op("pe", (lambda f=mm_cl: (lambda e: f(e)))(),
                 r=(xlnn, "se_T_glob"), w=(f"p{bank}",))
        s.op("dve", (lambda cm1=cm1, bA=bA: (lambda e: e.reduce_max(
            sb[cm1][:, :], ps[bA][:, 0:TT], axis=AX)))(), r=(f"p{bA}",), w=(cm1,))
        s.op("dve", (lambda cm2=cm2, bB=bB: (lambda e: e.reduce_max(
            sb[cm2][:, :], ps[bB][:, 0:TT], axis=AX)))(), r=(f"p{bB}",), w=(cm2,))
        s.op("dve", (lambda cm1=cm1, cm2=cm2: (lambda e: e.tensor_max(
            sb[cm1][:, :], sb[cm1][:, :], sb[cm2][:, :])))(), r=(cm1, cm2), w=(cm1,))
        s.op("dve", (lambda cm1=cm1: (lambda e: e.tensor_scalar_mul(
            sb[cm1][:, :], sb[cm1][:, :], -1.0)))(), r=(cm1,), w=(cm1,))
        s.op("act", (lambda e0b=e0b, cm1=cm1, cm2=cm2, bA=bA: (lambda e: e.activation(
            sb[e0b][:, :], ps[bA][:, 0:TT], AF.Exp,
            bias=sb[cm1][:, :], accum_out=sb[cm2][:, :])))(),
            r=(f"p{bA}", cm1), w=(e0b, cm2))
        s.op("act", (lambda e1b=e1b, cm1=cm1, cz=cz, bB=bB: (lambda e: e.activation(
            sb[e1b][:, :], ps[bB][:, 0:TT], AF.Exp,
            bias=sb[cm1][:, :], accum_out=sb[cz][:, :])))(),
            r=(f"p{bB}", cm1), w=(e1b, cz))
        s.op("dve", (lambda cz=cz, cm2=cm2: (lambda e: e.tensor_tensor(
            sb[cz][:, :], sb[cz][:, :], sb[cm2][:, :], ALU.add)))(), r=(cz, cm2), w=(cz,))
        s.op("act", (lambda cz=cz: (lambda e: e.activation(
            sb[cz][:, :], sb[cz][:, :], AF.Ln)))(), r=(cz,), w=(cz,))
        s.op("act", (lambda cz=cz: (lambda e: e.activation(
            sb[cz][:, :], sb[cz][:, :], AF.Exp, scale=-1.0)))(), r=(cz,), w=(cz,))
        for esc in range(NDC):
            ebuf = e0b if esc < 4 else e1b
            eoff = (esc % 4) * 128
            reg = esc % 4
            s.op("dve", (lambda ebuf=ebuf, eoff=eoff, tnb2=tnb2, cz=cz: (lambda e: e.tensor_scalar_mul(
                sb[tnb2][:, :], sb[ebuf][:, eoff:eoff + 128], sb[cz][:, :])))(),
                r=(ebuf, cz), w=(tnb2,))
            s.op("pe", (lambda tnb2=tnb2, reg=reg: (lambda e: e.transpose(
                ps["t0"][:, reg * 128:(reg + 1) * 128], sb[tnb2][:, :],
                sb["c_ident_bf"][:, :])))(),
                r=(tnb2, "c_ident_bf"), w=(f"pt0_{reg}",), tag="tr_c")
            s.op("dve", (lambda esc=esc, combT=combT, reg=reg: (lambda e: e.tensor_copy(
                sb[combT][:, esc * 128:(esc + 1) * 128], ps["t0"][:, reg * 128:(reg + 1) * 128])))(),
                r=(f"pt0_{reg}",), w=(combT,))
        for oc in range(NDC):
            def mm_f(e, oc=oc, combT=combT, bF=bF):
                for esc in range(NDC):
                    i = e.matmul(ps[bF][:, 0:128],
                                 sb["eout_bf"][:, esc * D + oc * 128: esc * D + (oc + 1) * 128],
                                 sb[combT][:, esc * 128:(esc + 1) * 128],
                                 start=(esc == 0), stop=(esc == NDC - 1))
                return i
            s.op("pe", (lambda f=mm_f: (lambda e: f(e)))(),
                 r=(combT, "eout_bf"), w=(f"p{bF}",))
            s.op("act", (lambda oc=oc, g1f=g1f, bF=bF: (lambda e: e.activation(
                sb[g1f][:, :], ps[bF][:, 0:128], AF.Silu)))(),
                r=(f"p{bF}",), w=(g1f,))
            s.op("dve", (lambda oc=oc, g1f=g1f, s1n=s1n: (lambda e: e.tensor_tensor(
                sb[g1f][:, :], sb[g1f][:, :],
                sb[s1n][:, oc * 128:(oc + 1) * 128], ALU.mult)))(),
                r=(g1f, s1n), w=(g1f,))
            s.op("dve", (lambda oc=oc, g1f=g1f, x1n=x1n, outn=outn: (lambda e: e.tensor_tensor(
                sb[outn][:, oc * 128:(oc + 1) * 128], sb[g1f][:, :],
                sb[x1n][:, oc * 128:(oc + 1) * 128], ALU.add)))(),
                r=(g1f, x1n), w=(outn,))
        s.dma("qa", (lambda nt=nt, outn=outn: (lambda e: e.dma_start(
            out=feat_view(k.out)[:, :, nt * 128:(nt + 1) * 128],
            in_=sb[outn][:, :].rearrange("p (dc t) -> p dc t", t=128))))(),
            r=(outn,), w=("out_fm",))
    s.fence()
    scf.close()
    k.scopes.remove(scf)


_PROG_CACHE = {}


def kernel(**inputs):
    import os
    global BUILD_PHASES, DEBUG_TAPS
    if os.environ.get("MK_PHASES"):
        BUILD_PHASES = tuple(os.environ["MK_PHASES"].split(","))
    if os.environ.get("MK_TAPS"):
        DEBUG_TAPS = tuple(os.environ["MK_TAPS"].split(","))
    maps = host_prep(inputs)
    key = (tuple(BUILD_PHASES), tuple(DEBUG_TAPS))
    if key not in _PROG_CACHE:
        _PROG_CACHE[key] = build_program(BUILD_PHASES, DEBUG_TAPS)
    nc = _PROG_CACHE[key]
    res = run_bass_kernel_spmd(nc, maps, core_ids=list(range(8)))
    outs = [res.results[2 * b]["out_fm"].T for b in range(B)]
    return np.stack(outs, axis=0).astype(np.float32)
